# version 21
# baseline (speedup 1.0000x reference)
import os
os.environ.setdefault("GAUGE_NTFF_IGNORE_DMA", "1")
import numpy as np
import ml_dtypes
import concourse.bass as bass
import concourse.mybir as mybir
from concourse.bass_utils import run_bass_kernel_spmd

F32 = mybir.dt.float32
BF16 = mybir.dt.bfloat16
I32 = mybir.dt.int32
AF = mybir.ActivationFunctionType
ALU = mybir.AluOpType

B_CORE = 2
S = 4096
D = 1024
DC = D // 128
TT0 = 512
NT = S // TT0
EPS = 1e-5

NO_SEM_SHARING = False
DEFER_CONV = True
SUBLAYERS = list(range(8))
GROUPS = None


class Ctr:
    def __init__(self, nc, name):
        self.sem = nc.alloc_semaphore(name)
        self.n = 0

    def inc(self, inst, k=1):
        inst.then_inc(self.sem, k)
        self.n += k
        return self.n

    def wait(self, eng, n=None):
        v = self.n if n is None else n
        if v > 0:
            eng.wait_ge(self.sem, v)


class PsumPool:
    def __init__(self, nc):
        self.banks = [nc.alloc_psum_tensor(f"psb{i}", [128, 512], F32) for i in range(8)]
        self.ev = [Ctr(nc, f"psev{i}") for i in range(8)]

    def tile(self, k, p=128, n=512, p0=0):
        return self.banks[k][p0:p0 + p, 0:n]

    def acquire(self, eng, k):
        self.ev[k].wait(eng)

    def release(self, k, inst):
        self.ev[k].inc(inst)


class Prog:
    def __init__(self, sublayers):
        self.nc = bass.Bass("TRN2", target_bir_lowering=False)
        self.sublayers = sublayers
        self.ps = PsumPool(self.nc)
        self.ctrs = {}
        self._slot_ctrs = []
        self._slot_gp = []
        self._dram = {}
        self._sbuf = {}
        self.ext_inputs = set()
        self.deferred_p1 = []
        self.deferred_p2 = []

    GP_DMA = frozenset(["wH0", "wH1", "wH2", "wo16", "w5",
                        "cvldpool", "cvstpool0", "cvstpool1",
                        "gld0", "gld1", "gst0", "gst1"])

    def ctr(self, name):
        tag, _, suffix = name.partition("_")
        if not suffix:
            tag, suffix = "g", name
        gp = suffix in self.GP_DMA
        pool = self._slot_gp if gp else self._slot_ctrs
        scope = self.ctrs.setdefault((tag, gp), {})
        if suffix not in scope:
            if NO_SEM_SHARING or gp:
                scope[suffix] = Ctr(self.nc, "u_" + name)
            else:
                slot = len(scope)
                while len(pool) <= slot:
                    pool.append(Ctr(self.nc, ("gp" if gp else "cs") + str(len(pool))))
                scope[suffix] = pool[slot]
        return scope[suffix]

    def din(self, name, shape, dtype=F32):
        if name not in self._dram:
            self._dram[name] = self.nc.dram_tensor(name, list(shape), dtype, kind="ExternalInput").ap()
            self.ext_inputs.add(name)
        return self._dram[name]

    def dout(self, name, shape, dtype=F32):
        if name not in self._dram:
            self._dram[name] = self.nc.dram_tensor(name, list(shape), dtype, kind="ExternalOutput").ap()
        return self._dram[name]

    def dscratch(self, name, shape, dtype=F32):
        if name not in self._dram:
            self._dram[name] = self.nc.dram_tensor(name, list(shape), dtype).ap()
        return self._dram[name]

    def sb(self, name, shape, dtype=F32):
        if name not in self._sbuf:
            self._sbuf[name] = self.nc.alloc_sbuf_tensor(name, list(shape), dtype)
        return self._sbuf[name]

    def t(self, name):
        return self._sbuf[name]



def host_layout(inputs):
    w = {}
    t = lambda a: np.ascontiguousarray(np.asarray(a).T)
    c = lambda a: np.ascontiguousarray(np.asarray(a))
    def col(a, nc_):
        a = np.asarray(a)
        if a.ndim == 1:
            a = a.reshape(nc_, 128, 1)
        else:
            a = a.reshape(nc_, 128, -1)
        return np.ascontiguousarray(a.transpose(1, 0, 2).reshape(128, -1))
    w["wqT0"] = t(inputs["L0_attn_a_Wq"]); w["wkT0"] = t(inputs["L0_attn_a_Wk"])
    w["wvT0"] = t(inputs["L0_attn_a_Wv"]); w["woT0"] = t(inputs["L0_attn_a_Wo"])
    for nm in ("bq", "bk", "bv", "bo"):
        w[nm + "0"] = col(inputs["L0_attn_a_" + nm], DC)
    w["w1T1"] = c(inputs["L1_attn_a_W1"])
    w["b1_1"] = col(inputs["L1_attn_a_b1"], 24)
    w["w2T1"] = c(inputs["L1_attn_a_W2"])
    w["b2_1"] = col(inputs["L1_attn_a_b2"], DC)
    for L in (0, 1):
        w[f"fw1T{L}"] = t(inputs[f"L{L}_ffn_w1_w"])
        w[f"fw2T{L}"] = t(inputs[f"L{L}_ffn_w2_w"])
        w[f"fh1T{L}"] = t(inputs[f"L{L}_ffn_h1_w"])
        w[f"fh2T{L}"] = t(inputs[f"L{L}_ffn_h2_w"])
        w[f"fw3T{L}"] = t(inputs[f"L{L}_ffn_w3_w"])
        w[f"fs1{L}"] = col(inputs[f"L{L}_ffn_w1_s"], 32)
        w[f"fs2{L}"] = col(inputs[f"L{L}_ffn_w2_s"], 32)
        w[f"fsh1{L}"] = col(inputs[f"L{L}_ffn_h1_s"], 32)
        w[f"fsh2{L}"] = col(inputs[f"L{L}_ffn_h2_s"], 32)
        w[f"fs3{L}"] = col(inputs[f"L{L}_ffn_w3_s"], 8)
    tq = t(inputs["L0_attn_b_Wq"]); tk = t(inputs["L0_attn_b_Wk"])
    def pad64(a):
        z = np.zeros((1024, 1024), np.float32)
        z.reshape(1024, 16, 64)[:, :, :16] = a.reshape(1024, 16, 16)
        return z
    w["wb0"] = np.ascontiguousarray(np.concatenate([pad64(tk), pad64(tq), t(inputs["L0_attn_b_Wv"])], axis=1))
    w["b0z"] = np.zeros((128, 24), np.float32)
    w["toT"] = t(inputs["L0_attn_b_Wo"])
    w["tg"] = np.ascontiguousarray(np.asarray(inputs["L0_attn_b_g_weight"]).reshape(64, 1))
    w["sinT"] = t(inputs["L1_attn_b_W_in"])
    w["sxT"] = t(inputs["L1_attn_b_W_xproj"])
    w["sdtT"] = t(inputs["L1_attn_b_W_dt"])
    w["soutT"] = t(inputs["L1_attn_b_W_out"])
    w["skx"] = col(inputs["L1_attn_b_Kx"].reshape(512, 3), 4)
    w["skz"] = col(inputs["L1_attn_b_Kz"].reshape(512, 3), 4)
    w["sbdt"] = col(inputs["L1_attn_b_b_dt"], 4)
    w["salog"] = col(inputs["L1_attn_b_A_log"], 4)
    w["sD"] = col(inputs["L1_attn_b_D"], 4)

    x = np.asarray(inputs["L0_attn_a_x"])
    per_core = []
    for core in range(8):
        xs = x[core * B_CORE:(core + 1) * B_CORE]
        xs = np.ascontiguousarray(xs.transpose(0, 2, 1)).reshape(B_CORE, DC, 128, S)
        m = dict(w)
        m["x_in"] = xs
        per_core.append(m)
    return per_core


W_BF16 = {
    "wqT0": (1024, 1024), "wkT0": (1024, 1024), "wvT0": (1024, 1024), "woT0": (1024, 1024),
    "fw1T0": (1024, 4096), "fw2T0": (1024, 4096), "fh1T0": (4096, 4096), "fh2T0": (4096, 4096), "fw3T0": (4096, 1024),
    "wb0": (1024, 3072), "toT": (1024, 1024),
    "w1T1": (1024, 3072), "w2T1": (1024, 1024),
    "fw1T1": (1024, 4096), "fw2T1": (1024, 4096), "fh1T1": (4096, 4096), "fh2T1": (4096, 4096), "fw3T1": (4096, 1024),
    "sinT": (1024, 1024), "sxT": (512, 80), "sdtT": (64, 512), "soutT": (1024, 1024),
}
W_SMALL = {
    "bq0": (128, 8), "bk0": (128, 8), "bv0": (128, 8), "bo0": (128, 8),
    "b1_1": (128, 24), "b2_1": (128, 8),
    "fs10": (128, 32), "fs20": (128, 32), "fsh10": (128, 32), "fsh20": (128, 32), "fs30": (128, 8),
    "fs11": (128, 32), "fs21": (128, 32), "fsh11": (128, 32), "fsh21": (128, 32), "fs31": (128, 8),
    "tg": (64, 1), "b0z": (128, 24),
    "skx": (128, 12), "skz": (128, 12), "sbdt": (128, 4), "salog": (128, 32), "sD": (128, 4),
}
SUB_W = {
    0: (["wqT0", "wkT0", "wvT0", "woT0"], ["bq0", "bk0", "bv0", "bo0"]),
    1: (["fw1T0", "fw2T0", "fh1T0", "fh2T0", "fw3T0"], ["fs10", "fs20", "fsh10", "fsh20", "fs30"]),
    2: (["wb0", "toT"], ["tg", "b0z"]),
    3: (["fw1T0", "fw2T0", "fh1T0", "fh2T0", "fw3T0"], ["fs10", "fs20", "fsh10", "fsh20", "fs30"]),
    4: (["w1T1", "w2T1"], ["b1_1", "b2_1"]),
    5: (["fw1T1", "fw2T1", "fh1T1", "fh2T1", "fw3T1"], ["fs11", "fs21", "fsh11", "fsh21", "fs31"]),
    6: (["sinT", "sxT", "sdtT", "soutT"], ["skx", "skz", "sbdt", "salog", "sD"]),
    7: (["fw1T1", "fw2T1", "fh1T1", "fh2T1", "fw3T1"], ["fs11", "fs21", "fsh11", "fsh21", "fs31"]),
}

SWZ = {"fh1T0", "fh2T0", "fw3T0", "fh1T1", "fh2T1", "fw3T1"}


def bf_scratch(P, nm):
    R, C = W_BF16[nm]
    if nm in SWZ:
        return P.dscratch(nm + "_bf", (C // 128, 128, R), BF16)
    return P.dscratch(nm + "_bf", (R, C), BF16)


def bf_store(eng, P, nm, r0, pr, c0, pc, sb_in):
    dstw = bf_scratch(P, nm)
    if nm in SWZ:
        return eng.dma_start(
            out=dstw[c0 // 128:(c0 + pc) // 128, :, r0:r0 + pr].transpose([1, 0, 2]),
            in_=sb_in.rearrange("p (j i) -> p j i", i=128))
    return eng.dma_start(out=dstw[r0:r0 + pr, c0:c0 + pc], in_=sb_in)



def emit_preamble(P):
    nc = P.nc
    need_bf = sorted({w for s in P.sublayers for w in SUB_W[s][0]})
    need_sm = sorted({w for s in P.sublayers for w in SUB_W[s][1]})
    P.deferred_p1 = []
    P.deferred_p2 = []
    if DEFER_CONV and P.sublayers and P.sublayers[0] == 0 and len(P.sublayers) > 1:
        first = set(SUB_W[0][0])
        early = {w for s in P.sublayers if s in (1, 2, 3, 4) for w in SUB_W[s][0]}
        P.deferred_p1 = sorted(w for w in (early - first) if w in need_bf)
        if 2 in P.sublayers:
            P.deferred_p2 = sorted(set(need_bf) - first - early)
        else:
            P.deferred_p1 = sorted(set(need_bf) - first)
        need_bf = sorted(first)

    ones_f = P.sb("ones_f", [128, 1], F32)
    ones_b = P.sb("ones_b", [128, 1], BF16)
    onesr_f = P.sb("onesr_f", [1, 128], F32)
    ident_b = P.sb("ident_b", [128, 128], BF16)
    masks = P.sb("masks", [128, 4 * 512], F32)
    itf = P.sb("iota_f", [128, 512], F32)
    iw = P.sb("iota_i32", [128, 512], I32)
    eps_t = P.sb("eps_t", [128, 1], F32)
    P.sb("ffn_xr", [128, DC * 512], F32)
    P.sb("ffn_xb", [128, DC * 512], BF16)
    P.sb("ffn_xsq", [128, DC * 512], BF16)
    P.sb("ffn_ta", [128, 2 * 512], F32)
    P.sb("ffn_tb", [128, 2 * 512], F32)
    P.sb("ffn_rout", [128, DC * 512], F32)
    P.sb("ffn_mrow", [1, 512], F32)
    P.sb("ffn_vrow", [1, 512], F32)
    P.sb("ffn_trow", [1, 512], F32)
    P.sb("ffn_wab", [128, 3 * 2 * 8 * 256], BF16)
    P.sb("ffn_hh", [128, 3 * 32 * 128], BF16)
    hin_t = P.sb("ffn_hin", [128, 32 * 512], BF16)
    g1_t = P.sb("ffn_g1", [128, 32 * 512], BF16)
    cv_in = hin_t.ap().bitcast(F32)
    cv_out = g1_t.ap()
    c_ld = P.ctr("pre_ld")
    c_ld2 = P.ctr("pre_ld2")
    c_cv = [P.ctr("pre_cv0"), P.ctr("pre_cv1")]
    c_g = P.ctr("pre_g")
    c_v = P.ctr("pre_v")

    with nc.Block() as blk:
        @blk.sync
        def _(sy):
            for nm in need_sm:
                shp = W_SMALL[nm]
                ap = P.din(nm, shp, F32)
                t = P.sb("sm_" + nm, [shp[0], shp[1]], F32)
                sy.dma_start(out=t[:], in_=ap[:]).then_inc(c_ld.sem, 16)
                c_ld.n += 16
            sy.wait_ge(c_ld.sem, c_ld.n)

        @blk.gpsimd
        def _(g):
            g.memset(ones_f[:], 1.0)
            g.memset(onesr_f[:], 1.0)
            g.memset(eps_t[:], EPS)
            g.iota(iw[:], [[1, 512]], base=0, channel_multiplier=-1)
            g.drain()
            g.tensor_copy(ones_b[:], ones_f[:])
            g.tensor_copy(itf[:], iw[:])
            g.drain()
            g.nop().then_inc(c_g.sem, 1)
        c_g.n += 1

        @blk.vector
        def _(v):
            c_g.wait(v)
            v.tensor_scalar(ident_b[:], itf[:, 0:128], 0.0, None, op0=ALU.is_equal)
            for mi in range(4):
                v.tensor_scalar(masks[:, mi * 512:(mi + 1) * 512], itf[:], float(mi * 128), None, op0=ALU.is_ge)
            v.drain()
            v.nop().then_inc(c_v.sem, 1)
        c_v.n += 1

    jobs = []
    for nm in need_bf:
        R, C = W_BF16[nm]
        for r0 in range(0, R, 128):
            jobs.append((nm, r0, min(128, R - r0), C))
    big = [j for j in jobs if j[3] > 2048]
    rest = [j for j in jobs if j[3] <= 2048]
    half = len(big) // 2
    work = {"act": big[:half], "dve": big[half:], "pool": rest}
    xr_t = P.sb("ffn_xr", [128, DC * 512], F32)
    rout_t = P.sb("ffn_rout", [128, DC * 512], F32)
    wab_t = P.t("ffn_wab"); hh_t = P.t("ffn_hh")
    xb_t = P.t("ffn_xb"); xsq_t = P.t("ffn_xsq")
    slots = {
        "act": ([cv_in[:, 0:4096], cv_in[:, 4096:8192]],
                [cv_out[:, 0:4096], cv_out[:, 4096:8192]]),
        "dve": ([xr_t.ap(), rout_t.ap()],
                [cv_out[:, 8192:12288], cv_out[:, 12288:16384]]),
        "pool": ([wab_t.ap().bitcast(F32)[:, 0:2048], hh_t.ap().bitcast(F32)[:, 0:2048]],
                 [xb_t.ap()[:, 0:2048], xsq_t.ap()[:, 0:2048]]),
    }
    c_ld_e = {k: P.ctr("pre_cvld" + k) for k in ("act", "pool")}
    c_ld_dve = [P.ctr("pre_cvldv0"), P.ctr("pre_cvldv1")]
    c_st_e = {k: [P.ctr("pre_cvst" + k + "0"), P.ctr("pre_cvst" + k + "1")] for k in work}

    def emit_convert(eng, key):
        ins, outs = slots[key]
        cl = c_ld_e[key]
        ci = 0
        for nm, r0, pr, C in work[key]:
            srcw = P.din(nm, W_BF16[nm], F32)
            par = ci % 2
            if ci >= 2:
                eng.wait_ge(c_st_e[key][par].sem, 16 * (ci // 2))
            eng.dma_start(out=ins[par][0:pr, 0:C], in_=srcw[r0:r0 + pr, :]).then_inc(cl.sem, 16)
            cl.n += 16
            eng.wait_ge(cl.sem, cl.n)
            if key == "act":
                eng.copy(outs[par][0:pr, 0:C], ins[par][0:pr, 0:C])
            else:
                eng.tensor_copy(outs[par][0:pr, 0:C], ins[par][0:pr, 0:C])
            eng.drain()
            bf_store(eng, P, nm, r0, pr, 0, C, outs[par][0:pr, 0:C]).then_inc(c_st_e[key][par].sem, 16)
            c_st_e[key][par].n += 16
            ci += 1
        for cs in c_st_e[key]:
            eng.wait_ge(cs.sem, cs.n)

    c_cvc = P.ctr("pre_cvc")
    with nc.Block() as blk2:
        @blk2.scalar
        def _(s):
            emit_convert(s, "act")

        @blk2.sync
        def _(sy):
            ins, outs = slots["dve"]
            wk = work["dve"]
            for ci, (nm, r0, pr, C) in enumerate(wk):
                srcw = P.din(nm, W_BF16[nm], F32)
                par = ci % 2
                if ci >= 2:
                    sy.wait_ge(c_st_e["dve"][par].sem, 16 * (ci // 2))
                sy.dma_start(out=ins[par][0:pr, 0:C], in_=srcw[r0:r0 + pr, :]).then_inc(c_ld_dve[par].sem, 16)
                c_ld_dve[par].n += 16
                if ci >= 1:
                    pnm, pr0, ppr, pC = wk[ci - 1]
                    c_cvc.wait(sy, ci)
                    bf_store(sy, P, pnm, pr0, ppr, 0, pC, outs[(ci - 1) % 2][0:ppr, 0:pC]
                             ).then_inc(c_st_e["dve"][(ci - 1) % 2].sem, 16)
                    c_st_e["dve"][(ci - 1) % 2].n += 16
            if wk:
                ci = len(wk)
                pnm, pr0, ppr, pC = wk[ci - 1]
                c_cvc.wait(sy, ci)
                bf_store(sy, P, pnm, pr0, ppr, 0, pC, outs[(ci - 1) % 2][0:ppr, 0:pC]
                         ).then_inc(c_st_e["dve"][(ci - 1) % 2].sem, 16)
                c_st_e["dve"][(ci - 1) % 2].n += 16
            for cs in c_st_e["dve"]:
                sy.wait_ge(cs.sem, cs.n)

        @blk2.vector
        def _(v):
            ins, outs = slots["dve"]
            for ci, (nm, r0, pr, C) in enumerate(work["dve"]):
                par = ci % 2
                v.wait_ge(c_ld_dve[par].sem, 16 * (ci // 2 + 1))
                v.tensor_copy(outs[par][0:pr, 0:C], ins[par][0:pr, 0:C])
                v.drain()
                v.nop().then_inc(c_cvc.sem, 1)
                c_cvc.n += 1

        @blk2.gpsimd
        def _(g):
            emit_convert(g, "pool")


class DConv:

    def __init__(self, P, tag, names, arena="ffn_hh"):
        self.P = P
        hh_t = P.t(arena)
        hv = hh_t.ap().bitcast(F32)
        self.ins = [hv[:, 0:2048], hv[:, 2048:4096]]
        self.outs = [hh_t.ap()[:, 8192:10240], hh_t.ap()[:, 10240:12288]]
        self.cl = [P.ctr(tag + "_gld0"), P.ctr(tag + "_gld1")]
        self.cs = [P.ctr(tag + "_gst0"), P.ctr(tag + "_gst1")]
        self.jobs = []
        for nm in names:
            R, C = W_BF16[nm]
            for r0 in range(0, R, 128):
                for c0 in range(0, C, 2048):
                    self.jobs.append((nm, r0, min(128, R - r0), c0, min(2048, C - c0)))
        self.n_ld = 0
        self.n_pr = 0

    def _load(self, g):
        nm, r0, pr, c0, pc = self.jobs[self.n_ld]
        srcw = self.P.din(nm, W_BF16[nm], F32)
        par = self.n_ld % 2
        g.dma_start(out=self.ins[par][0:pr, 0:pc],
                    in_=srcw[r0:r0 + pr, c0:c0 + pc]).then_inc(self.cl[par].sem, 16)
        self.cl[par].n += 16
        self.n_ld += 1

    def step(self, g, count):
        while self.n_ld < min(2, len(self.jobs)):
            self._load(g)
        for _ in range(count):
            if self.n_pr >= len(self.jobs):
                return
            ci = self.n_pr
            nm, r0, pr, c0, pc = self.jobs[ci]
            par = ci % 2
            if ci >= 2:
                g.wait_ge(self.cs[par].sem, 16 * (ci // 2))
            g.wait_ge(self.cl[par].sem, 16 * (ci // 2 + 1))
            g.tensor_copy(self.outs[par][0:pr, 0:pc], self.ins[par][0:pr, 0:pc])
            g.drain()
            bf_store(g, self.P, nm, r0, pr, c0, pc,
                     self.outs[par][0:pr, 0:pc]).then_inc(self.cs[par].sem, 16)
            self.cs[par].n += 16
            self.n_pr += 1
            if self.n_ld < len(self.jobs):
                self._load(g)

    def finish(self, g):
        self.step(g, len(self.jobs) - self.n_pr)
        for c in self.cs:
            if c.n:
                g.wait_ge(c.sem, c.n)



def emit_ffn(P, L, tag, src, dst):
    nc = P.nc
    w1 = P.dscratch(f"fw1T{L}_bf", (1024, 4096), BF16)
    w2 = P.dscratch(f"fw2T{L}_bf", (1024, 4096), BF16)
    h1 = bf_scratch(P, f"fh1T{L}")
    h2 = bf_scratch(P, f"fh2T{L}")
    w3 = bf_scratch(P, f"fw3T{L}")
    s1 = P.t(f"sm_fs1{L}"); s2 = P.t(f"sm_fs2{L}")
    sh1 = P.t(f"sm_fsh1{L}"); sh2 = P.t(f"sm_fsh2{L}")
    s3 = P.t(f"sm_fs3{L}")

    xr = P.sb("ffn_xr", [128, DC * 512], F32)
    xb = P.sb("ffn_xb", [128, DC * 512], BF16)
    xsq = P.sb("ffn_xsq", [128, DC * 512], BF16)
    hin = P.sb("ffn_hin", [128, 32 * 512], BF16)
    g1b = P.sb("ffn_g1", [128, 32 * 512], BF16)
    ta = P.sb("ffn_ta", [128, 2 * 512], F32)
    tb = P.sb("ffn_tb", [128, 2 * 512], F32)
    rout = P.sb("ffn_rout", [128, DC * 512], F32)
    mrow = P.sb("ffn_mrow", [1, 512], F32)
    vrow = P.sb("ffn_vrow", [1, 512], F32)
    trow = P.sb("ffn_trow", [1, 512], F32)
    RB = 3; RH = 3
    wab = P.sb("ffn_wab", [128, RB * 2 * 8 * 256], BF16)
    hh = P.sb("ffn_hh", [128, RH * 32 * 128], BF16)

    ones_b = P.t("ones_b"); onesr_f = P.t("onesr_f")

    C = lambda nm: P.ctr(tag + "_" + nm)
    c_ldx = C("ldx")
    c_wB = [C("wB0"), C("wB1"), C("wB2")]
    c_wH = [C("wH0"), C("wH1"), C("wH2")]
    c_sq = C("sq"); c_stat = C("stat"); c_var = C("var"); c_rstd = C("rstd")
    c_mr = C("mr"); c_bc = C("bc"); c_lnc = C("lnc"); c_xq = C("xq")
    c_Ba = C("Ba"); c_Bb = C("Bb"); c_h = C("h")
    c_C = C("C"); c_D = C("D"); c_E = C("E")
    c_st = [C("st0"), C("st1")]
    base = {c: c.n for c in [c_ldx, c_sq, c_stat, c_var, c_rstd,
                             c_mr, c_bc, c_Ba, c_Bb, c_h, c_C, c_D, c_E] + c_wB + c_wH + c_st}
    B = lambda c: base[c]
    ev = P.ps.ev
    evb = [e.n for e in ev]
    PERT = [20, 20, 16, 16, 32, 32, 2, 2]

    def evw(eng, k, val):
        if val > 0:
            eng.wait_ge(ev[k].sem, evb[k] + val)

    NTILE = B_CORE * NT

    def tiles():
        for b in range(B_CORE):
            for ti in range(NT):
                yield b * NT + ti, b, ti * 512

    cons_sig = []
    for idx in range(NTILE):
        for j in range(32):
            cons_sig.append((c_C, B(c_C) + 32 * idx + j + 1))
        for j in range(32):
            cons_sig.append((c_D, B(c_D) + 32 * idx + j + 1))
        for j in range(8):
            cons_sig.append((c_E, B(c_E) + 8 * idx + j + 1))

    with nc.Block() as blk:
        @blk.sync
        def _(sy):
            slot = 0
            prev = None
            for idx, b, t0 in tiles():
                for i in range(DC):
                    if idx > 0:
                        evw(sy, i % 2, 20 * (idx - 1) + 16 + i // 2 + 1)
                    sy.dma_start(out=xr.ap()[:, i * 512:(i + 1) * 512],
                                 in_=src[b, i, :, t0:t0 + 512]
                                 ).then_inc(c_ldx.sem, 16)
                    c_ldx.n += 16
                if idx > 0:
                    evw(sy, 0, 20 * idx)
                    evw(sy, 1, 20 * idx)
                    pb, pt0 = prev
                    sy.dma_start(out=dst[pb, :, :, pt0:pt0 + 512].transpose([1, 0, 2]),
                                 in_=rout.ap().rearrange("p (c k) -> p c k", c=DC)
                                 ).then_inc(c_st[(idx - 1) % 2].sem, 16)
                    c_st[(idx - 1) % 2].n += 16
                prev = (b, t0)
                for j in range(16):
                    if slot >= RB:
                        sidx, sj = divmod(slot - RB, 16)
                        c_Bb.wait(sy, B(c_Bb) + 32 * sidx + 2 * (sj + 1))
                    cw = c_wB[slot % RB]
                    off = (slot % RB) * 2 * 8 * 256
                    sy.dma_start(out=wab[:, off:off + 8 * 256].rearrange("p (k i) -> p k i", k=8),
                                 in_=w1[:, j * 256:(j + 1) * 256].rearrange("(k p) i -> p k i", p=128)
                                 ).then_inc(cw.sem, 16)
                    sy.dma_start(out=wab[:, off + 8 * 256:off + 16 * 256].rearrange("p (k i) -> p k i", k=8),
                                 in_=w2[:, j * 256:(j + 1) * 256].rearrange("(k p) i -> p k i", p=128)
                                 ).then_inc(cw.sem, 16)
                    cw.n += 32
                    slot += 1
            evw(sy, 0, 20 * NTILE)
            evw(sy, 1, 20 * NTILE)
            pb, pt0 = prev
            cs = c_st[(NTILE - 1) % 2]
            sy.dma_start(out=dst[pb, :, :, pt0:pt0 + 512].transpose([1, 0, 2]),
                         in_=rout.ap().rearrange("p (c k) -> p c k", c=DC)
                         ).then_inc(cs.sem, 16)
            cs.n += 16
            sy.wait_ge(c_st[0].sem, c_st[0].n)
            sy.wait_ge(c_st[1].sem, c_st[1].n)

        @blk.gpsimd
        def _(g):
            slot = 0
            for idx, b, t0 in tiles():
                for phase, wsrc, nslot in (("C", h1, 32), ("D", h2, 32), ("E", w3, 8)):
                    for j in range(nslot):
                        if slot >= RH:
                            ctr, val = cons_sig[slot - RH]
                            ctr.wait(g, val)
                        cw = c_wH[slot % RH]
                        off = (slot % RH) * 32 * 128
                        g.dma_start(out=hh[:, off:off + 32 * 128],
                                    in_=wsrc[j]
                                    ).then_inc(cw.sem, 16)
                        cw.n += 16
                        slot += 1

        @blk.tensor
        def _(t):
            bslot = 0
            hslot = 0
            for idx, b, t0 in tiles():
                c_sq.wait(t, B(c_sq) + idx + 1)
                evw(t, 6, 2 * idx); evw(t, 7, 2 * idx)
                for dc in range(DC):
                    t.matmul(P.ps.tile(6, 1, 512), ones_b[:], xb[:, dc * 512:(dc + 1) * 512],
                             start=(dc == 0), stop=(dc == DC - 1))
                for dc in range(DC):
                    inst = t.matmul(P.ps.tile(7, 1, 512), ones_b[:], xsq[:, dc * 512:(dc + 1) * 512],
                                    start=(dc == 0), stop=(dc == DC - 1))
                c_stat.inc(inst)
                c_mr.wait(t, B(c_mr) + idx + 1)
                evw(t, 6, 2 * idx + 1); evw(t, 7, 2 * idx + 1)
                t.matmul(P.ps.tile(6), onesr_f[:], trow[:], start=True, stop=True)
                inst = t.matmul(P.ps.tile(7), onesr_f[:], mrow[:], start=True, stop=True)
                c_bc.inc(inst)
                evw(t, 6, 2 * (idx + 1)); evw(t, 7, 2 * (idx + 1))
                for j in range(16):
                    c_wB[bslot % RB].wait(t, B(c_wB[bslot % RB]) + 32 * (bslot // RB + 1))
                    off = (bslot % RB) * 2 * 8 * 256
                    for h in range(2):
                        i = 2 * j + h
                        ka = i % 2
                        kb = 2 + i % 2
                        evw(t, ka, 20 * idx + i // 2)
                        for k in range(8):
                            inst = t.matmul(P.ps.tile(ka), wab[:, off + k * 256 + h * 128: off + k * 256 + h * 128 + 128],
                                            xb[:, k * 512:(k + 1) * 512], start=(k == 0), stop=(k == 7))
                        c_Ba.inc(inst)
                        evw(t, kb, 16 * idx + i // 2)
                        for k in range(8):
                            inst = t.matmul(P.ps.tile(kb), wab[:, off + 8 * 256 + k * 256 + h * 128: off + 8 * 256 + k * 256 + h * 128 + 128],
                                            xb[:, k * 512:(k + 1) * 512], start=(k == 0), stop=(k == 7))
                        c_Bb.inc(inst)
                    bslot += 1
                evw(t, 2, 16 * (idx + 1)); evw(t, 3, 16 * (idx + 1))
                for i in range(32):
                    c_wH[hslot % RH].wait(t, B(c_wH[hslot % RH]) + 16 * (hslot // RH + 1))
                    off = (hslot % RH) * 32 * 128
                    kk = 4 + i % 2
                    evw(t, kk, 32 * idx + i // 2)
                    for k in range(32):
                        inst = t.matmul(P.ps.tile(kk), hh[:, off + k * 128: off + (k + 1) * 128],
                                        hin[:, k * 512:(k + 1) * 512], start=(k == 0), stop=(k == 31))
                    c_C.inc(inst)
                    hslot += 1
                evw(t, 4, 32 * idx + 16); evw(t, 5, 32 * idx + 16)
                for i in range(32):
                    c_wH[hslot % RH].wait(t, B(c_wH[hslot % RH]) + 16 * (hslot // RH + 1))
                    off = (hslot % RH) * 32 * 128
                    kk = 4 + i % 2
                    evw(t, kk, 32 * idx + 16 + i // 2)
                    for k in range(32):
                        inst = t.matmul(P.ps.tile(kk), hh[:, off + k * 128: off + (k + 1) * 128],
                                        g1b[:, k * 512:(k + 1) * 512], start=(k == 0), stop=(k == 31))
                    c_D.inc(inst)
                    hslot += 1
                evw(t, 4, 32 * (idx + 1)); evw(t, 5, 32 * (idx + 1))
                for i in range(8):
                    c_wH[hslot % RH].wait(t, B(c_wH[hslot % RH]) + 16 * (hslot // RH + 1))
                    off = (hslot % RH) * 32 * 128
                    kk = i % 2
                    evw(t, kk, 20 * idx + 16 + i // 2)
                    for k in range(32):
                        inst = t.matmul(P.ps.tile(kk), hh[:, off + k * 128: off + (k + 1) * 128],
                                        hin[:, k * 512:(k + 1) * 512], start=(k == 0), stop=(k == 31))
                    c_E.inc(inst)
                    hslot += 1

        @blk.scalar
        def _(s):
            for idx, b, t0 in tiles():
                c_ldx.wait(s, B(c_ldx) + 128 * (idx + 1))
                if idx > 0:
                    c_Bb.wait(s, B(c_Bb) + 32 * idx)
                inst = s.copy(xb[:, 0:2048], xr[:, 0:2048])
                s.copy(xb[:, 2048:4096], xr[:, 2048:4096])
                s.drain()
                s.activation(xsq[:, 0:2048], xb[:, 0:2048], AF.Square)
                inst = s.activation(xsq[:, 2048:4096], xb[:, 2048:4096], AF.Square)
                c_sq.inc(inst)
                c_var.wait(s, B(c_var) + idx + 1)
                if idx > 0:
                    c_bc.wait(s, B(c_bc) + idx)
                s.activation(trow[:], vrow[:], AF.Ln, bias=P.t('eps_t')[0:1, :])
                s.drain()
                inst = s.activation(trow[:], trow[:], AF.Exp, scale=-0.5)
                c_rstd.inc(inst)
                for i in range(32):
                    c_Ba.wait(s, B(c_Ba) + 32 * idx + i + 1)
                    if 32 * idx + i >= 2:
                        evw(s, 2 + i % 2, 16 * idx + i // 2)
                    ka = i % 2
                    inst = s.activation(ta[:, (i % 2) * 512:(i % 2 + 1) * 512], P.ps.tile(ka),
                                        AF.Silu, scale=s1[:, i:i + 1])
                    P.ps.release(ka, inst)
                if idx > 0:
                    c_D.wait(s, B(c_D) + 32 * idx)
                for i in range(32):
                    c_C.wait(s, B(c_C) + 32 * idx + i + 1)
                    kk = 4 + i % 2
                    inst = s.activation(g1b[:, i * 512:(i + 1) * 512], P.ps.tile(kk),
                                        AF.Gelu, scale=sh1[:, i:i + 1])
                    P.ps.release(kk, inst)
                c_C.wait(s, B(c_C) + 32 * (idx + 1))
                for i in range(32):
                    c_D.wait(s, B(c_D) + 32 * idx + i + 1)
                    kk = 4 + i % 2
                    inst = s.activation(hin[:, i * 512:(i + 1) * 512], P.ps.tile(kk),
                                        AF.Gelu, scale=sh2[:, i:i + 1])
                    P.ps.release(kk, inst)

        @blk.vector
        def _(v):
            for idx, b, t0 in tiles():
                c_stat.wait(v, B(c_stat) + idx + 1)
                if idx > 0:
                    c_bc.wait(v, B(c_bc) + idx)
                inst = v.tensor_scalar(mrow[:], P.ps.tile(6, 1, 512), 1.0 / D, None, op0=ALU.mult)
                P.ps.release(6, inst)
                inst = v.tensor_scalar(vrow[:], P.ps.tile(7, 1, 512), 1.0 / D, None, op0=ALU.mult)
                P.ps.release(7, inst)
                v.drain()
                v.tensor_tensor(trow[:], mrow[:], mrow[:], op=ALU.mult)
                v.drain()
                inst = v.tensor_tensor(vrow[:], vrow[:], trow[:], op=ALU.subtract)
                c_var.inc(inst)
                v.drain()
                c_rstd.wait(v, B(c_rstd) + idx + 1)
                inst = v.tensor_tensor(mrow[:], mrow[:], trow[:], op=ALU.mult)
                c_mr.inc(inst)
                v.drain()
                c_bc.wait(v, B(c_bc) + idx + 1)
                for dc in range(0, DC, 2):
                    s0 = slice(dc * 512, (dc + 1) * 512)
                    s1_ = slice((dc + 1) * 512, (dc + 2) * 512)
                    v.tensor_tensor(tb[:, 0:512], xr[:, s0], P.ps.tile(6), op=ALU.mult)
                    v.tensor_tensor(tb[:, 512:1024], xr[:, s1_], P.ps.tile(6), op=ALU.mult)
                    v.drain()
                    inst1 = v.tensor_tensor(xb[:, s0], tb[:, 0:512], P.ps.tile(7), op=ALU.subtract)
                    inst2 = v.tensor_tensor(xb[:, s1_], tb[:, 512:1024], P.ps.tile(7), op=ALU.subtract)
                    v.drain()
                    if dc == DC - 2:
                        P.ps.release(6, inst1); P.ps.release(7, inst2)
                if idx > 0:
                    c_E.wait(v, B(c_E) + 8 * idx)
                for i in range(32):
                    c_Bb.wait(v, B(c_Bb) + 32 * idx + i + 1)
                    kb = 2 + i % 2
                    evw(v, i % 2, 20 * idx + i // 2 + 1)
                    inst = v.scalar_tensor_tensor(hin[:, i * 512:(i + 1) * 512], P.ps.tile(kb),
                                                  s2[:, i:i + 1], ta[:, (i % 2) * 512:(i % 2 + 1) * 512],
                                                  op0=ALU.mult, op1=ALU.mult)
                    P.ps.release(kb, inst)
                if idx > 0:
                    p_ = (idx - 1) % 2
                    c_st[p_].wait(v, B(c_st[p_]) + 16 * ((idx - 1) // 2 + 1))
                for i in range(DC):
                    c_E.wait(v, B(c_E) + 8 * idx + i + 1)
                    kk = i % 2
                    inst = v.scalar_tensor_tensor(rout[:, i * 512:(i + 1) * 512], P.ps.tile(kk),
                                                  s3[:, i:i + 1], xr[:, i * 512:(i + 1) * 512],
                                                  op0=ALU.mult, op1=ALU.add)
                    P.ps.release(kk, inst)



def emit_attn_a0(P, src, dst):
    nc = P.nc
    wq = P.dscratch("wqT0_bf", (1024, 1024), BF16)
    wk = P.dscratch("wkT0_bf", (1024, 1024), BF16)
    wv = P.dscratch("wvT0_bf", (1024, 1024), BF16)
    wo = P.dscratch("woT0_bf", (1024, 1024), BF16)
    qbuf = P.dscratch("a0_qbuf", (B_CORE, DC, 128, S), BF16)
    kbuf = P.dscratch("a0_kbuf", (B_CORE, DC, 128, S), BF16)
    vtbuf = P.dscratch("a0_vtbuf", (B_CORE, S // 128, 128, 1024), BF16)

    xr = P.t("ffn_xr"); xb = P.t("ffn_xb"); xsq = P.t("ffn_xsq")
    hin = P.t("ffn_hin"); g1b = P.t("ffn_g1")
    ta = P.t("ffn_ta"); tb = P.t("ffn_tb")
    rout = P.t("ffn_rout")
    mrow = P.t("ffn_mrow"); vrow = P.t("ffn_vrow"); trow = P.t("ffn_trow")
    wab = P.t("ffn_wab")
    hh = P.t("ffn_hh")
    masks = P.t("masks")
    ones_b = P.t("ones_b"); onesr_f = P.t("onesr_f"); ident_b = P.t("ident_b")
    bq = P.t("sm_bq0"); bk = P.t("sm_bk0"); bv = P.t("sm_bv0"); bo = P.t("sm_bo0")
    bq32 = P.sb("a0_bq32", [128, DC], F32)

    q_sb = [hin.ap()[:, 0:4096], hin.ap()[:, 4096:8192]]
    k_sb = [hin.ap()[:, 8192:12288], hin.ap()[:, 12288:16384]]
    vt_sb = [g1b.ap()[:, 0:4096], g1b.ap()[:, 4096:8192]]
    vtmp = ta.ap().bitcast(BF16)
    P_all = hin.ap()
    qblk = g1b.ap()[:, 0:4096]
    out_blk = g1b.ap()[:, 4096:8192]
    KR = 4; VR = 4
    kring = g1b.ap()[:, 8192:8192 + KR * 1024]
    vring = g1b.ap()[:, 12288:12288 + VR * 1024]
    rb_sb = tb.ap()[:, 0:512]
    tmp_abs = [ta.ap()[:, 0:512], ta.ap()[:, 512:1024]]
    _dg = xsq.ap().bitcast(F32)
    tmp_dg = [_dg[:, 0:512], _dg[:, 512:1024]]

    Ct = lambda nm: P.ctr("a0_" + nm)
    c_ldx = Ct("ldx"); c_sq = Ct("sq"); c_stat = Ct("stat"); c_var = Ct("var")
    c_rstd = Ct("rstd"); c_mr = Ct("mr"); c_bc = Ct("bc")
    c_pq = Ct("pq"); c_pk = Ct("pk"); c_pv = Ct("pv"); c_ptr = Ct("ptr")
    c_wr = [Ct("wr0"), Ct("wr1"), Ct("wr2")]
    c_stq = [Ct("stq0"), Ct("stq1")]; c_stk = [Ct("stk0"), Ct("stk1")]; c_stv = [Ct("stv0"), Ct("stv1")]
    c_ldq = Ct("ldq"); c_ldx2 = Ct("ldx2")
    c_S = Ct("S"); c_abs = Ct("abs"); c_Pa = Ct("Pa"); c_Pd = Ct("Pd")
    c_den = Ct("den"); c_bcm = Ct("bcm"); c_rb = Ct("rb")
    c_AV = Ct("AV"); c_ave = Ct("ave"); c_Wo = Ct("Wo")
    c_kr = [Ct(f"kr{i}") for i in range(KR)]; c_vr = [Ct(f"vr{i}") for i in range(VR)]
    c_wor = [Ct("wor0"), Ct("wor1"), Ct("wor2")]
    c_sto = [Ct("sto0"), Ct("sto1")]
    ev = P.ps.ev
    c_pvv = Ct("pvv")
    allc = ([c_ldx, c_sq, c_stat, c_var, c_rstd, c_mr, c_bc, c_pq, c_pk, c_pv, c_ptr, c_pvv,
             c_ldq, c_ldx2, c_S, c_abs, c_Pa, c_Pd, c_den, c_bcm, c_rb, c_AV, c_ave, c_Wo]
            + c_wr + c_stq + c_stk + c_stv + c_kr + c_vr + c_wor + c_sto)
    base = {c: c.n for c in allc}
    B = lambda c: base[c]
    NTILE = B_CORE * NT

    evb = [e.n for e in ev]
    def evw(eng, kk, val):
        if val > 0:
            eng.wait_ge(ev[kk].sem, evb[kk] + val)
    with nc.Block() as blk:
        @blk.vector
        def _v0(v):
            v.tensor_scalar(bq32[:], bq[:], 1.0 / 32.0, None, op0=ALU.mult)
            v.drain()

        @blk.sync
        def _(sy):
            slot = 0
            for idx in range(NTILE):
                b, ti = divmod(idx, NT); t0 = ti * 512
                if idx > 0:
                    evw(sy, 6, 18 * (idx - 1) + 2)
                sy.dma_start(out=xr.ap().rearrange("p (c k) -> p c k", c=DC),
                             in_=src[b, :, :, t0:t0 + 512].transpose([1, 0, 2])
                             ).then_inc(c_ldx.sem, 16)
                c_ldx.n += 16
                for i in range(8):
                    if slot >= 3:
                        sidx, si = divmod(slot - 3, 8)
                        c_pv.wait(sy, B(c_pv) + 8 * sidx + si + 1)
                    cw = c_wr[slot % 3]
                    off = (slot % 3) * 4096
                    for wsrc, o2 in ((wq, 0), (wk, 1024), (wv, 2048)):
                        sy.dma_start(out=wab[:, off + o2:off + o2 + 1024].rearrange("p (k i) -> p k i", k=8),
                                     in_=wsrc[:, i * 128:(i + 1) * 128].rearrange("(k p) i -> p k i", p=128)
                                     ).then_inc(cw.sem, 16)
                        cw.n += 16
                    slot += 1
                if idx > 0:
                    pidx = idx - 1
                    pb, pti = divmod(pidx, NT); pt0 = pti * 512
                    par = pidx % 2
                    evw(sy, 0, 4 * (pidx + 1)); evw(sy, 1, 4 * (pidx + 1))
                    cs = c_stq[par]
                    sy.dma_start(out=qbuf[pb, :, :, pt0:pt0 + 512].transpose([1, 0, 2]),
                                 in_=q_sb[par].rearrange("p (c k) -> p c k", c=DC)).then_inc(cs.sem, 16)
                    cs.n += 16
                    evw(sy, 2, 4 * (pidx + 1)); evw(sy, 3, 4 * (pidx + 1))
                    cs = c_stk[par]
                    sy.dma_start(out=kbuf[pb, :, :, pt0:pt0 + 512].transpose([1, 0, 2]),
                                 in_=k_sb[par].rearrange("p (c k) -> p c k", c=DC)).then_inc(cs.sem, 16)
                    cs.n += 16
                    evw(sy, 6, 18 * (pidx + 1)); evw(sy, 7, 18 * (pidx + 1))
                    cs = c_stv[par]
                    sy.dma_start(out=vtbuf[pb, pti * 4:(pti + 1) * 4].rearrange("c p k -> p c k"),
                                 in_=vt_sb[par].rearrange("p (c k) -> p c k", c=4)).then_inc(cs.sem, 16)
                    cs.n += 16
            pidx = NTILE - 1
            pb, pti = divmod(pidx, NT); pt0 = pti * 512
            par = pidx % 2
            evw(sy, 0, 4 * (pidx + 1)); evw(sy, 1, 4 * (pidx + 1))
            cs = c_stq[par]
            sy.dma_start(out=qbuf[pb, :, :, pt0:pt0 + 512].transpose([1, 0, 2]),
                         in_=q_sb[par].rearrange("p (c k) -> p c k", c=DC)).then_inc(cs.sem, 16)
            cs.n += 16
            evw(sy, 2, 4 * (pidx + 1)); evw(sy, 3, 4 * (pidx + 1))
            cs = c_stk[par]
            sy.dma_start(out=kbuf[pb, :, :, pt0:pt0 + 512].transpose([1, 0, 2]),
                         in_=k_sb[par].rearrange("p (c k) -> p c k", c=DC)).then_inc(cs.sem, 16)
            cs.n += 16
            evw(sy, 6, 18 * (pidx + 1)); evw(sy, 7, 18 * (pidx + 1))
            cs = c_stv[par]
            sy.dma_start(out=vtbuf[pb, pti * 4:(pti + 1) * 4].rearrange("c p k -> p c k"),
                         in_=vt_sb[par].rearrange("p (c k) -> p c k", c=4)).then_inc(cs.sem, 16)
            cs.n += 16
            for cs in c_stq + c_stk + c_stv:
                sy.wait_ge(cs.sem, cs.n)

        @blk.tensor
        def _(t):
            slot = 0
            for idx in range(NTILE):
                c_sq.wait(t, B(c_sq) + idx + 1)
                evw(t, 6, 18 * idx); evw(t, 7, 18 * idx)
                for dc in range(DC):
                    t.matmul(P.ps.tile(6, 1, 512), ones_b[:], xb[:, dc * 512:(dc + 1) * 512],
                             start=(dc == 0), stop=(dc == DC - 1))
                for dc in range(DC):
                    inst = t.matmul(P.ps.tile(7, 1, 512), ones_b[:], xsq[:, dc * 512:(dc + 1) * 512],
                                    start=(dc == 0), stop=(dc == DC - 1))
                c_stat.inc(inst)
                c_mr.wait(t, B(c_mr) + idx + 1)
                evw(t, 6, 18 * idx + 1); evw(t, 7, 18 * idx + 1)
                t.matmul(P.ps.tile(6), onesr_f[:], trow[:], start=True, stop=True)
                inst = t.matmul(P.ps.tile(7), onesr_f[:], mrow[:], start=True, stop=True)
                c_bc.inc(inst)
                evw(t, 6, 18 * idx + 2); evw(t, 7, 18 * idx + 2)
                ntr = 0
                for i in range(8):
                    cw = c_wr[slot % 3]
                    cw.wait(t, B(cw) + 48 * (slot // 3 + 1))
                    off = (slot % 3) * 4096
                    evw(t, i % 2, 4 * idx + i // 2)
                    for k in range(8):
                        inst = t.matmul(P.ps.tile(i % 2), wab[:, off + k * 128:off + (k + 1) * 128],
                                        xb[:, k * 512:(k + 1) * 512], start=(k == 0), stop=(k == 7))
                    c_pq.inc(inst)
                    evw(t, 2 + i % 2, 4 * idx + i // 2)
                    for k in range(8):
                        inst = t.matmul(P.ps.tile(2 + i % 2), wab[:, off + 1024 + k * 128:off + 1024 + (k + 1) * 128],
                                        xb[:, k * 512:(k + 1) * 512], start=(k == 0), stop=(k == 7))
                    c_pk.inc(inst)
                    evw(t, 4 + i % 2, 4 * idx + i // 2)
                    for k in range(8):
                        inst = t.matmul(P.ps.tile(4 + i % 2), wab[:, off + 2048 + k * 128:off + 2048 + (k + 1) * 128],
                                        xb[:, k * 512:(k + 1) * 512], start=(k == 0), stop=(k == 7))
                    c_pv.inc(inst)
                    slot += 1
                    if i >= 1:
                        ii = i - 1
                        evw(t, 4 + ii % 2, 4 * idx + ii // 2 + 1)
                        for sub in range(4):
                            tr = ii * 4 + sub
                            evw(t, 6 + tr % 2, 18 * idx + 2 + tr // 2)
                            inst = t.transpose(P.ps.tile(6 + tr % 2).bitcast(BF16)[:, 0:128],
                                               vtmp[:, (ii % 2) * 512 + sub * 128:(ii % 2) * 512 + (sub + 1) * 128],
                                               ident_b[:])
                            c_ptr.inc(inst)
                ii = 7
                evw(t, 4 + ii % 2, 4 * idx + ii // 2 + 1)
                for sub in range(4):
                    tr = ii * 4 + sub
                    evw(t, 6 + tr % 2, 18 * idx + 2 + tr // 2)
                    inst = t.transpose(P.ps.tile(6 + tr % 2).bitcast(BF16)[:, 0:128],
                                       vtmp[:, (ii % 2) * 512 + sub * 128:(ii % 2) * 512 + (sub + 1) * 128],
                                       ident_b[:])
                    c_ptr.inc(inst)

        @blk.scalar
        def _(s):
            for idx in range(NTILE):
                par = idx % 2
                c_ldx.wait(s, B(c_ldx) + 16 * (idx + 1))
                if idx > 0:
                    c_pk.wait(s, B(c_pk) + 8 * idx)
                    c_pv.wait(s, B(c_pv) + 8 * idx)
                s.copy(xb[:], xr[:])
                s.drain()
                inst = s.activation(xsq[:], xb[:], AF.Square)
                c_sq.inc(inst)
                c_var.wait(s, B(c_var) + idx + 1)
                if idx > 0:
                    c_bc.wait(s, B(c_bc) + idx)
                s.activation(trow[:], vrow[:], AF.Ln, bias=P.t('eps_t')[0:1, :])
                s.drain()
                inst = s.activation(trow[:], trow[:], AF.Exp, scale=-0.5)
                c_rstd.inc(inst)
                if idx >= 2:
                    for cs in (c_stq[par], c_stk[par], c_stv[par]):
                        s.wait_ge(cs.sem, B(cs) + 16 * (idx // 2))
                for i in range(8):
                    c_pq.wait(s, B(c_pq) + 8 * idx + i + 1)
                    inst = s.activation(q_sb[par][:, i * 512:(i + 1) * 512], P.ps.tile(i % 2),
                                        AF.Identity, scale=1.0 / 32.0, bias=bq32[:, i:i + 1])
                    P.ps.release(i % 2, inst)
                    c_pk.wait(s, B(c_pk) + 8 * idx + i + 1)
                    inst = s.activation(k_sb[par][:, i * 512:(i + 1) * 512], P.ps.tile(2 + i % 2),
                                        AF.Identity, bias=bk[:, i:i + 1])
                    P.ps.release(2 + i % 2, inst)
                    c_pv.wait(s, B(c_pv) + 8 * idx + i + 1)
                    if i >= 2:
                        c_ptr.wait(s, B(c_ptr) + 32 * idx + 4 * (i - 1))
                    inst = s.activation(vtmp[:, (i % 2) * 512:(i % 2 + 1) * 512], P.ps.tile(4 + i % 2),
                                        AF.Identity, bias=bv[:, i:i + 1])
                    P.ps.release(4 + i % 2, inst)

        @blk.vector
        def _(v):
            for idx in range(NTILE):
                par = idx % 2
                c_stat.wait(v, B(c_stat) + idx + 1)
                if idx > 0:
                    c_bc.wait(v, B(c_bc) + idx)
                inst = v.tensor_scalar(mrow[:], P.ps.tile(6, 1, 512), 1.0 / D, None, op0=ALU.mult)
                P.ps.release(6, inst)
                inst = v.tensor_scalar(vrow[:], P.ps.tile(7, 1, 512), 1.0 / D, None, op0=ALU.mult)
                P.ps.release(7, inst)
                v.drain()
                v.tensor_tensor(trow[:], mrow[:], mrow[:], op=ALU.mult)
                v.drain()
                inst = v.tensor_tensor(vrow[:], vrow[:], trow[:], op=ALU.subtract)
                c_var.inc(inst)
                v.drain()
                c_rstd.wait(v, B(c_rstd) + idx + 1)
                inst = v.tensor_tensor(mrow[:], mrow[:], trow[:], op=ALU.mult)
                c_mr.inc(inst)
                v.drain()
                c_bc.wait(v, B(c_bc) + idx + 1)
                for dc in range(DC):
                    sl = slice(dc * 512, (dc + 1) * 512)
                    inst1 = v.tensor_tensor(tb[:, 0:512], xr[:, sl], P.ps.tile(6), op=ALU.mult)
                    v.drain()
                    inst2 = v.tensor_tensor(xb[:, sl], tb[:, 0:512], P.ps.tile(7), op=ALU.subtract)
                    if dc == DC - 1:
                        P.ps.release(6, inst1); P.ps.release(7, inst2)
                    v.drain()
                if idx >= 2:
                    cs = c_stv[par]
                    v.wait_ge(cs.sem, B(cs) + 16 * (idx // 2))
                for tr in range(32):
                    c_ptr.wait(v, B(c_ptr) + 32 * idx + tr + 1)
                    i, sub = divmod(tr, 4)
                    kk = 6 + tr % 2
                    inst = v.tensor_copy(vt_sb[par][:, sub * 1024 + i * 128: sub * 1024 + (i + 1) * 128],
                                         P.ps.tile(kk).bitcast(BF16)[:, 0:128])
                    P.ps.release(kk, inst)

    evb = [e.n for e in ev]

    cum01 = []; cum25 = []; cum67 = []; cumS = []; cumPd = []
    a0 = a25 = a67 = aS = aPd = 0
    for g in range(NTILE):
        qb = g % NT; nkt = 4 * qb + 4
        cum01.append(a0); cum25.append(a25); cum67.append(a67)
        cumS.append(aS); cumPd.append(aPd)
        a0 += (2 * qb + 2) + 1 + 4
        a25 += 1; a67 += 2; aS += nkt; aPd += 4
    totS = aS

    sig_k = []; sig_v = []
    for g in range(NTILE):
        qb = g % NT; nkt = 4 * qb + 4
        for kt in range(nkt):
            sig_k.append(B(c_S) + cumS[g] + kt + 1)
            sig_v.append(B(c_AV) + cumS[g] + kt + 1)

    c_wo16 = Ct("wo16")
    wo_sb = hh.ap()[:, 0:8192]

    with nc.Block() as blk:
        @blk.gpsimd
        def _(g_):
            g_.dma_start(out=wo_sb.rearrange("p (k i) -> p k i", k=8),
                         in_=wo.rearrange("(k p) i -> p k i", p=128)).then_inc(c_wo16.sem, 16)
            c_wo16.n += 16
            if P.deferred_p1:
                dc = DConv(P, "dca", P.deferred_p1, arena="ffn_wab")
                dc.finish(g_)

        @blk.sync
        def _(sy):
            kl = 0; vl = 0
            prev = None
            for g in range(NTILE):
                b, qb = divmod(g, NT); t0 = qb * 512; nkt = 4 * qb + 4
                if g > 0:
                    evw(sy, 0, cum01[g]); evw(sy, 1, cum01[g])
                    pb, pt0 = prev
                    cs = c_sto[(g - 1) % 2]
                    sy.dma_start(out=dst[pb, :, :, pt0:pt0 + 512].transpose([1, 0, 2]),
                                 in_=rout.ap().rearrange("p (c k) -> p c k", c=DC)).then_inc(cs.sem, 16)
                    cs.n += 16
                prev = (b, t0)
                sy.dma_start(out=xr.ap().rearrange("p (c k) -> p c k", c=DC),
                             in_=src[b, :, :, t0:t0 + 512].transpose([1, 0, 2])).then_inc(c_ldx2.sem, 16)
                c_ldx2.n += 16
                c_S.wait(sy, B(c_S) + cumS[g])
                sy.dma_start(out=qblk.rearrange("p (c k) -> p c k", c=DC),
                             in_=qbuf[b, :, :, t0:t0 + 512].transpose([1, 0, 2])).then_inc(c_ldq.sem, 16)
                c_ldq.n += 16
                for kt in range(nkt):
                    if kl >= KR:
                        c_S.wait(sy, sig_k[kl - KR])
                    cw = c_kr[kl % KR]
                    sy.dma_start(out=kring[:, (kl % KR) * 1024:(kl % KR + 1) * 1024].rearrange("p (c k) -> p c k", c=DC),
                                 in_=kbuf[b, :, :, kt * 128:(kt + 1) * 128].transpose([1, 0, 2])
                                 ).then_inc(cw.sem, 16)
                    cw.n += 16
                    kl += 1
                for kt in range(nkt):
                    if vl >= VR:
                        c_AV.wait(sy, sig_v[vl - VR])
                    cw = c_vr[vl % VR]
                    sy.dma_start(out=vring[:, (vl % VR) * 1024:(vl % VR + 1) * 1024],
                                 in_=vtbuf[b, kt]).then_inc(cw.sem, 16)
                    cw.n += 16
                    vl += 1
            evw(sy, 0, cum01[NTILE - 1] + (2 * (NT - 1) + 2) + 5)
            evw(sy, 1, cum01[NTILE - 1] + (2 * (NT - 1) + 2) + 5)
            pb, pt0 = prev
            cs = c_sto[(NTILE - 1) % 2]
            sy.dma_start(out=dst[pb, :, :, pt0:pt0 + 512].transpose([1, 0, 2]),
                         in_=rout.ap().rearrange("p (c k) -> p c k", c=DC)).then_inc(cs.sem, 16)
            cs.n += 16
            sy.wait_ge(c_sto[0].sem, c_sto[0].n)
            sy.wait_ge(c_sto[1].sem, c_sto[1].n)

        @blk.tensor
        def _(t):
            kl = 0; vl = 0
            c_wo16.wait(t)
            for g in range(NTILE):
                b, qb = divmod(g, NT); nkt = 4 * qb + 4
                c_ldq.wait(t, B(c_ldq) + 16 * (g + 1))
                def denom_mm(kt):
                    if kt == 0:
                        evw(t, 6, cum67[g])
                    c_Pa.wait(t, B(c_Pa) + cumS[g] + kt + 1)
                    if kt >= nkt - 4:
                        c_Pd.wait(t, B(c_Pd) + cumPd[g] + (kt - (nkt - 4)) + 1)
                    inst = t.matmul(P.ps.tile(6, 1, 512), ones_b[:], P_all[:, kt * 512:(kt + 1) * 512],
                                    start=(kt == 0), stop=(kt == nkt - 1))
                    if kt == nkt - 1:
                        c_den.inc(inst)
                for kt in range(nkt):
                    cw = c_kr[kl % KR]
                    cw.wait(t, B(cw) + 16 * (kl // KR + 1))
                    evw(t, kt % 2, cum01[g] + kt // 2)
                    koff = (kl % KR) * 1024
                    for dc in range(DC):
                        inst = t.matmul(P.ps.tile(kt % 2), kring[:, koff + dc * 128:koff + (dc + 1) * 128],
                                        qblk[:, dc * 512:(dc + 1) * 512], start=(dc == 0), stop=(dc == DC - 1))
                    c_S.inc(inst)
                    kl += 1
                    if kt >= 2:
                        denom_mm(kt - 2)
                denom_mm(nkt - 2); denom_mm(nkt - 1)
                evw(t, 6, cum67[g] + 1)
                evw(t, 7, cum67[g])
                inst = t.matmul(P.ps.tile(7), onesr_f[:], mrow[:], start=True, stop=True)
                c_bcm.inc(inst)
                for kt in range(nkt):
                    cw = c_vr[vl % VR]
                    cw.wait(t, B(cw) + 16 * (vl // VR + 1))
                    voff = (vl % VR) * 1024
                    if kt == 0:
                        evw(t, 0, cum01[g] + 2 * qb + 2); evw(t, 1, cum01[g] + 2 * qb + 2)
                        for kk in range(2, 6):
                            evw(t, kk, cum25[g])
                        evw(t, 6, cum67[g] + 1); evw(t, 7, cum67[g] + 1)
                    for vd in range(8):
                        inst = t.matmul(P.ps.tile(vd), vring[:, voff + vd * 128:voff + (vd + 1) * 128],
                                        P_all[:, kt * 512:(kt + 1) * 512], start=(kt == 0), stop=(kt == nkt - 1))
                    c_AV.inc(inst)
                    vl += 1
                for kk in range(2, 6):
                    evw(t, kk, cum25[g] + 1)
                evw(t, 6, cum67[g] + 2); evw(t, 7, cum67[g] + 2)
                for i in range(8):
                    evw(t, i % 2, cum01[g] + (2 * qb + 2) + 1 + i // 2)
                    for k in range(8):
                        inst = t.matmul(P.ps.tile(i % 2), wo_sb[:, k * 1024 + i * 128:k * 1024 + (i + 1) * 128],
                                        out_blk[:, k * 512:(k + 1) * 512], start=(k == 0), stop=(k == 7))
                    c_Wo.inc(inst)

        @blk.scalar
        def _(s):
            for g in range(NTILE):
                b, qb = divmod(g, NT); nkt = 4 * qb + 4
                if g > 0:
                    c_AV.wait(s, B(c_AV) + cumS[g])
                for kt in range(nkt):
                    c_S.wait(s, B(c_S) + cumS[g] + kt + 1)
                    inst = s.activation(tmp_abs[kt % 2], P.ps.tile(kt % 2), AF.Abs)
                    P.ps.release(kt % 2, inst)
                    s.drain()
                    diag = kt >= nkt - 4
                    if diag:
                        mi = kt - (nkt - 4)
                        if mi >= 2:
                            c_Pd.wait(s, B(c_Pd) + cumPd[g] + mi - 1)
                        out_ap = tmp_dg[kt % 2]
                    else:
                        out_ap = P_all[:, kt * 512:(kt + 1) * 512]
                    inst = s.activation(out_ap, tmp_abs[kt % 2], AF.Exp)
                    c_Pa.inc(inst)
                    s.drain()

        @blk.vector
        def _(v):
            for g in range(NTILE):
                b, qb = divmod(g, NT); nkt = 4 * qb + 4
                def mask_op(ktd):
                    mi = ktd - (nkt - 4)
                    c_Pa.wait(v, B(c_Pa) + cumS[g] + ktd + 1)
                    inst = v.tensor_tensor(P_all[:, ktd * 512:(ktd + 1) * 512], tmp_dg[ktd % 2],
                                           masks[:, mi * 512:(mi + 1) * 512], op=ALU.mult)
                    c_Pd.inc(inst)
                for ktd in range(max(0, nkt - 4), nkt):
                    mask_op(ktd)
                c_den.wait(v, B(c_den) + g + 1)
                inst = v.reciprocal(mrow[:], P.ps.tile(6, 1, 512))
                P.ps.release(6, inst)
                v.drain()
                c_bcm.wait(v, B(c_bcm) + g + 1)
                inst = v.tensor_copy(rb_sb, P.ps.tile(7))
                P.ps.release(7, inst)
                v.drain()
                c_AV.wait(v, B(c_AV) + cumS[g] + nkt)
                for vd in range(8):
                    inst = v.tensor_tensor(out_blk[:, vd * 512:(vd + 1) * 512], P.ps.tile(vd), rb_sb, op=ALU.mult)
                    P.ps.release(vd, inst)
                if g > 0:
                    p_ = (g - 1) % 2
                    c_sto[p_].wait(v, B(c_sto[p_]) + 16 * ((g - 1) // 2 + 1))
                c_ldx2.wait(v, B(c_ldx2) + 16 * (g + 1))
                for i in range(8):
                    c_Wo.wait(v, B(c_Wo) + 8 * g + i + 1)
                    inst = v.scalar_tensor_tensor(rout.ap()[:, i * 512:(i + 1) * 512], P.ps.tile(i % 2),
                                                  bo[:, i:i + 1], xr[:, i * 512:(i + 1) * 512],
                                                  op0=ALU.add, op1=ALU.add)
                    P.ps.release(i % 2, inst)



def emit_qkv_phaseP(P, tag, w1, b1c, qbuf, kbuf, ktbuf, vtbuf, src):
    nc = P.nc

    xr = P.t("ffn_xr"); xb = P.t("ffn_xb"); xsq = P.t("ffn_xsq")
    hin = P.t("ffn_hin"); g1b = P.t("ffn_g1")
    ta = P.t("ffn_ta"); tb = P.t("ffn_tb")
    rout = P.t("ffn_rout")
    mrow = P.t("ffn_mrow"); vrow = P.t("ffn_vrow"); trow = P.t("ffn_trow")
    wab = P.t("ffn_wab"); hh = P.t("ffn_hh")
    masks = P.t("masks")
    ones_b = P.t("ones_b"); onesr_f = P.t("onesr_f"); ident_b = P.t("ident_b")

    q_sb = [hin.ap()[:, 0:4096], hin.ap()[:, 4096:8192]]
    k_sb = [hin.ap()[:, 8192:12288], hin.ap()[:, 12288:16384]]
    vt_sb = [g1b.ap()[:, 0:4096], g1b.ap()[:, 4096:8192]]
    kt_sb = [g1b.ap()[:, 8192:12288], g1b.ap()[:, 12288:16384]]
    vtmp = ta.ap().bitcast(BF16)

    Ct = lambda nm: P.ctr(tag + "_" + nm)
    c_ldx = Ct("ldx"); c_sq = Ct("sq"); c_stat = Ct("stat"); c_var = Ct("var")
    c_rstd = Ct("rstd"); c_mr = Ct("mr"); c_bc = Ct("bc")
    c_pq = Ct("pq"); c_pk = Ct("pk"); c_pv = Ct("pv")
    c_ek = Ct("ek"); c_ptrk = Ct("ptrk"); c_ptrv = Ct("ptrv")
    c_wr = [Ct("wr0"), Ct("wr1"), Ct("wr2")]
    c_stq = [Ct("stq0"), Ct("stq1")]; c_stk = [Ct("stk0"), Ct("stk1")]
    c_stkt = [Ct("stkt0"), Ct("stkt1")]; c_stv = [Ct("stv0"), Ct("stv1")]
    ev = P.ps.ev
    allc = ([c_ldx, c_sq, c_stat, c_var, c_rstd, c_mr, c_bc, c_pq, c_pk, c_pv,
             c_ek, c_ptrk, c_ptrv]
            + c_wr + c_stq + c_stk + c_stkt + c_stv)
    base = {c: c.n for c in allc}
    B = lambda c: base[c]
    NTILE = B_CORE * NT

    evb = [e.n for e in ev]
    def evw(eng, kk, val):
        if val > 0:
            eng.wait_ge(ev[kk].sem, evb[kk] + val)
    with nc.Block() as blk:
        @blk.sync
        def _(sy):
            slot = 0
            for idx in range(NTILE):
                b, ti = divmod(idx, NT); t0 = ti * 512
                if idx > 0:
                    evw(sy, 6, 34 * (idx - 1) + 2)
                sy.dma_start(out=xr.ap().rearrange("p (c k) -> p c k", c=DC),
                             in_=src[b, :, :, t0:t0 + 512].transpose([1, 0, 2])
                             ).then_inc(c_ldx.sem, 16)
                c_ldx.n += 16
                for i in range(24):
                    if slot >= 3:
                        sidx, si = divmod(slot - 3, 24)
                        gctr = c_pk if si < 8 else (c_pq if si < 16 else c_pv)
                        gctr.wait(sy, B(gctr) + 8 * sidx + (si % 8) + 1)
                    cw = c_wr[slot % 3]
                    off = (slot % 3) * 4096
                    sy.dma_start(out=wab[:, off:off + 1024].rearrange("p (k i) -> p k i", k=8),
                                 in_=w1[:, i * 128:(i + 1) * 128].rearrange("(k p) i -> p k i", p=128)
                                 ).then_inc(cw.sem, 16)
                    cw.n += 16
                    slot += 1
                def stores(pidx):
                    pb, pti = divmod(pidx, NT); pt0 = pti * 512
                    par = pidx % 2
                    evw(sy, 0, 4 * (pidx + 1)); evw(sy, 1, 4 * (pidx + 1))
                    cs = c_stq[par]
                    sy.dma_start(out=qbuf[pb, :, :, pt0:pt0 + 512].transpose([1, 0, 2]),
                                 in_=q_sb[par].rearrange("p (c k) -> p c k", c=DC)).then_inc(cs.sem, 16)
                    cs.n += 16
                    evw(sy, 2, 4 * (pidx + 1)); evw(sy, 3, 4 * (pidx + 1))
                    cs = c_stk[par]
                    sy.dma_start(out=kbuf[pb, :, :, pt0:pt0 + 512].transpose([1, 0, 2]),
                                 in_=k_sb[par].rearrange("p (c k) -> p c k", c=DC)).then_inc(cs.sem, 16)
                    cs.n += 16
                    evw(sy, 6, 34 * (pidx + 1)); evw(sy, 7, 34 * (pidx + 1))
                    cs = c_stkt[par]
                    sy.dma_start(out=ktbuf[pb, pti * 4:(pti + 1) * 4].rearrange("c p k -> p c k"),
                                 in_=kt_sb[par].rearrange("p (c k) -> p c k", c=4)).then_inc(cs.sem, 16)
                    cs.n += 16
                    cs = c_stv[par]
                    sy.dma_start(out=vtbuf[pb, pti * 4:(pti + 1) * 4].rearrange("c p k -> p c k"),
                                 in_=vt_sb[par].rearrange("p (c k) -> p c k", c=4)).then_inc(cs.sem, 16)
                    cs.n += 16
                if idx > 0:
                    stores(idx - 1)
            stores(NTILE - 1)
            for cs in c_stq + c_stk + c_stkt + c_stv:
                sy.wait_ge(cs.sem, cs.n)

        @blk.tensor
        def _(t):
            slot = 0
            for idx in range(NTILE):
                c_sq.wait(t, B(c_sq) + idx + 1)
                evw(t, 6, 34 * idx); evw(t, 7, 34 * idx)
                for dc in range(DC):
                    t.matmul(P.ps.tile(6, 1, 512), ones_b[:], xb[:, dc * 512:(dc + 1) * 512],
                             start=(dc == 0), stop=(dc == DC - 1))
                for dc in range(DC):
                    inst = t.matmul(P.ps.tile(7, 1, 512), ones_b[:], xsq[:, dc * 512:(dc + 1) * 512],
                                    start=(dc == 0), stop=(dc == DC - 1))
                c_stat.inc(inst)
                c_mr.wait(t, B(c_mr) + idx + 1)
                evw(t, 6, 34 * idx + 1); evw(t, 7, 34 * idx + 1)
                t.matmul(P.ps.tile(6), onesr_f[:], trow[:], start=True, stop=True)
                inst = t.matmul(P.ps.tile(7), onesr_f[:], mrow[:], start=True, stop=True)
                c_bc.inc(inst)
                evw(t, 6, 34 * idx + 2); evw(t, 7, 34 * idx + 2)
                par = idx % 2

                def vtrs(ii):
                    for sub in range(4):
                        tr = ii * 4 + sub
                        evw(t, 4 + ii % 2, 4 * idx + ii // 2 + 1)
                        evw(t, 6 + tr % 2, 34 * idx + 2 + tr // 2)
                        inst = t.transpose(P.ps.tile(6 + tr % 2).bitcast(BF16)[:, 0:128],
                                           vtmp[:, (ii % 2) * 512 + sub * 128:(ii % 2) * 512 + (sub + 1) * 128],
                                           ident_b[:])
                        c_ptrv.inc(inst)
                for i in range(24):
                    cw = c_wr[slot % 3]
                    cw.wait(t, B(cw) + 16 * (slot // 3 + 1))
                    off = (slot % 3) * 4096
                    if i < 8:
                        bank = 2 + i % 2; ctr = c_pk; rel = 4 * idx + i // 2
                    elif i < 16:
                        bank = (i - 8) % 2; ctr = c_pq; rel = 4 * idx + (i - 8) // 2
                    else:
                        bank = 4 + (i - 16) % 2; ctr = c_pv; rel = 4 * idx + (i - 16) // 2
                    evw(t, bank, rel)
                    for k in range(8):
                        inst = t.matmul(P.ps.tile(bank), wab[:, off + k * 128:off + (k + 1) * 128],
                                        xb[:, k * 512:(k + 1) * 512], start=(k == 0), stop=(k == 7))
                    ctr.inc(inst)
                    slot += 1
                    if i >= 17:
                        vtrs(i - 17)
                vtrs(7)
                for trk in range(32):
                    i, sub = divmod(trk, 4)
                    tr = 32 + trk
                    evw(t, 2 + i % 2, 4 * idx + i // 2 + 1)
                    evw(t, 6 + tr % 2, 34 * idx + 2 + tr // 2)
                    inst = t.transpose(P.ps.tile(6 + tr % 2).bitcast(BF16)[:, 0:128],
                                       k_sb[par][:, i * 512 + sub * 128: i * 512 + (sub + 1) * 128],
                                       ident_b[:])
                    c_ptrk.inc(inst)

        @blk.scalar
        def _(s):
            for idx in range(NTILE):
                par = idx % 2
                c_ldx.wait(s, B(c_ldx) + 16 * (idx + 1))
                if idx > 0:
                    c_pv.wait(s, B(c_pv) + 8 * idx)
                s.copy(xb[:], xr[:])
                s.drain()
                inst = s.activation(xsq[:], xb[:], AF.Square)
                c_sq.inc(inst)
                c_var.wait(s, B(c_var) + idx + 1)
                if idx > 0:
                    c_bc.wait(s, B(c_bc) + idx)
                s.activation(trow[:], vrow[:], AF.Ln, bias=P.t('eps_t')[0:1, :])
                s.drain()
                inst = s.activation(trow[:], trow[:], AF.Exp, scale=-0.5)
                c_rstd.inc(inst)
                if idx >= 2:
                    for cs in (c_stq[par], c_stk[par], c_stkt[par], c_stv[par]):
                        s.wait_ge(cs.sem, B(cs) + 16 * (idx // 2))
                    c_ptrk.wait(s, B(c_ptrk) + 32 * (idx - 1))
                for i in range(8):
                    c_pk.wait(s, B(c_pk) + 8 * idx + i + 1)
                    inst = s.activation(k_sb[par][:, i * 512:(i + 1) * 512], P.ps.tile(2 + i % 2),
                                        AF.Identity, bias=b1c[:, i:i + 1])
                    P.ps.release(2 + i % 2, inst)
                for i in range(8):
                    c_pq.wait(s, B(c_pq) + 8 * idx + i + 1)
                    inst = s.activation(q_sb[par][:, i * 512:(i + 1) * 512], P.ps.tile(i % 2),
                                        AF.Identity, bias=b1c[:, 8 + i:9 + i])
                    P.ps.release(i % 2, inst)
                for i in range(8):
                    c_pv.wait(s, B(c_pv) + 8 * idx + i + 1)
                    if i >= 2:
                        c_ptrv.wait(s, B(c_ptrv) + 32 * idx + 4 * (i - 1))
                    inst = s.activation(vtmp[:, (i % 2) * 512:(i % 2 + 1) * 512], P.ps.tile(4 + i % 2),
                                        AF.Identity, bias=b1c[:, 16 + i:17 + i])
                    P.ps.release(4 + i % 2, inst)

        @blk.vector
        def _(v):
            for idx in range(NTILE):
                par = idx % 2
                c_stat.wait(v, B(c_stat) + idx + 1)
                if idx > 0:
                    c_bc.wait(v, B(c_bc) + idx)
                inst = v.tensor_scalar(mrow[:], P.ps.tile(6, 1, 512), 1.0 / D, None, op0=ALU.mult)
                P.ps.release(6, inst)
                inst = v.tensor_scalar(vrow[:], P.ps.tile(7, 1, 512), 1.0 / D, None, op0=ALU.mult)
                P.ps.release(7, inst)
                v.drain()
                v.tensor_tensor(trow[:], mrow[:], mrow[:], op=ALU.mult)
                v.drain()
                inst = v.tensor_tensor(vrow[:], vrow[:], trow[:], op=ALU.subtract)
                c_var.inc(inst)
                v.drain()
                c_rstd.wait(v, B(c_rstd) + idx + 1)
                inst = v.tensor_tensor(mrow[:], mrow[:], trow[:], op=ALU.mult)
                c_mr.inc(inst)
                v.drain()
                c_bc.wait(v, B(c_bc) + idx + 1)
                for dc in range(DC):
                    sl = slice(dc * 512, (dc + 1) * 512)
                    inst1 = v.tensor_tensor(tb[:, 0:512], xr[:, sl], P.ps.tile(6), op=ALU.mult)
                    v.drain()
                    inst2 = v.tensor_tensor(xb[:, sl], tb[:, 0:512], P.ps.tile(7), op=ALU.subtract)
                    if dc == DC - 1:
                        P.ps.release(6, inst1); P.ps.release(7, inst2)
                    v.drain()
                if idx >= 2:
                    for cs in (c_stkt[par], c_stv[par]):
                        v.wait_ge(cs.sem, B(cs) + 16 * (idx // 2))
                for tr in range(64):
                    (c_ptrv if tr < 32 else c_ptrk).wait(
                        v, (B(c_ptrv) if tr < 32 else B(c_ptrk)) + 32 * idx + (tr % 32) + 1)
                    kk = 6 + tr % 2
                    if tr < 32:
                        i, sub = divmod(tr, 4)
                        dst_ap = vt_sb[par][:, sub * 1024 + i * 128: sub * 1024 + (i + 1) * 128]
                    else:
                        i, sub = divmod(tr - 32, 4)
                        dst_ap = kt_sb[par][:, sub * 1024 + i * 128: sub * 1024 + (i + 1) * 128]
                    inst = v.tensor_copy(dst_ap, P.ps.tile(kk).bitcast(BF16)[:, 0:128])
                    P.ps.release(kk, inst)


def emit_attn_a1(P, src, dst):
    nc = P.nc
    w1 = P.dscratch("w1T1_bf", (1024, 3072), BF16)
    w2 = P.dscratch("w2T1_bf", (1024, 1024), BF16)
    qbuf = P.dscratch("a1_qbuf", (B_CORE, DC, 128, S), BF16)
    kbuf = P.dscratch("a1_kbuf", (B_CORE, DC, 128, S), BF16)
    ktbuf = P.dscratch("a1_ktbuf", (B_CORE, S // 128, 128, 1024), BF16)
    vtbuf = P.dscratch("a1_vtbuf", (B_CORE, S // 128, 128, 1024), BF16)
    b1c = P.t("sm_b1_1"); b2c = P.t("sm_b2_1")
    emit_qkv_phaseP(P, "a1p", w1, b1c, qbuf, kbuf, ktbuf, vtbuf, src)

    xr = P.t("ffn_xr")
    hin = P.t("ffn_hin"); g1b = P.t("ffn_g1")
    rout = P.t("ffn_rout")
    hh = P.t("ffn_hh")
    masks = P.t("masks")
    qblk = hin.ap()[:, 0:4096]
    kblk = hin.ap()[:, 4096:8192]
    ktblk = hin.ap()[:, 8192:12288]
    vtblk = hin.ap()[:, 12288:16384]
    yall = g1b.ap()[:, 0:4096]
    Abuf = g1b.ap().bitcast(BF16)[:, 4096 * 2:4096 * 2 + 1024]
    Mf = P.sb("a1_Mf", [128, 8 * 64], F32)
    Mb = P.sb("a1_Mb", [128, 8 * 64], BF16)
    w2_sb = hh.ap()[:, 0:8192]
    Ct = lambda nm: P.ctr("a1_" + nm)
    c_ldq = Ct("ldq"); c_ldk = Ct("ldk"); c_ldkt = Ct("ldkt"); c_ldvt = Ct("ldvt"); c_ldx2 = Ct("ldx2")
    c_S = Ct("S"); c_ymm = Ct("ymm"); c_dM = Ct("dM"); c_Mu = Ct("Mu")
    c_W2 = Ct("W2"); c_wo16 = Ct("wo16")
    c_sto = [Ct("sto0"), Ct("sto1")]
    ev = P.ps.ev
    allc = [c_ldq, c_ldk, c_ldkt, c_ldvt, c_ldx2, c_S, c_ymm, c_dM, c_Mu, c_W2, c_wo16] + c_sto
    base = {c: c.n for c in allc}
    B = lambda c: base[c]
    NTILE = B_CORE * NT
    evb = [e.n for e in ev]
    def evw(eng, kk, val):
        if val > 0:
            eng.wait_ge(ev[kk].sem, evb[kk] + val)
    with nc.Block() as blk:
        @blk.gpsimd
        def _(g_):
            g_.dma_start(out=w2_sb.rearrange("p (k i) -> p k i", k=8),
                         in_=w2.rearrange("(k p) i -> p k i", p=128)).then_inc(c_wo16.sem, 16)
            c_wo16.n += 16

        @blk.sync
        def _(sy):
            prev = None
            for g in range(NTILE):
                b, bk = divmod(g, NT); t0 = bk * 512
                if g > 0:
                    evw(sy, 6, 4 * g); evw(sy, 7, 4 * g)
                    pb, pt0 = prev
                    cs = c_sto[(g - 1) % 2]
                    sy.dma_start(out=dst[pb, :, :, pt0:pt0 + 512].transpose([1, 0, 2]),
                                 in_=rout.ap().rearrange("p (c k) -> p c k", c=DC)).then_inc(cs.sem, 16)
                    cs.n += 16
                prev = (b, t0)
                sy.dma_start(out=xr.ap().rearrange("p (c k) -> p c k", c=DC),
                             in_=src[b, :, :, t0:t0 + 512].transpose([1, 0, 2])).then_inc(c_ldx2.sem, 16)
                c_ldx2.n += 16
                c_S.wait(sy, B(c_S) + 64 * g)
                sy.dma_start(out=qblk.rearrange("p (c k) -> p c k", c=DC),
                             in_=qbuf[b, :, :, t0:t0 + 512].transpose([1, 0, 2])).then_inc(c_ldq.sem, 16)
                c_ldq.n += 16
                sy.dma_start(out=kblk.rearrange("p (c k) -> p c k", c=DC),
                             in_=kbuf[b, :, :, t0:t0 + 512].transpose([1, 0, 2])).then_inc(c_ldk.sem, 16)
                c_ldk.n += 16
                c_dM.wait(sy, B(c_dM) + 16 * g)
                sy.dma_start(out=ktblk.rearrange("p (c k) -> p c k", c=4),
                             in_=ktbuf[b, bk * 4:(bk + 1) * 4].rearrange("c p k -> p c k")).then_inc(c_ldkt.sem, 16)
                c_ldkt.n += 16
                sy.dma_start(out=vtblk.rearrange("p (c k) -> p c k", c=4),
                             in_=vtbuf[b, bk * 4:(bk + 1) * 4].rearrange("c p k -> p c k")).then_inc(c_ldvt.sem, 16)
                c_ldvt.n += 16
            evw(sy, 6, 4 * NTILE); evw(sy, 7, 4 * NTILE)
            pb, pt0 = prev
            cs = c_sto[(NTILE - 1) % 2]
            sy.dma_start(out=dst[pb, :, :, pt0:pt0 + 512].transpose([1, 0, 2]),
                         in_=rout.ap().rearrange("p (c k) -> p c k", c=DC)).then_inc(cs.sem, 16)
            cs.n += 16
            sy.wait_ge(c_sto[0].sem, c_sto[0].n)
            sy.wait_ge(c_sto[1].sem, c_sto[1].n)

        @blk.tensor
        def _(t):
            c_wo16.wait(t)
            for g in range(NTILE):
                b, bk = divmod(g, NT)
                c_ldq.wait(t, B(c_ldq) + 16 * (g + 1))
                c_ldk.wait(t, B(c_ldk) + 16 * (g + 1))
                c_ldkt.wait(t, B(c_ldkt) + 16 * (g + 1))
                c_ldvt.wait(t, B(c_ldvt) + 16 * (g + 1))
                if bk > 0:
                    c_Mu.wait(t, B(c_Mu) + 16 * g)
                for h in range(16):
                    hp, hf = divmod(h, 2)
                    pb_ = 64 * hf
                    yb = h % 2
                    evw(t, yb, 8 * g + h // 2)
                    if bk > 0:
                        t.matmul(P.ps.tile(yb, 64, 512), Mb[pb_:pb_ + 64, hp * 64:(hp + 1) * 64],
                                 qblk[pb_:pb_ + 64, hp * 512:(hp + 1) * 512], start=True, stop=False)
                    for sub in range(4):
                        sb_ = (h * 4 + sub) % 2
                        evw(t, 2 + sb_, 32 * g + (h * 4 + sub) // 2)
                        inst = t.matmul(P.ps.tile(2 + sb_), kblk[pb_:pb_ + 64, hp * 512 + sub * 128:hp * 512 + (sub + 1) * 128],
                                        qblk[pb_:pb_ + 64, hp * 512:(hp + 1) * 512], start=True, stop=True)
                        c_S.inc(inst)
                    for sub in range(4):
                        j = 4 * h + sub
                        evw(t, 2 + j % 2, 32 * g + j // 2 + 1)
                        inst = t.matmul(P.ps.tile(yb, 64, 512), vtblk[:, sub * 1024 + h * 64:sub * 1024 + (h + 1) * 64],
                                        Abuf[:, ((h * 4 + sub) % 2) * 512:(((h * 4 + sub) % 2) + 1) * 512],
                                        start=(bk == 0 and sub == 0), stop=(sub == 3))
                        c_ymm.inc(inst)
                    mb = 4 + h % 2
                    evw(t, mb, 8 * g + h // 2)
                    for sub in range(4):
                        inst = t.matmul(P.ps.tile(mb, 64, 64), ktblk[:, sub * 1024 + h * 64:sub * 1024 + (h + 1) * 64],
                                        vtblk[:, sub * 1024 + h * 64:sub * 1024 + (h + 1) * 64],
                                        start=(sub == 0), stop=(sub == 3))
                    c_dM.inc(inst)
                evw(t, 0, 8 * (g + 1)); evw(t, 1, 8 * (g + 1))
                for i in range(8):
                    evw(t, 6 + i % 2, 4 * g + i // 2)
                    for k in range(8):
                        inst = t.matmul(P.ps.tile(6 + i % 2), w2_sb[:, k * 1024 + i * 128:k * 1024 + (i + 1) * 128],
                                        yall[:, k * 512:(k + 1) * 512], start=(k == 0), stop=(k == 7))
                    c_W2.inc(inst)

        @blk.scalar
        def _(s):
            for g in range(NTILE):
                b, bk = divmod(g, NT)
                if g > 0:
                    c_W2.wait(s, B(c_W2) + 8 * g)
                for h in range(16):
                    hp, hf = divmod(h, 2)
                    c_ymm.wait(s, B(c_ymm) + 64 * g + 4 * h + 4)
                    yb = h % 2
                    inst = s.copy(yall[64 * hf:64 * hf + 64, hp * 512:(hp + 1) * 512],
                                  P.ps.tile(yb, 64, 512))
                    P.ps.release(yb, inst)
                    s.drain()

        @blk.vector
        def _(v):
            for g in range(NTILE):
                b, bk = divmod(g, NT)
                if bk == 0:
                    c_dM.wait(v, B(c_dM) + 16 * g)
                    v.memset(Mf[:], 0.0)
                    v.drain()
                for h in range(16):
                    for sub in range(4):
                        j = 4 * h + sub
                        c_S.wait(v, B(c_S) + 64 * g + j + 1)
                        if 64 * g + j >= 2:
                            c_ymm.wait(v, B(c_ymm) + 64 * g + j - 1)
                        sb_ = j % 2
                        inst = v.tensor_tensor(Abuf[:, sb_ * 512:(sb_ + 1) * 512], P.ps.tile(2 + sb_),
                                               masks[:, sub * 512:(sub + 1) * 512], op=ALU.mult)
                        P.ps.release(2 + sb_, inst)
                    hp, hf = divmod(h, 2)
                    mb = 4 + h % 2
                    c_dM.wait(v, B(c_dM) + 16 * g + h + 1)
                    msl = Mf[64 * hf:64 * hf + 64, hp * 64:(hp + 1) * 64]
                    inst = v.tensor_tensor(msl, P.ps.tile(mb, 64, 64), msl, op=ALU.add)
                    P.ps.release(mb, inst)
                    v.drain()
                    inst = v.tensor_copy(Mb[64 * hf:64 * hf + 64, hp * 64:(hp + 1) * 64], msl)
                    c_Mu.inc(inst)
                    v.drain()
                if g > 0:
                    p_ = (g - 1) % 2
                    c_sto[p_].wait(v, B(c_sto[p_]) + 16 * ((g - 1) // 2 + 1))
                c_ldx2.wait(v, B(c_ldx2) + 16 * (g + 1))
                for i in range(8):
                    c_W2.wait(v, B(c_W2) + 8 * g + i + 1)
                    inst = v.scalar_tensor_tensor(rout.ap()[:, i * 512:(i + 1) * 512], P.ps.tile(6 + i % 2),
                                                  b2c[:, i:i + 1], xr[:, i * 512:(i + 1) * 512],
                                                  op0=ALU.add, op1=ALU.add)
                    P.ps.release(6 + i % 2, inst)




def emit_attn_b0(P, src, dst):
    nc = P.nc
    w1 = P.dscratch("wb0_bf", (1024, 3072), BF16)
    wo = P.dscratch("toT_bf", (1024, 1024), BF16)
    qbuf = P.dscratch("b0_qbuf", (B_CORE, DC, 128, S), BF16)
    kbuf = P.dscratch("b0_kbuf", (B_CORE, DC, 128, S), BF16)
    ktbuf = P.dscratch("b0_ktbuf", (B_CORE, S // 128, 128, 1024), BF16)
    vtbuf = P.dscratch("b0_vtbuf", (B_CORE, S // 128, 128, 1024), BF16)
    b0z = P.t("sm_b0z")
    emit_qkv_phaseP(P, "b0p", w1, b0z, qbuf, kbuf, ktbuf, vtbuf, src)

    RRD = float(np.sqrt(np.sqrt(16.0)))
    CQ2 = float(1.0 / (np.sqrt(2.0) * 4.0))
    S32 = float(1.0 / np.sqrt(32.0))
    RMS_EPS = 1e-5

    xr = P.t("ffn_xr")
    hin = P.t("ffn_hin"); g1b = P.t("ffn_g1")
    ta = P.t("ffn_ta"); tb = P.t("ffn_tb")
    rout = P.t("ffn_rout")
    mrow = P.t("ffn_mrow"); vrow = P.t("ffn_vrow")
    hh = P.t("ffn_hh")
    masks = P.t("masks")
    ones_b = P.t("ones_b"); onesr_f = P.t("onesr_f")
    gcol = P.t("sm_tg")
    iw = P.t("iota_i32"); itf = P.t("iota_f")

    qblk = hin.ap()[:, 0:4096]
    kblk = hin.ap()[:, 4096:8192]
    ktblk = hin.ap()[:, 8192:12288]
    vtblk = hin.ap()[:, 12288:16384]
    g1h = g1b.ap()
    yall = g1h[:, 0:4096]
    qf_c0 = g1h[:, 4096:4608]; qf_c1 = g1h[:, 4608:5120]
    Abuf = g1h[:, 5120:7168]
    kfT_s = [g1h[:, 7168:7168 + 4 * 273], g1h[:, 15016:15016 + 4 * 273]]
    ysq_sb = g1h[:, 8360:8872]
    xbv = P.t("ffn_xb").ap()
    xsqv = P.t("ffn_xsq").ap()
    y_sb = xbv.bitcast(F32)[0:64, 0:512]
    qfc2 = xbv[0:32, 1024:1536]
    Rt = xbv[:, 1536:2048]
    onesrow_b = xbv[0:1, 512:1024]
    sqb_t = xbv.bitcast(F32)[:, 1536:1537]
    rb_sb = xbv.bitcast(F32)[0:64, 1024:1536]
    Rf = tb.ap()[:, 0:512]
    qrep_sb = ta.ap()
    t1_sb = tb.ap()
    Mf = g1h.bitcast(F32)[:, 4436:7508]
    Mb = xsqv[:, 0:3072]
    wo_sb = hh.ap()[:, 0:8192]

    Ct = lambda nm: P.ctr("b0_" + nm)
    c_ldq = Ct("ldq"); c_ldk = Ct("ldk"); c_ldkt = Ct("ldkt"); c_ldvt = Ct("ldvt"); c_ldx2 = Ct("ldx2")
    c_rep = Ct("rep"); c_qf = Ct("qf"); c_S = Ct("S"); c_sq2 = Ct("sq2"); c_A = Ct("A")
    c_yd = Ct("yd"); c_dM = Ct("dM"); c_Mu = Ct("Mu"); c_kf = Ct("kf")
    c_ysq = Ct("ysq"); c_rs = Ct("rs"); c_rr = Ct("rr"); c_rbc = Ct("rbc"); c_yn = Ct("yn")
    c_Wo = Ct("Wo"); c_wo16 = Ct("wo16"); c_init = Ct("init")
    c_sto = [Ct("sto0"), Ct("sto1")]
    ev = P.ps.ev
    allc = [c_ldq, c_ldk, c_ldkt, c_ldvt, c_ldx2, c_rep, c_qf, c_S, c_sq2, c_A,
            c_yd, c_dM, c_Mu, c_kf, c_ysq, c_rs, c_rr, c_rbc, c_yn, c_Wo, c_wo16, c_init] + c_sto
    base = {c: c.n for c in allc}
    B = lambda c: base[c]
    NTILE = B_CORE * NT
    evb = [e.n for e in ev]
    def evw(eng, kk, val):
        if val > 0:
            eng.wait_ge(ev[kk].sem, evb[kk] + val)
    SQB = float(np.sqrt(2.0) / 2.0)

    with nc.Block() as blk:
        @blk.gpsimd
        def _(g_):
            g_.dma_start(out=wo_sb.rearrange("p (k i) -> p k i", k=8),
                         in_=wo.rearrange("(k p) i -> p k i", p=128)).then_inc(c_wo16.sem, 16)
            c_wo16.n += 16
            g_.iota(iw[:, 0:128].rearrange("p (a b) -> p a b", a=8), [[1, 8], [0, 16]], base=0, channel_multiplier=-1)
            g_.iota(iw[:, 128:256].rearrange("p (a b) -> p a b", a=8), [[0, 8], [1, 16]], base=0, channel_multiplier=-1)
            g_.iota(iw[:, 256:384].rearrange("p (a b) -> p a b", a=8), [[1, 8], [0, 16]], base=8, channel_multiplier=-1)
            g_.drain()
            g_.tensor_copy(itf[:], iw[:])
            g_.drain()
            g_.nop().then_inc(c_init.sem, 1)
            dc2 = DConv(P, "dcb", P.deferred_p2, arena="ffn_wab") if P.deferred_p2 else None
            if dc2:
                dc2.step(g_, 0)
            for g in range(NTILE):
                for h in range(16):
                    evw(g_, 7, 16 * g + h + 1)
                    inst = g_.tensor_copy(Mb[:, h * 192:(h + 1) * 192], Mf[:, h * 192:(h + 1) * 192])
                    c_Mu.inc(inst)
                    g_.drain()
                    if dc2:
                        dc2.step(g_, 1)
            if dc2:
                dc2.finish(g_)
        c_init.n += 1

        @blk.vector
        def _(v):
            c_init.wait(v, B(c_init) + 1)
            v.tensor_scalar(Rf[:, 0:384], itf[:, 0:384], 0.0, None, op0=ALU.is_equal)
            v.drain()
            v.tensor_copy(Rt[:, 0:384], Rf[:, 0:384])
            v.memset(onesrow_b[:], 1.0)
            v.memset(sqb_t[:], SQB)
            v.drain()
            v.nop().then_inc(c_init.sem, 1)
        c_init.n += 1

        @blk.sync
        def _(sy):
            sy.wait_ge(c_init.sem, B(c_init) + 2)
            sy.dma_start(out=Rt[64:80, 0:384], in_=Rt[0:16, 0:384]).then_inc(c_init.sem, 16)
            c_init.n += 16
            sy.dma_start(out=qfc2[16:17, :], in_=onesrow_b[:]).then_inc(c_init.sem, 16)
            c_init.n += 16
            sy.wait_ge(c_init.sem, B(c_init) + 34)
            prev = None
            for g in range(NTILE):
                b, bk = divmod(g, NT); t0 = bk * 512
                if g > 0:
                    evw(sy, 0, 36 * g); evw(sy, 1, 36 * g)
                    pb, pt0 = prev
                    cs = c_sto[(g - 1) % 2]
                    sy.dma_start(out=dst[pb, :, :, pt0:pt0 + 512].transpose([1, 0, 2]),
                                 in_=rout.ap().rearrange("p (c k) -> p c k", c=DC)).then_inc(cs.sem, 16)
                    cs.n += 16
                prev = (b, t0)
                sy.dma_start(out=xr.ap().rearrange("p (c k) -> p c k", c=DC),
                             in_=src[b, :, :, t0:t0 + 512].transpose([1, 0, 2])).then_inc(c_ldx2.sem, 16)
                c_ldx2.n += 16
                c_S.wait(sy, B(c_S) + 64 * g)
                sy.dma_start(out=qblk.rearrange("p (c k) -> p c k", c=DC),
                             in_=qbuf[b, :, :, t0:t0 + 512].transpose([1, 0, 2])).then_inc(c_ldq.sem, 16)
                c_ldq.n += 16
                sy.dma_start(out=kblk.rearrange("p (c k) -> p c k", c=DC),
                             in_=kbuf[b, :, :, t0:t0 + 512].transpose([1, 0, 2])).then_inc(c_ldk.sem, 16)
                c_ldk.n += 16
                c_kf.wait(sy, B(c_kf) + 16 * g)
                sy.dma_start(out=ktblk.rearrange("p (c k) -> p c k", c=4),
                             in_=ktbuf[b, bk * 4:(bk + 1) * 4].rearrange("c p k -> p c k")).then_inc(c_ldkt.sem, 16)
                c_ldkt.n += 16
                c_dM.wait(sy, B(c_dM) + 16 * g)
                sy.dma_start(out=vtblk.rearrange("p (c k) -> p c k", c=4),
                             in_=vtbuf[b, bk * 4:(bk + 1) * 4].rearrange("c p k -> p c k")).then_inc(c_ldvt.sem, 16)
                c_ldvt.n += 16
            evw(sy, 0, 36 * NTILE); evw(sy, 1, 36 * NTILE)
            pb, pt0 = prev
            cs = c_sto[(NTILE - 1) % 2]
            sy.dma_start(out=dst[pb, :, :, pt0:pt0 + 512].transpose([1, 0, 2]),
                         in_=rout.ap().rearrange("p (c k) -> p c k", c=DC)).then_inc(cs.sem, 16)
            cs.n += 16
            sy.wait_ge(c_sto[0].sem, c_sto[0].n)
            sy.wait_ge(c_sto[1].sem, c_sto[1].n)

        @blk.tensor
        def _(t):
            c_wo16.wait(t)
            for g in range(NTILE):
                b, bk = divmod(g, NT)
                c_ldq.wait(t, B(c_ldq) + 16 * (g + 1))
                c_ldk.wait(t, B(c_ldk) + 16 * (g + 1))
                c_ldvt.wait(t, B(c_ldvt) + 16 * (g + 1))
                if bk > 0:
                    c_Mu.wait(t, B(c_Mu) + 16 * g)

                def head_front(h):
                    hb = 64 * (h % 2); hc = h // 2
                    qh = qblk[hb:hb + 16, hc * 512:(hc + 1) * 512]
                    kh = kblk[hb:hb + 16, hc * 512:(hc + 1) * 512]
                    for c in range(2):
                        evw(t, 0, 36 * g + 2 * h + c)
                        t.matmul(P.ps.tile(0), Rt[hb:hb + 16, 256 * c:256 * c + 128], qh, start=True, stop=True)
                        evw(t, 1, 36 * g + 2 * h + c)
                        inst = t.matmul(P.ps.tile(1), Rt[hb:hb + 16, 128:256], qh, start=True, stop=True)
                        c_rep.inc(inst)
                    yb = 4 + h % 2
                    evw(t, yb, 8 * g + h // 2)
                    evw(t, 1, 36 * g + 2 * h + 2)
                    c_qf.wait(t, B(c_qf) + 16 * g + h + 1)
                    if bk > 0:
                        t.matmul(P.ps.tile(yb, 64, 512), Mb[:, h * 192:h * 192 + 64], qf_c0, start=True, stop=False)
                        t.matmul(P.ps.tile(yb, 64, 512), Mb[:, h * 192 + 64:h * 192 + 128], qf_c1, start=False, stop=False)
                        t.matmul(P.ps.tile(yb, 64, 512), Mb[0:17, h * 192 + 128:h * 192 + 192], qfc2[0:17, :], start=False, stop=False)
                    for sub in range(4):
                        j = 4 * h + sub
                        evw(t, 2 + j % 2, 32 * g + j // 2)
                        inst = t.matmul(P.ps.tile(2 + j % 2), kh[:, sub * 128:(sub + 1) * 128], qh,
                                        start=True, stop=True)
                        c_S.inc(inst)

                def head_back(h):
                    yb = 4 + h % 2
                    for sub in range(4):
                        c_A.wait(t, B(c_A) + 4 * (16 * g + h) + sub + 1)
                        inst = t.matmul(P.ps.tile(yb, 64, 512), vtblk[:, sub * 1024 + 64 * h:sub * 1024 + 64 * h + 64],
                                        Abuf[:, sub * 512:(sub + 1) * 512],
                                        start=(bk == 0 and sub == 0), stop=(sub == 3))
                    c_yd.inc(inst)
                    evw(t, 7, 16 * g + h)
                    c_kf.wait(t, B(c_kf) + 16 * g + h + 1)
                    kfTh = kfT_s[h % 2]
                    for c in range(3):
                        fc = 128 if c < 2 else 17
                        for sub in range(4):
                            inst = t.matmul(P.ps.banks[7][0:fc, 64 * c:64 * c + 64],
                                            kfTh[:, sub * 273 + 128 * c: sub * 273 + 128 * c + fc],
                                            vtblk[:, sub * 1024 + 64 * h:sub * 1024 + 64 * h + 64],
                                            start=(sub == 0), stop=(sub == 3))
                    c_dM.inc(inst)

                def head_back2(h):
                    evw(t, 6, 32 * g + 2 * h)
                    c_ysq.wait(t, B(c_ysq) + 16 * g + h + 1)
                    inst = t.matmul(P.ps.tile(6, 1, 512), ones_b[0:64, :], ysq_sb[0:64, :], start=True, stop=True)
                    c_rs.inc(inst)
                    evw(t, 6, 32 * g + 2 * h + 1)
                    c_rr.wait(t, B(c_rr) + 16 * g + h + 1)
                    inst = t.matmul(P.ps.tile(6, 64, 512), onesr_f[:, 0:64], mrow[:], start=True, stop=True)
                    c_rbc.inc(inst)

                for h in range(18):
                    if h < 16:
                        head_front(h)
                    if 1 <= h <= 16:
                        head_back(h - 1)
                    if h >= 2:
                        head_back2(h - 2)
                c_yn.wait(t, B(c_yn) + 16 * (g + 1))
                for i in range(8):
                    evw(t, i % 2, 36 * g + 32 + i // 2)
                    for k in range(8):
                        inst = t.matmul(P.ps.tile(i % 2), wo_sb[:, k * 1024 + i * 128:k * 1024 + (i + 1) * 128],
                                        yall[:, k * 512:(k + 1) * 512], start=(k == 0), stop=(k == 7))
                    c_Wo.inc(inst)

        @blk.scalar
        def _(s):
            for g in range(NTILE):
                b, bk = divmod(g, NT)

                def a_front(h):
                    for c in range(2):
                        c_rep.wait(s, B(c_rep) + 2 * (16 * g + h) + c + 1)
                        evw(s, 1, 36 * g + 2 * h + c - 1)
                        inst = s.copy(qrep_sb[:, c * 512:(c + 1) * 512], P.ps.tile(0))
                        P.ps.release(0, inst)
                    for sub in range(4):
                        j = 4 * h + sub
                        cb = 4 * (16 * g + h) + sub
                        c_S.wait(s, B(c_S) + cb + 1)
                        if cb >= 2:
                            c_A.wait(s, B(c_A) + cb - 1)
                        inst = s.activation(t1_sb[:, (j % 2) * 512:(j % 2 + 1) * 512], P.ps.tile(2 + j % 2),
                                            AF.Square, scale=S32, bias=sqb_t[:, 0:1])
                        P.ps.release(2 + j % 2, inst)

                def a_mid(h):
                    yb = 4 + h % 2
                    c_yd.wait(s, B(c_yd) + 16 * g + h + 1)
                    c_rs.wait(s, B(c_rs) + 16 * g + h)
                    inst = s.activation(ysq_sb[0:64, :], P.ps.tile(yb, 64, 512), AF.Square)
                    c_ysq.inc(inst)
                    c_yn.wait(s, B(c_yn) + 16 * g + h)
                    inst = s.copy(y_sb[:], P.ps.tile(yb, 64, 512))
                    P.ps.release(yb, inst)

                def a_back(h):
                    c_rs.wait(s, B(c_rs) + 16 * g + h + 1)
                    if 16 * g + h > 0:
                        c_rbc.wait(s, B(c_rbc) + 16 * g + h)
                    inst = s.activation(mrow[:], P.ps.tile(6, 1, 512), AF.Ln, bias=P.t('eps_t')[0:1, :], scale=1.0 / 64.0)
                    P.ps.release(6, inst)
                    s.drain()
                    inst = s.activation(mrow[:], mrow[:], AF.Exp, scale=-0.5)
                    c_rr.inc(inst)
                    c_rbc.wait(s, B(c_rbc) + 16 * g + h + 1)
                    c_yn.wait(s, B(c_yn) + 16 * g + h)
                    inst = s.copy(rb_sb[:], P.ps.tile(6, 64, 512))
                    P.ps.release(6, inst)

                for h in range(18):
                    if h < 16:
                        a_front(h)
                    if h >= 2:
                        a_back(h - 2)
                    if 1 <= h <= 16:
                        a_mid(h - 1)

        @blk.vector
        def _(v):
            for g in range(NTILE):
                b, bk = divmod(g, NT)
                c_ldkt.wait(v, B(c_ldkt) + 16 * (g + 1))
                if bk == 0:
                    if g > 0:
                        c_Mu.wait(v, B(c_Mu) + 16 * g)
                    v.memset(Mf[:], 0.0)
                    v.drain()
                if g == 0:
                    for sl_ in kfT_s:
                        for sub in range(4):
                            v.memset(sl_[:, sub * 273 + 272:sub * 273 + 273], 1.0)
                    v.drain()

                def d_front(h):
                    hb = 64 * (h % 2); hc = h // 2
                    kfT = kfT_s[h % 2]
                    c_dM.wait(v, B(c_dM) + 16 * g + h - 1)
                    for sub in range(4):
                        kts = ktblk[:, sub * 1024 + 64 * h:sub * 1024 + 64 * h + 16]
                        in_i = kts.unsqueeze(2).broadcast_to([128, 16, 16])
                        in_j = kts.unsqueeze(1).broadcast_to([128, 16, 16])
                        v.scalar_tensor_tensor(kfT[:, sub * 273:sub * 273 + 256].rearrange("p (i j) -> p i j", i=16),
                                               in_i, CQ2, in_j, op0=ALU.mult, op1=ALU.mult)
                        inst = v.tensor_scalar(kfT[:, sub * 273 + 256:sub * 273 + 272], kts, 1.0 / RRD, None, op0=ALU.mult)
                    c_kf.inc(inst)
                    for c in range(2):
                        evw(v, 0, 36 * g + 2 * h + c + 1)
                        c_rep.wait(v, B(c_rep) + 2 * (16 * g + h) + c + 1)
                        inst = v.scalar_tensor_tensor(qf_c0 if c == 0 else qf_c1, P.ps.tile(1),
                                                      CQ2, qrep_sb[:, c * 512:(c + 1) * 512],
                                                      op0=ALU.mult, op1=ALU.mult)
                        P.ps.release(1, inst)
                    inst = v.tensor_scalar(qfc2[0:16, :], qh_ap(qblk, h), 1.0 / RRD, None, op0=ALU.mult)
                    c_qf.inc(inst)
                    for sub in range(4):
                        j = 4 * h + sub
                        evw(v, 2 + j % 2, 32 * g + j // 2 + 1)
                        if 16 * g + h > 0 and sub == 0:
                            c_yd.wait(v, B(c_yd) + 16 * g + h)
                        inst = v.scalar_tensor_tensor(Abuf[:, sub * 512:(sub + 1) * 512],
                                                      t1_sb[:, (j % 2) * 512:(j % 2 + 1) * 512], 0.5,
                                                      masks[:, sub * 512:(sub + 1) * 512],
                                                      op0=ALU.add, op1=ALU.mult)
                        c_A.inc(inst)

                def d_mid(h):
                    c_dM.wait(v, B(c_dM) + 16 * g + h + 1)
                    if g > 0:
                        c_Mu.wait(v, B(c_Mu) + 16 * (g - 1) + h + 1)
                    msl = Mf[:, h * 192:(h + 1) * 192]
                    inst = v.tensor_tensor(msl, P.ps.banks[7][:, 0:192], msl, op=ALU.add)
                    P.ps.release(7, inst)

                def d_back(h):
                    hb = 64 * (h % 2); hc = h // 2
                    evw(v, 4 + h % 2, 8 * g + h // 2 + 1)
                    evw(v, 6, 32 * g + 2 * h + 2)
                    inst = v.scalar_tensor_tensor(yall[hb:hb + 64, hc * 512:(hc + 1) * 512],
                                                  y_sb[:], gcol[:, 0:1], rb_sb[:],
                                                  op0=ALU.mult, op1=ALU.mult)
                    c_yn.inc(inst)

                for h in range(18):
                    if h < 16:
                        d_front(h)
                    if 1 <= h <= 16:
                        d_mid(h - 1)
                    if h >= 2:
                        d_back(h - 2)
                if g > 0:
                    p_ = (g - 1) % 2
                    c_sto[p_].wait(v, B(c_sto[p_]) + 16 * ((g - 1) // 2 + 1))
                c_ldx2.wait(v, B(c_ldx2) + 16 * (g + 1))
                for i in range(8):
                    c_Wo.wait(v, B(c_Wo) + 8 * g + i + 1)
                    inst = v.tensor_tensor(rout.ap()[:, i * 512:(i + 1) * 512], P.ps.tile(i % 2),
                                           xr[:, i * 512:(i + 1) * 512], op=ALU.add)
                    P.ps.release(i % 2, inst)



def qh_ap(qblk, h):
    hb = 64 * (h % 2); hc = h // 2
    return qblk[hb:hb + 16, hc * 512:(hc + 1) * 512]


def emit_ssm(P, src, dst):
    nc = P.nc
    w_in = P.dscratch("sinT_bf", (1024, 1024), BF16)
    w_xp = P.dscratch("sxT_bf", (512, 80), BF16)
    w_dt = P.dscratch("sdtT_bf", (64, 512), BF16)
    w_out = P.dscratch("soutT_bf", (1024, 1024), BF16)
    hbuf = P.dscratch("s_hbuf", (B_CORE, 8, 128, S), F32)
    xcf = P.dscratch("s_xcf", (B_CORE, 4, 128, S), F32)
    xcb = P.dscratch("s_xcb", (B_CORE, 4, 128, S), BF16)
    zbuf = P.dscratch("s_zbuf", (B_CORE, 4, 128, S), BF16)
    dbuf = P.dscratch("s_dbuf", (B_CORE, 4, 128, S), F32)
    bcbuf = P.dscratch("s_bcbuf", (B_CORE, 16, S), F32)
    ybuf = P.dscratch("s_ybuf", (B_CORE, 4, 128, S), BF16)

    xr = P.t("ffn_xr"); xb_ = P.t("ffn_xb"); xsq = P.t("ffn_xsq")
    hin = P.t("ffn_hin"); g1b = P.t("ffn_g1")
    ta = P.t("ffn_ta"); tb = P.t("ffn_tb")
    rout = P.t("ffn_rout")
    mrow = P.t("ffn_mrow"); vrow = P.t("ffn_vrow"); trow = P.t("ffn_trow")
    wab = P.t("ffn_wab"); hh = P.t("ffn_hh")
    ones_b = P.t("ones_b"); onesr_f = P.t("onesr_f")
    skx = P.t("sm_skx"); skz = P.t("sm_skz")
    sbdt = P.t("sm_sbdt"); salog = P.t("sm_salog"); sD = P.t("sm_sD")
    acol = P.sb("s_acol", [128, 32], F32)
    nacol = P.sb("s_nacol", [128, 32], F32)
    brow = P.sb("s_brow", [1, 2048], F32)
    scols = P.sb("s_scols", [128, 32], F32)

    win_sb = hh.ap()[:, 0:8192]
    wout_sb = hh.ap()[:, 0:8192]
    wxp_sb = wab.ap()[:, 0:320]
    wdt_sb = wab.ap()[:, 512:1024]

    Ct = lambda nm: P.ctr("ssm_" + nm)
    c_init = Ct("init"); c_w1 = Ct("w1"); c_w5 = Ct("w5")
    c_sq = Ct("sq"); c_stat = Ct("stat"); c_var = Ct("var"); c_rstd = Ct("rstd")
    c_mr = Ct("mr"); c_bc = Ct("bc")
    c_ldx = Ct("ldx"); c_mm1 = Ct("mm1"); c_sth = [Ct("sth0"), Ct("sth1")]
    c_ldh = Ct("ldh"); c_cv = Ct("cv"); c_cs = Ct("cs"); c_stc = [Ct("stc0"), Ct("stc1")]
    c_ldxc = Ct("ldxc"); c_xp = Ct("xp"); c_dte = Ct("dte"); c_dtm = Ct("dtm")
    c_u = Ct("u"); c_del = Ct("del"); c_dchain = Ct("dchain")
    c_std = [Ct("std0"), Ct("std1")]; c_stbc = Ct("stbc")
    c_ldd = [Ct("ldd0"), Ct("ldd1")]; c_ldxf = Ct("ldxf"); c_ldbc = Ct("ldbc"); c_bcm4 = Ct("bcm4")
    c_dec = Ct("dec"); c_clip = Ct("clip"); c_u1 = Ct("u1"); c_drv = Ct("drv")
    c_scan = Ct("scan"); c_acc = Ct("acc"); c_y = Ct("y"); c_sty = [Ct("sty0"), Ct("sty1")]
    c_ldy = Ct("ldy"); c_ldx5 = Ct("ldx5"); c_mm5 = Ct("mm5"); c_sto = [Ct("sto0"), Ct("sto1")]
    ev = P.ps.ev
    allc = [c_init, c_w1, c_w5, c_sq, c_stat, c_var, c_rstd, c_mr, c_bc, c_ldx, c_mm1,
            c_ldh, c_cv, c_cs, c_ldxc, c_xp, c_dte, c_dtm, c_u, c_del, c_dchain, c_stbc,
            c_ldxf, c_ldbc, c_bcm4, c_dec, c_clip, c_u1, c_drv, c_scan, c_acc,
            c_y, c_ldy, c_ldx5, c_mm5] + c_sth + c_stc + c_std + c_sty + c_sto + c_ldd
    base = {c: c.n for c in allc}
    B = lambda c: base[c]
    NTILE = B_CORE * NT

    with nc.Block() as blk:
        @blk.sync
        def _(sy):
            sy.dma_start(out=win_sb.rearrange("p (k i) -> p k i", k=8),
                         in_=w_in.rearrange("(k p) i -> p k i", p=128)).then_inc(c_w1.sem, 16)
            c_w1.n += 16
            sy.dma_start(out=wxp_sb.rearrange("p (k i) -> p k i", k=4),
                         in_=w_xp.rearrange("(k p) i -> p k i", p=128)).then_inc(c_w1.sem, 16)
            c_w1.n += 16
            sy.dma_start(out=wdt_sb[0:64, :].rearrange("p (c i) -> p c i", c=4),
                         in_=w_dt.rearrange("k (c i) -> k c i", c=4)).then_inc(c_w1.sem, 16)
            c_w1.n += 16

        @blk.scalar
        def _(s):
            s.activation(acol[:], salog[:], AF.Abs)
            s.drain()
            s.activation(acol[:], acol[:], AF.Exp, scale=-1.0)
            s.drain()
            inst = s.activation(acol[:], acol[:], AF.Ln, bias=1.0)
            c_init.inc(inst)

        @blk.vector
        def _(v):
            c_init.wait(v)
            v.tensor_scalar(nacol[:], salog[:], 0.0, None, op0=ALU.max)
            v.drain()
            v.tensor_tensor(acol[:], acol[:], nacol[:], op=ALU.add)
            v.drain()
            v.tensor_scalar(acol[:], acol[:], 1e-4, None, op0=ALU.add)
            v.drain()
            v.tensor_scalar(nacol[:], acol[:], -1.0, None, op0=ALU.mult)
            v.drain()

    evb = [e.n for e in ev]
    def evw(eng, kk, val):
        if val > 0:
            eng.wait_ge(ev[kk].sem, evb[kk] + val)
    with nc.Block() as blk:
        @blk.sync
        def _(sy):
            def store(pidx):
                pb, pti = divmod(pidx, NT)
                for kk in range(4):
                    evw(sy, kk, 2 * (pidx + 1))
                cs = c_sth[pidx % 2]
                sy.dma_start(out=hbuf[pb, :, :, pti * 512:(pti + 1) * 512].transpose([1, 0, 2]),
                             in_=rout.ap().rearrange("p (c k) -> p c k", c=DC)).then_inc(cs.sem, 16)
                cs.n += 16
            for idx in range(NTILE):
                b, ti = divmod(idx, NT); t0 = ti * 512
                if idx > 0:
                    evw(sy, 6, 2 * (idx - 1) + 2)
                sy.dma_start(out=xr.ap().rearrange("p (c k) -> p c k", c=DC),
                             in_=src[b, :, :, t0:t0 + 512].transpose([1, 0, 2])).then_inc(c_ldx.sem, 16)
                c_ldx.n += 16
                if idx > 0:
                    store(idx - 1)
            store(NTILE - 1)
            sy.wait_ge(c_sth[0].sem, c_sth[0].n)
            sy.wait_ge(c_sth[1].sem, c_sth[1].n)

        @blk.tensor
        def _(t):
            c_w1.wait(t)
            for idx in range(NTILE):
                c_sq.wait(t, B(c_sq) + idx + 1)
                evw(t, 6, 2 * idx); evw(t, 7, 2 * idx)
                for dc in range(DC):
                    t.matmul(P.ps.tile(6, 1, 512), ones_b[:], xb_[:, dc * 512:(dc + 1) * 512],
                             start=(dc == 0), stop=(dc == DC - 1))
                for dc in range(DC):
                    inst = t.matmul(P.ps.tile(7, 1, 512), ones_b[:], xsq[:, dc * 512:(dc + 1) * 512],
                                    start=(dc == 0), stop=(dc == DC - 1))
                c_stat.inc(inst)
                c_mr.wait(t, B(c_mr) + idx + 1)
                evw(t, 6, 2 * idx + 1); evw(t, 7, 2 * idx + 1)
                t.matmul(P.ps.tile(6), onesr_f[:], trow[:], start=True, stop=True)
                inst = t.matmul(P.ps.tile(7), onesr_f[:], mrow[:], start=True, stop=True)
                c_bc.inc(inst)
                evw(t, 6, 2 * idx + 2); evw(t, 7, 2 * idx + 2)
                for i in range(8):
                    evw(t, i % 4, 2 * idx + i // 4)
                    for k in range(8):
                        inst = t.matmul(P.ps.tile(i % 4), win_sb[:, k * 1024 + i * 128:k * 1024 + (i + 1) * 128],
                                        xb_[:, k * 512:(k + 1) * 512], start=(k == 0), stop=(k == 7))
                    c_mm1.inc(inst)

        @blk.scalar
        def _(s):
            for idx in range(NTILE):
                c_ldx.wait(s, B(c_ldx) + 16 * (idx + 1))
                if idx > 0:
                    c_mm1.wait(s, B(c_mm1) + 8 * idx)
                s.copy(xb_[:], xr[:])
                s.drain()
                inst = s.activation(xsq[:], xb_[:], AF.Square)
                c_sq.inc(inst)
                c_var.wait(s, B(c_var) + idx + 1)
                if idx > 0:
                    c_bc.wait(s, B(c_bc) + idx)
                s.activation(trow[:], vrow[:], AF.Ln, bias=P.t('eps_t')[0:1, :])
                s.drain()
                inst = s.activation(trow[:], trow[:], AF.Exp, scale=-0.5)
                c_rstd.inc(inst)
                if idx > 0:
                    cs = c_sth[(idx - 1) % 2]
                    s.wait_ge(cs.sem, B(cs) + 16 * ((idx - 1) // 2 + 1))
                for i in range(8):
                    c_mm1.wait(s, B(c_mm1) + 8 * idx + i + 1)
                    inst = s.copy(rout.ap()[:, i * 512:(i + 1) * 512], P.ps.tile(i % 4))
                    P.ps.release(i % 4, inst)
                    s.drain()

        @blk.vector
        def _(v):
            for idx in range(NTILE):
                c_stat.wait(v, B(c_stat) + idx + 1)
                if idx > 0:
                    c_bc.wait(v, B(c_bc) + idx)
                inst = v.tensor_scalar(mrow[:], P.ps.tile(6, 1, 512), 1.0 / D, None, op0=ALU.mult)
                P.ps.release(6, inst)
                inst = v.tensor_scalar(vrow[:], P.ps.tile(7, 1, 512), 1.0 / D, None, op0=ALU.mult)
                P.ps.release(7, inst)
                v.drain()
                v.tensor_tensor(trow[:], mrow[:], mrow[:], op=ALU.mult)
                v.drain()
                inst = v.tensor_tensor(vrow[:], vrow[:], trow[:], op=ALU.subtract)
                c_var.inc(inst)
                v.drain()
                c_rstd.wait(v, B(c_rstd) + idx + 1)
                inst = v.tensor_tensor(mrow[:], mrow[:], trow[:], op=ALU.mult)
                c_mr.inc(inst)
                v.drain()
                c_bc.wait(v, B(c_bc) + idx + 1)
                for dc in range(DC):
                    sl = slice(dc * 512, (dc + 1) * 512)
                    inst1 = v.tensor_tensor(tb[:, 0:512], xr[:, sl], P.ps.tile(6), op=ALU.mult)
                    v.drain()
                    inst2 = v.tensor_tensor(xb_[:, sl], tb[:, 0:512], P.ps.tile(7), op=ALU.subtract)
                    if dc == DC - 1:
                        P.ps.release(6, inst1); P.ps.release(7, inst2)
                    v.drain()

    hv = hin.ap().bitcast(F32)
    hp = hv[:, 0:S + 2]
    q_ = rout.ap()[:, 0:S]
    xc_t = g1b.ap().bitcast(F32)[:, 0:S]
    xcb_t = g1b.ap()[:, 2 * S:3 * S]
    NCH = B_CORE * 8
    store_sig = []
    with nc.Block() as blk:
        @blk.vector
        def _(v):
            v.memset(hp[:, 0:1], 0.0)
            v.memset(hp[:, S + 1:S + 2], 0.0)
            v.drain()
            v.nop().then_inc(c_init.sem, 1)
        c_init.n += 1

        @blk.sync
        def _(sy):
            sy.wait_ge(c_init.sem, c_init.n)
            for j in range(NCH):
                b, ch = divmod(j, 8)
                if j > 0:
                    c_cv.wait(sy, B(c_cv) + j)
                sy.dma_start(out=hp[:, 1:S + 1], in_=hbuf[b, ch]).then_inc(c_ldh.sem, 16)
                c_ldh.n += 16
                if j > 0:
                    pj = j - 1
                    pb, pch = divmod(pj, 8)
                    c_cs.wait(sy, B(c_cs) + 2 * (pj + 1))
                    cs = c_stc[pj % 2]
                    if pch < 4:
                        sy.dma_start(out=xcf[pb, pch], in_=xc_t[:]).then_inc(cs.sem, 16)
                        cs.n += 16
                        sy.dma_start(out=xcb[pb, pch], in_=xcb_t[:]).then_inc(cs.sem, 16)
                        cs.n += 16
                    else:
                        sy.dma_start(out=zbuf[pb, pch - 4], in_=xcb_t[:]).then_inc(cs.sem, 16)
                        cs.n += 16
                    store_sig.append((cs, cs.n))
            pj = NCH - 1
            pb, pch = divmod(pj, 8)
            c_cs.wait(sy, B(c_cs) + 2 * (pj + 1))
            cs = c_stc[pj % 2]
            sy.dma_start(out=zbuf[pb, pch - 4], in_=xcb_t[:]).then_inc(cs.sem, 16)
            cs.n += 16
            store_sig.append((cs, cs.n))
            sy.wait_ge(c_stc[0].sem, c_stc[0].n)
            sy.wait_ge(c_stc[1].sem, c_stc[1].n)

        @blk.vector
        def _(v):
            for j in range(NCH):
                b, ch = divmod(j, 8)
                kc = skx if ch < 4 else skz
                cc = ch % 4
                v.wait_ge(c_ldh.sem, B(c_ldh) + 16 * (j + 1))
                if j > 0:
                    c_cs.wait(v, B(c_cs) + 2 * j)
                v.tensor_scalar(q_[:], hp[:, 0:S], kc[:, cc * 3:cc * 3 + 1], None, op0=ALU.mult)
                v.drain()
                v.scalar_tensor_tensor(q_[:], hp[:, 1:S + 1], kc[:, cc * 3 + 1:cc * 3 + 2], q_[:],
                                       op0=ALU.mult, op1=ALU.add)
                v.drain()
                inst = v.scalar_tensor_tensor(q_[:], hp[:, 2:S + 2], kc[:, cc * 3 + 2:cc * 3 + 3], q_[:],
                                              op0=ALU.mult, op1=ALU.add)
                c_cv.inc(inst)
                v.drain()

        @blk.scalar
        def _(s):
            for j in range(NCH):
                c_cv.wait(s, B(c_cv) + j + 1)
                if j >= 1:
                    ctr, val = store_sig[j - 1]
                    s.wait_ge(ctr.sem, val)
                inst = s.activation(xc_t[:], q_[:], AF.Silu)
                c_cs.inc(inst)
                s.drain()
                inst = s.copy(xcb_t[:], xc_t[:])
                c_cs.inc(inst)
                s.drain()
    evb = [e.n for e in ev]
    xcb_t3 = hin.ap()[:, 0:2048]
    dtr_sb = hin.ap()[:, 2048:2560]
    dstage = rout.ap()[:, 0:2048]
    bc_st = g1b.ap().bitcast(F32)[0:16, 0:512]
    u_sb = g1b.ap().bitcast(F32)[:, 512:1024]
    ua_sb = g1b.ap().bitcast(F32)[:, 1024:1536]
    with nc.Block() as blk:
        @blk.sync
        def _(sy):
            def store(pidx):
                pb, pti = divmod(pidx, NT); pt0 = pti * 512
                c_dchain.wait(sy, B(c_dchain) + 4 * (pidx + 1))
                cs = c_std[pidx % 2]
                sy.dma_start(out=dbuf[pb, :, :, pt0:pt0 + 512].transpose([1, 0, 2]),
                             in_=dstage.rearrange("p (c k) -> p c k", c=4)).then_inc(cs.sem, 16)
                cs.n += 16
                evw(sy, 4, pidx + 1)
                sy.dma_start(out=bcbuf[pb, :, pt0:pt0 + 512], in_=bc_st[:]).then_inc(c_stbc.sem, 16)
                c_stbc.n += 16
            for idx in range(NTILE):
                b, ti = divmod(idx, NT); t0 = ti * 512
                if idx > 0:
                    c_xp.wait(sy, B(c_xp) + idx)
                sy.dma_start(out=xcb_t3.rearrange("p (c k) -> p c k", c=4),
                             in_=xcb[b, :, :, t0:t0 + 512].transpose([1, 0, 2])).then_inc(c_ldxc.sem, 16)
                c_ldxc.n += 16
                if idx > 0:
                    store(idx - 1)
            store(NTILE - 1)
            sy.wait_ge(c_std[0].sem, c_std[0].n)
            sy.wait_ge(c_std[1].sem, c_std[1].n)
            sy.wait_ge(c_stbc.sem, c_stbc.n)

        @blk.tensor
        def _(t):
            for idx in range(NTILE):
                c_ldxc.wait(t, B(c_ldxc) + 16 * (idx + 1))
                evw(t, 4, idx)
                for k in range(4):
                    inst = t.matmul(P.ps.tile(4, 80, 512), wxp_sb[:, k * 80:(k + 1) * 80],
                                    xcb_t3[:, k * 512:(k + 1) * 512], start=(k == 0), stop=(k == 3))
                c_xp.inc(inst)
                c_dte.wait(t, B(c_dte) + idx + 1)
                for i in range(4):
                    evw(t, i, idx)
                    inst = t.matmul(P.ps.tile(i), wdt_sb[0:64, i * 128:(i + 1) * 128],
                                    dtr_sb[0:64, 0:512], start=True, stop=True)
                    c_dtm.inc(inst)

        @blk.scalar
        def _(s):
            for idx in range(NTILE):
                c_xp.wait(s, B(c_xp) + idx + 1)
                if idx > 0:
                    c_dtm.wait(s, B(c_dtm) + 4 * idx)
                inst = s.copy(dtr_sb[0:64, 0:512], P.ps.tile(4, 64, 512))
                c_dte.inc(inst)
                s.drain()
                if idx > 0:
                    s.wait_ge(c_stbc.sem, B(c_stbc) + 16 * idx)
                inst = s.activation(bc_st[:], P.ps.banks[4][64:80, 0:512], AF.Tanh)
                P.ps.release(4, inst)
                s.drain()
                for i in range(4):
                    cb = 4 * idx + i
                    evw(s, i, idx + 1)
                    if cb > 0:
                        c_dchain.wait(s, B(c_dchain) + cb)
                    s.activation(ua_sb[:], u_sb[:], AF.Abs)
                    s.drain()
                    s.activation(ua_sb[:], ua_sb[:], AF.Exp, scale=-1.0)
                    s.drain()
                    inst = s.activation(ua_sb[:], ua_sb[:], AF.Ln, bias=1.0)
                    c_del.inc(inst)
                    s.drain()

        @blk.vector
        def _(v):
            for idx in range(NTILE):
                for i in range(4):
                    cb = 4 * idx + i
                    c_dtm.wait(v, B(c_dtm) + cb + 1)
                    if cb > 0:
                        c_del.wait(v, B(c_del) + cb)
                    if i == 0 and idx > 0:
                        cs = c_std[(idx - 1) % 2]
                        v.wait_ge(cs.sem, B(cs) + 16 * ((idx - 1) // 2 + 1))
                    inst = v.tensor_scalar(u_sb[:], P.ps.tile(i), sbdt[:, i:i + 1], None, op0=ALU.add)
                    P.ps.release(i, inst)
                    v.drain()
                    c_del.wait(v, B(c_del) + cb + 1)
                    v.tensor_scalar(dstage[:, i * 512:(i + 1) * 512], u_sb[:], 0.0, None, op0=ALU.max)
                    v.drain()
                    v.tensor_tensor(dstage[:, i * 512:(i + 1) * 512], dstage[:, i * 512:(i + 1) * 512],
                                    ua_sb[:], op=ALU.add)
                    v.drain()
                    inst = v.tensor_scalar(dstage[:, i * 512:(i + 1) * 512], dstage[:, i * 512:(i + 1) * 512],
                                           1e-4, 1.0, op0=ALU.max, op1=ALU.min)
                    c_dchain.inc(inst)
                    v.drain()

    evb = [e.n for e in ev]
    TC = 1024
    NTC = S // TC
    hv4 = hin.ap().bitcast(F32)
    xcf_sb = hv4
    g4 = g1b.ap().bitcast(F32)
    btb = g4[:, 0:1024]
    ctb_hold = [g4[:, 1024:2048], g4[:, 4096:5120]]
    dec_sb = [g4[:, 2048:3072], g4[:, 3072:4096]]
    rv4 = rout.ap()
    del_sb = [rv4[:, 0:1024], rv4[:, 1024:2048]]
    u1_sb = [rv4[:, 2048:3072], rv4[:, 3072:4096]]
    xrv = xr.ap()
    drv_sb = [xrv[:, 0:1024], xrv[:, 1024:2048]]
    stt_sb = [xrv[:, 2048:3072], xrv[:, 3072:4096]]
    yac = hv4[:, 4096:8192]
    ybf = [xb_.ap()[:, 0:1024], xb_.ap()[:, 1024:2048]]
    with nc.Block() as blk:
        @blk.sync
        def _(sy):
            for cN in range(B_CORE * NTC):
                b, tc = divmod(cN, NTC); t0 = tc * TC
                if cN > 0:
                    c_y.wait(sy, B(c_y) + 4 * cN)
                for c in range(4):
                    sy.dma_start(out=xcf_sb[:, c * TC:(c + 1) * TC],
                                 in_=xcf[b, c, :, t0:t0 + TC]).then_inc(c_ldxf.sem, 16)
                    c_ldxf.n += 16
                for ds in range(8):
                    j = cN * 8 + ds
                    if j > 0:
                        c_bcm4.wait(sy, B(c_bcm4) + j)
                    sy.dma_start(out=brow[:, 0:TC], in_=bcbuf[b, ds, t0:t0 + TC].unsqueeze(0)
                                 ).then_inc(c_ldbc.sem, 16)
                    c_ldbc.n += 16
                    sy.dma_start(out=brow[:, TC:2 * TC], in_=bcbuf[b, 8 + ds, t0:t0 + TC].unsqueeze(0)
                                 ).then_inc(c_ldbc.sem, 16)
                    c_ldbc.n += 16
                    for c in range(4):
                        cb = j * 4 + c
                        if cb >= 2:
                            c_dec.wait(sy, B(c_dec) + cb - 1)
                        cl = c_ldd[cb % 2]
                        sy.dma_start(out=del_sb[cb % 2][:],
                                     in_=dbuf[b, c, :, t0:t0 + TC]).then_inc(cl.sem, 16)
                        cl.n += 16
                for c in range(4):
                    c_y.wait(sy, B(c_y) + 4 * cN + c + 1)
                    cs = c_sty[c % 2]
                    sy.dma_start(out=ybuf[b, c, :, t0:t0 + TC], in_=ybf[c % 2][:]).then_inc(cs.sem, 16)
                    cs.n += 16
            sy.wait_ge(c_sty[0].sem, c_sty[0].n)
            sy.wait_ge(c_sty[1].sem, c_sty[1].n)

        @blk.tensor
        def _(t):
            for cN in range(B_CORE * NTC):
                for ds in range(8):
                    j = cN * 8 + ds
                    t.wait_ge(c_ldbc.sem, B(c_ldbc) + 32 * (j + 1))
                    for q in range(2):
                        evw(t, 4 + q, j)
                        t.matmul(P.ps.tile(4 + q), onesr_f[:], brow[0:1, q * 512:(q + 1) * 512],
                                 start=True, stop=True)
                    for q in range(2):
                        evw(t, 6 + q, j)
                        inst = t.matmul(P.ps.tile(6 + q), onesr_f[:], brow[0:1, TC + q * 512:TC + (q + 1) * 512],
                                        start=True, stop=True)
                    c_bcm4.inc(inst)

        @blk.scalar
        def _(s):
            for cN in range(B_CORE * NTC):
                for ds in range(8):
                    j = cN * 8 + ds
                    for c in (0, 2):
                        e = j * 4 + c
                        s.wait_ge(c_ldd[0].sem, B(c_ldd[0]) + 16 * (e // 2 + 1))
                        s.wait_ge(c_ldd[1].sem, B(c_ldd[1]) + 16 * (e // 2 + 1))
                        if e >= 2:
                            c_scan.wait(s, B(c_scan) + e - 1)
                        inst = s.activation(dec_sb[0][:], del_sb[0][:], AF.Exp,
                                            scale=nacol[:, c * 8 + ds:c * 8 + ds + 1])
                        c_dec.inc(inst)
                        inst = s.activation(dec_sb[1][:], del_sb[1][:], AF.Exp,
                                            scale=nacol[:, (c + 1) * 8 + ds:(c + 1) * 8 + ds + 1])
                        c_dec.inc(inst)

        @blk.gpsimd
        def _(g_):
            for cN in range(B_CORE * NTC):
                c_y.wait(g_, B(c_y) + 4 * cN)
                for ds in range(8):
                    j = cN * 8 + ds
                    for c in (0, 2):
                        e = j * 4 + c; o = e + 1
                        c_scan.wait(g_, B(c_scan) + o + 1)
                        if ds > 0:
                            c_acc.wait(g_, B(c_acc) + e - 2)
                        inst = g_.tensor_tensor(stt_sb[0][:], stt_sb[0][:], ctb_hold[j % 2][:], op=ALU.mult)
                        c_clip.inc(inst); k_s0 = c_clip.n
                        inst = g_.tensor_tensor(stt_sb[1][:], stt_sb[1][:], ctb_hold[j % 2][:], op=ALU.mult)
                        c_clip.inc(inst); k_s1 = c_clip.n
                        if ds == 0:
                            c_clip.wait(g_, k_s0)
                            inst = g_.tensor_copy(yac[:, c * TC:(c + 1) * TC], stt_sb[0][:])
                            c_acc.inc(inst)
                            c_clip.wait(g_, k_s1)
                            inst = g_.tensor_copy(yac[:, (c + 1) * TC:(c + 2) * TC], stt_sb[1][:])
                            c_acc.inc(inst)
                        else:
                            c_clip.wait(g_, k_s0)
                            inst = g_.tensor_tensor(yac[:, c * TC:(c + 1) * TC], yac[:, c * TC:(c + 1) * TC],
                                                    stt_sb[0][:], op=ALU.add)
                            c_acc.inc(inst)
                            c_clip.wait(g_, k_s1)
                            inst = g_.tensor_tensor(yac[:, (c + 1) * TC:(c + 2) * TC], yac[:, (c + 1) * TC:(c + 2) * TC],
                                                    stt_sb[1][:], op=ALU.add)
                            c_acc.inc(inst)

        @blk.vector
        def _(v):
            for cN in range(B_CORE * NTC):
                b, tc = divmod(cN, NTC)
                c_ldxf.wait(v, B(c_ldxf) + 64 * (cN + 1))
                for ds in range(8):
                    j = cN * 8 + ds
                    c_bcm4.wait(v, B(c_bcm4) + j + 1)
                    inst = v.tensor_copy(btb[:, 0:512], P.ps.tile(4))
                    P.ps.release(4, inst)
                    inst = v.tensor_copy(btb[:, 512:1024], P.ps.tile(5))
                    P.ps.release(5, inst)
                    inst = v.tensor_copy(ctb_hold[j % 2][:, 0:512], P.ps.tile(6))
                    P.ps.release(6, inst)
                    inst = v.tensor_copy(ctb_hold[j % 2][:, 512:1024], P.ps.tile(7))
                    P.ps.release(7, inst)
                    evw(v, 7, j + 1)
                    for c in (0, 2):
                        e = j * 4 + c
                        if e >= 2:
                            c_acc.wait(v, B(c_acc) + e)
                        inst = v.tensor_tensor(u1_sb[0][:], btb[:], xcf_sb[:, c * TC:(c + 1) * TC], op=ALU.mult)
                        c_u1.inc(inst)
                        inst = v.tensor_tensor(u1_sb[1][:], btb[:], xcf_sb[:, (c + 1) * TC:(c + 2) * TC], op=ALU.mult)
                        c_u1.inc(inst)
                        c_dec.wait(v, B(c_dec) + e + 2)
                        inst = v.tensor_scalar(dec_sb[0][:], dec_sb[0][:], 1e-4, 1.0,
                                               op0=ALU.max, op1=ALU.min)
                        c_u1.inc(inst); k_ce = c_u1.n
                        inst = v.tensor_scalar(dec_sb[1][:], dec_sb[1][:], 1e-4, 1.0,
                                               op0=ALU.max, op1=ALU.min)
                        c_u1.inc(inst)
                        c_u1.wait(v, k_ce)
                        inst = v.tensor_tensor(drv_sb[0][:], dec_sb[0][:], u1_sb[0][:], op=ALU.mult)
                        c_u1.inc(inst); k_d1 = c_u1.n
                        c_u1.wait(v, k_ce + 1)
                        inst = v.tensor_tensor(drv_sb[1][:], dec_sb[1][:], u1_sb[1][:], op=ALU.mult)
                        c_u1.inc(inst)
                        c_u1.wait(v, k_d1)
                        inst = v.tensor_tensor(drv_sb[0][:], u1_sb[0][:], drv_sb[0][:], op=ALU.subtract)
                        c_u1.inc(inst); k_d2 = c_u1.n
                        c_u1.wait(v, k_d1 + 1)
                        inst = v.tensor_tensor(drv_sb[1][:], u1_sb[1][:], drv_sb[1][:], op=ALU.subtract)
                        c_u1.inc(inst)
                        init_e = 0.0 if tc == 0 else scols[:, c * 8 + ds:c * 8 + ds + 1]
                        init_o = 0.0 if tc == 0 else scols[:, (c + 1) * 8 + ds:(c + 1) * 8 + ds + 1]
                        c_u1.wait(v, k_d2)
                        inst = v.tensor_tensor_scan(stt_sb[0][:], dec_sb[0][:], drv_sb[0][:],
                                                    init_e, op0=ALU.mult, op1=ALU.add)
                        c_u1.inc(inst); k_se = c_u1.n
                        c_u1.wait(v, k_d2 + 1)
                        inst = v.tensor_tensor_scan(stt_sb[1][:], dec_sb[1][:], drv_sb[1][:],
                                                    init_o, op0=ALU.mult, op1=ALU.add)
                        c_u1.inc(inst); k_so = c_u1.n
                        c_u1.wait(v, k_se)
                        inst = v.tensor_copy(scols[:, c * 8 + ds:c * 8 + ds + 1],
                                             stt_sb[0][:, TC - 1:TC])
                        c_scan.inc(inst)
                        c_u1.wait(v, k_so)
                        inst = v.tensor_copy(scols[:, (c + 1) * 8 + ds:(c + 1) * 8 + ds + 1],
                                             stt_sb[1][:, TC - 1:TC])
                        c_scan.inc(inst)
                c_acc.wait(v, B(c_acc) + 32 * (cN + 1))
                for c in (0, 2):
                    for cc in (c, c + 1):
                        if cN >= 1 or cc >= 2:
                            cs = c_sty[cc % 2]
                            nst = (4 * cN + cc - 2) // 2 + 1
                            v.wait_ge(cs.sem, B(cs) + 16 * nst)
                        inst = v.scalar_tensor_tensor(yac[:, cc * TC:(cc + 1) * TC], xcf_sb[:, cc * TC:(cc + 1) * TC],
                                                      sD[:, cc:cc + 1], yac[:, cc * TC:(cc + 1) * TC],
                                                      op0=ALU.mult, op1=ALU.add)
                        c_drv.inc(inst)
                        if cc == c:
                            k_y0 = c_drv.n
                    c_drv.wait(v, k_y0)
                    inst = v.tensor_copy(ybf[c % 2][:], yac[:, c * TC:(c + 1) * TC])
                    c_y.inc(inst)
                    c_drv.wait(v, k_y0 + 1)
                    inst = v.tensor_copy(ybf[(c + 1) % 2][:], yac[:, (c + 1) * TC:(c + 2) * TC])
                    c_y.inc(inst)

    evb = [e.n for e in ev]
    rhs5 = hin.ap()[:, 0:4096]
    with nc.Block() as blk:
        @blk.gpsimd
        def _(g_):
            g_.dma_start(out=wout_sb.rearrange("p (k i) -> p k i", k=8),
                         in_=w_out.rearrange("(k p) i -> p k i", p=128)).then_inc(c_w5.sem, 16)
            c_w5.n += 16

        @blk.sync
        def _(sy):
            prev = None
            for idx in range(NTILE):
                b, ti = divmod(idx, NT); t0 = ti * 512
                if idx > 0:
                    evw(sy, 0, 4 * idx); evw(sy, 1, 4 * idx)
                    pb, pt0 = prev
                    cs = c_sto[(idx - 1) % 2]
                    sy.dma_start(out=dst[pb, :, :, pt0:pt0 + 512].transpose([1, 0, 2]),
                                 in_=rout.ap().rearrange("p (c k) -> p c k", c=DC)).then_inc(cs.sem, 16)
                    cs.n += 16
                prev = (b, t0)
                sy.dma_start(out=xr.ap().rearrange("p (c k) -> p c k", c=DC),
                             in_=src[b, :, :, t0:t0 + 512].transpose([1, 0, 2])).then_inc(c_ldx5.sem, 16)
                c_ldx5.n += 16
                if idx > 0:
                    c_mm5.wait(sy, B(c_mm5) + 8 * idx)
                sy.dma_start(out=rhs5[:, 0:2048].rearrange("p (c k) -> p c k", c=4),
                             in_=ybuf[b, :, :, t0:t0 + 512].transpose([1, 0, 2])).then_inc(c_ldy.sem, 16)
                c_ldy.n += 16
                sy.dma_start(out=rhs5[:, 2048:4096].rearrange("p (c k) -> p c k", c=4),
                             in_=zbuf[b, :, :, t0:t0 + 512].transpose([1, 0, 2])).then_inc(c_ldy.sem, 16)
                c_ldy.n += 16
            evw(sy, 0, 4 * NTILE); evw(sy, 1, 4 * NTILE)
            pb, pt0 = prev
            cs = c_sto[(NTILE - 1) % 2]
            sy.dma_start(out=dst[pb, :, :, pt0:pt0 + 512].transpose([1, 0, 2]),
                         in_=rout.ap().rearrange("p (c k) -> p c k", c=DC)).then_inc(cs.sem, 16)
            cs.n += 16
            sy.wait_ge(c_sto[0].sem, c_sto[0].n)
            sy.wait_ge(c_sto[1].sem, c_sto[1].n)

        @blk.tensor
        def _(t):
            c_w5.wait(t)
            for idx in range(NTILE):
                c_ldy.wait(t, B(c_ldy) + 32 * (idx + 1))
                for i in range(8):
                    evw(t, i % 2, 4 * idx + i // 2)
                    for k in range(8):
                        inst = t.matmul(P.ps.tile(i % 2), wout_sb[:, k * 1024 + i * 128:k * 1024 + (i + 1) * 128],
                                        rhs5[:, k * 512:(k + 1) * 512], start=(k == 0), stop=(k == 7))
                    c_mm5.inc(inst)

        @blk.vector
        def _(v):
            for idx in range(NTILE):
                if idx > 0:
                    p_ = (idx - 1) % 2
                    c_sto[p_].wait(v, B(c_sto[p_]) + 16 * ((idx - 1) // 2 + 1))
                c_ldx5.wait(v, B(c_ldx5) + 16 * (idx + 1))
                for i in range(8):
                    c_mm5.wait(v, B(c_mm5) + 8 * idx + i + 1)
                    inst = v.tensor_tensor(rout.ap()[:, i * 512:(i + 1) * 512], P.ps.tile(i % 2),
                                           xr[:, i * 512:(i + 1) * 512], op=ALU.add)
                    P.ps.release(i % 2, inst)



def emit_sublayer(P, sub, src, dst):
    if sub in (1, 3):
        emit_ffn(P, 0, f"f{sub}", src, dst)
    elif sub in (5, 7):
        emit_ffn(P, 1, f"f{sub}", src, dst)
    elif sub == 0:
        emit_attn_a0(P, src, dst)
    elif sub == 2:
        emit_attn_b0(P, src, dst)
    elif sub == 4:
        emit_attn_a1(P, src, dst)
    elif sub == 6:
        emit_ssm(P, src, dst)
    else:
        raise NotImplementedError(sub)


def build_group(subs):
    P = Prog(subs)
    x_in = P.din("x_in", (B_CORE, DC, 128, S), F32)
    x_out = P.dout("x_out", (B_CORE, DC, 128, S), F32)
    emit_preamble(P)
    n = len(subs)
    resid = P.dscratch("resid", (B_CORE, DC, 128, S), F32) if n > 1 else None
    for j, sub in enumerate(subs):
        src = x_in if j == 0 else resid
        dst = x_out if j == n - 1 else resid
        emit_sublayer(P, sub, src, dst)
    P.nc.finalize()
    return P


def input_names(P):
    names = []
    for name, ap in P._dram.items():
        t = ap.tensor
        try:
            kind = t.kind
        except Exception:
            kind = None
        names.append(name)
    return names


def kernel(**inputs):
    per_core = host_layout(inputs)
    groups = GROUPS if GROUPS is not None else [list(SUBLAYERS)]
    cur = [m["x_in"] for m in per_core]
    for subs in groups:
        P = build_group(subs)
        ext_in = sorted(P.ext_inputs)
        in_maps = []
        for ci in range(8):
            m = {}
            for nm in ext_in:
                m[nm] = cur[ci] if nm == "x_in" else per_core[ci][nm]
            in_maps.append(m)
        res = run_bass_kernel_spmd(P.nc, in_maps, core_ids=list(range(8)))
        cur = [np.asarray(res.results[ci]["x_out"]) for ci in range(8)]
    out = np.empty((16, S, D), dtype=np.float32)
    for ci in range(8):
        xs = cur[ci].reshape(B_CORE, D, S)
        out[ci * B_CORE:(ci + 1) * B_CORE] = xs.transpose(0, 2, 1)
    return out


# version 22 (final)
# speedup vs baseline: 1.1666x; 1.1666x over previous
import os
os.environ.setdefault("GAUGE_NTFF_IGNORE_DMA", "1")
import numpy as np
import ml_dtypes
import concourse.bass as bass
import concourse.mybir as mybir
from concourse.bass_utils import run_bass_kernel_spmd

F32 = mybir.dt.float32
BF16 = mybir.dt.bfloat16
I32 = mybir.dt.int32
AF = mybir.ActivationFunctionType
ALU = mybir.AluOpType

B_CORE = 2
S = 4096
D = 1024
DC = D // 128
TT0 = 512
NT = S // TT0
EPS = 1e-5

NO_SEM_SHARING = False
DEFER_CONV = True
SUBLAYERS = list(range(8))
GROUPS = None


class Ctr:
    def __init__(self, nc, name):
        self.sem = nc.alloc_semaphore(name)
        self.n = 0

    def inc(self, inst, k=1):
        inst.then_inc(self.sem, k)
        self.n += k
        return self.n

    def wait(self, eng, n=None):
        v = self.n if n is None else n
        if v > 0:
            eng.wait_ge(self.sem, v)


class PsumPool:
    def __init__(self, nc):
        self.banks = [nc.alloc_psum_tensor(f"psb{i}", [128, 512], F32) for i in range(8)]
        self.ev = [Ctr(nc, f"psev{i}") for i in range(8)]

    def tile(self, k, p=128, n=512, p0=0):
        return self.banks[k][p0:p0 + p, 0:n]

    def acquire(self, eng, k):
        self.ev[k].wait(eng)

    def release(self, k, inst):
        self.ev[k].inc(inst)


class Prog:
    def __init__(self, sublayers):
        self.nc = bass.Bass("TRN2", target_bir_lowering=False)
        self.sublayers = sublayers
        self.ps = PsumPool(self.nc)
        self.ctrs = {}
        self._slot_ctrs = []
        self._slot_gp = []
        self._dram = {}
        self._sbuf = {}
        self.ext_inputs = set()
        self.deferred_p1 = []
        self.deferred_p2 = []

    GP_DMA = frozenset(["wH0", "wH1", "wH2", "wo16", "w5",
                        "cvldpool", "cvstpool0", "cvstpool1",
                        "gld0", "gld1", "gst0", "gst1"])

    def ctr(self, name):
        tag, _, suffix = name.partition("_")
        if not suffix:
            tag, suffix = "g", name
        gp = suffix in self.GP_DMA
        pool = self._slot_gp if gp else self._slot_ctrs
        scope = self.ctrs.setdefault((tag, gp), {})
        if suffix not in scope:
            if NO_SEM_SHARING or gp:
                scope[suffix] = Ctr(self.nc, "u_" + name)
            else:
                slot = len(scope)
                while len(pool) <= slot:
                    pool.append(Ctr(self.nc, ("gp" if gp else "cs") + str(len(pool))))
                scope[suffix] = pool[slot]
        return scope[suffix]

    def din(self, name, shape, dtype=F32):
        if name not in self._dram:
            self._dram[name] = self.nc.dram_tensor(name, list(shape), dtype, kind="ExternalInput").ap()
            self.ext_inputs.add(name)
        return self._dram[name]

    def dout(self, name, shape, dtype=F32):
        if name not in self._dram:
            self._dram[name] = self.nc.dram_tensor(name, list(shape), dtype, kind="ExternalOutput").ap()
        return self._dram[name]

    def dscratch(self, name, shape, dtype=F32):
        if name not in self._dram:
            self._dram[name] = self.nc.dram_tensor(name, list(shape), dtype).ap()
        return self._dram[name]

    def sb(self, name, shape, dtype=F32):
        if name not in self._sbuf:
            self._sbuf[name] = self.nc.alloc_sbuf_tensor(name, list(shape), dtype)
        return self._sbuf[name]

    def t(self, name):
        return self._sbuf[name]



def host_layout(inputs):
    w = {}
    t = lambda a: np.ascontiguousarray(np.asarray(a).T)
    c = lambda a: np.ascontiguousarray(np.asarray(a))
    def col(a, nc_):
        a = np.asarray(a)
        if a.ndim == 1:
            a = a.reshape(nc_, 128, 1)
        else:
            a = a.reshape(nc_, 128, -1)
        return np.ascontiguousarray(a.transpose(1, 0, 2).reshape(128, -1))
    w["wqT0"] = t(inputs["L0_attn_a_Wq"]); w["wkT0"] = t(inputs["L0_attn_a_Wk"])
    w["wvT0"] = t(inputs["L0_attn_a_Wv"]); w["woT0"] = t(inputs["L0_attn_a_Wo"])
    for nm in ("bq", "bk", "bv", "bo"):
        w[nm + "0"] = col(inputs["L0_attn_a_" + nm], DC)
    w["w1T1"] = c(inputs["L1_attn_a_W1"])
    w["b1_1"] = col(inputs["L1_attn_a_b1"], 24)
    w["w2T1"] = c(inputs["L1_attn_a_W2"])
    w["b2_1"] = col(inputs["L1_attn_a_b2"], DC)
    for L in (0, 1):
        w[f"fw1T{L}"] = t(inputs[f"L{L}_ffn_w1_w"])
        w[f"fw2T{L}"] = t(inputs[f"L{L}_ffn_w2_w"])
        w[f"fh1T{L}"] = t(inputs[f"L{L}_ffn_h1_w"])
        w[f"fh2T{L}"] = t(inputs[f"L{L}_ffn_h2_w"])
        w[f"fw3T{L}"] = t(inputs[f"L{L}_ffn_w3_w"])
        w[f"fs1{L}"] = col(inputs[f"L{L}_ffn_w1_s"], 32)
        w[f"fs2{L}"] = col(inputs[f"L{L}_ffn_w2_s"], 32)
        w[f"fsh1{L}"] = col(inputs[f"L{L}_ffn_h1_s"], 32)
        w[f"fsh2{L}"] = col(inputs[f"L{L}_ffn_h2_s"], 32)
        w[f"fs3{L}"] = col(inputs[f"L{L}_ffn_w3_s"], 8)
    tq = t(inputs["L0_attn_b_Wq"]); tk = t(inputs["L0_attn_b_Wk"])
    def pad64(a):
        z = np.zeros((1024, 1024), np.float32)
        z.reshape(1024, 16, 64)[:, :, :16] = a.reshape(1024, 16, 16)
        return z
    w["wb0"] = np.ascontiguousarray(np.concatenate([pad64(tk), pad64(tq), t(inputs["L0_attn_b_Wv"])], axis=1))
    w["b0z"] = np.zeros((128, 24), np.float32)
    w["toT"] = t(inputs["L0_attn_b_Wo"])
    w["tg"] = np.ascontiguousarray(np.asarray(inputs["L0_attn_b_g_weight"]).reshape(64, 1))
    w["sinT"] = t(inputs["L1_attn_b_W_in"])
    w["sxT"] = t(inputs["L1_attn_b_W_xproj"])
    w["sdtT"] = t(inputs["L1_attn_b_W_dt"])
    w["soutT"] = t(inputs["L1_attn_b_W_out"])
    w["skx"] = col(inputs["L1_attn_b_Kx"].reshape(512, 3), 4)
    w["skz"] = col(inputs["L1_attn_b_Kz"].reshape(512, 3), 4)
    w["sbdt"] = col(inputs["L1_attn_b_b_dt"], 4)
    w["salog"] = col(inputs["L1_attn_b_A_log"], 4)
    w["sD"] = col(inputs["L1_attn_b_D"], 4)

    x = np.asarray(inputs["L0_attn_a_x"])
    per_core = []
    for core in range(8):
        xs = x[core * B_CORE:(core + 1) * B_CORE]
        xs = np.ascontiguousarray(xs.transpose(0, 2, 1)).reshape(B_CORE, DC, 128, S)
        m = dict(w)
        m["x_in"] = xs
        per_core.append(m)
    return per_core


W_BF16 = {
    "wqT0": (1024, 1024), "wkT0": (1024, 1024), "wvT0": (1024, 1024), "woT0": (1024, 1024),
    "fw1T0": (1024, 4096), "fw2T0": (1024, 4096), "fh1T0": (4096, 4096), "fh2T0": (4096, 4096), "fw3T0": (4096, 1024),
    "wb0": (1024, 3072), "toT": (1024, 1024),
    "w1T1": (1024, 3072), "w2T1": (1024, 1024),
    "fw1T1": (1024, 4096), "fw2T1": (1024, 4096), "fh1T1": (4096, 4096), "fh2T1": (4096, 4096), "fw3T1": (4096, 1024),
    "sinT": (1024, 1024), "sxT": (512, 80), "sdtT": (64, 512), "soutT": (1024, 1024),
}
W_SMALL = {
    "bq0": (128, 8), "bk0": (128, 8), "bv0": (128, 8), "bo0": (128, 8),
    "b1_1": (128, 24), "b2_1": (128, 8),
    "fs10": (128, 32), "fs20": (128, 32), "fsh10": (128, 32), "fsh20": (128, 32), "fs30": (128, 8),
    "fs11": (128, 32), "fs21": (128, 32), "fsh11": (128, 32), "fsh21": (128, 32), "fs31": (128, 8),
    "tg": (64, 1), "b0z": (128, 24),
    "skx": (128, 12), "skz": (128, 12), "sbdt": (128, 4), "salog": (128, 32), "sD": (128, 4),
}
SUB_W = {
    0: (["wqT0", "wkT0", "wvT0", "woT0"], ["bq0", "bk0", "bv0", "bo0"]),
    1: (["fw1T0", "fw2T0", "fh1T0", "fh2T0", "fw3T0"], ["fs10", "fs20", "fsh10", "fsh20", "fs30"]),
    2: (["wb0", "toT"], ["tg", "b0z"]),
    3: (["fw1T0", "fw2T0", "fh1T0", "fh2T0", "fw3T0"], ["fs10", "fs20", "fsh10", "fsh20", "fs30"]),
    4: (["w1T1", "w2T1"], ["b1_1", "b2_1"]),
    5: (["fw1T1", "fw2T1", "fh1T1", "fh2T1", "fw3T1"], ["fs11", "fs21", "fsh11", "fsh21", "fs31"]),
    6: (["sinT", "sxT", "sdtT", "soutT"], ["skx", "skz", "sbdt", "salog", "sD"]),
    7: (["fw1T1", "fw2T1", "fh1T1", "fh2T1", "fw3T1"], ["fs11", "fs21", "fsh11", "fsh21", "fs31"]),
}

SWZ = {"fh1T0": 128, "fh2T0": 128, "fw3T0": 128, "fh1T1": 128, "fh2T1": 128,
       "fw3T1": 128, "fw1T0": 256, "fw2T0": 256, "fw1T1": 256, "fw2T1": 256}


def bf_scratch(P, nm):
    R, C = W_BF16[nm]
    if nm in SWZ:
        cb = SWZ[nm]
        return P.dscratch(nm + "_bf", (C // cb, 128, (R // 128) * cb), BF16)
    return P.dscratch(nm + "_bf", (R, C), BF16)


def bf_store(eng, P, nm, r0, pr, c0, pc, sb_in):
    dstw = bf_scratch(P, nm)
    if nm in SWZ:
        cb = SWZ[nm]
        k0 = r0 // 128
        return eng.dma_start(
            out=dstw[c0 // cb:(c0 + pc) // cb, :, k0 * cb:(k0 + 1) * cb].transpose([1, 0, 2]),
            in_=sb_in.rearrange("p (j i) -> p j i", i=cb))
    return eng.dma_start(out=dstw[r0:r0 + pr, c0:c0 + pc], in_=sb_in)



def emit_preamble(P):
    nc = P.nc
    need_bf = sorted({w for s in P.sublayers for w in SUB_W[s][0]})
    need_sm = sorted({w for s in P.sublayers for w in SUB_W[s][1]})
    P.deferred_p1 = []
    P.deferred_p2 = []
    if DEFER_CONV and P.sublayers and P.sublayers[0] == 0 and len(P.sublayers) > 1:
        first = set(SUB_W[0][0])
        early = {w for s in P.sublayers if s in (1, 2, 3, 4) for w in SUB_W[s][0]}
        P.deferred_p1 = sorted(w for w in (early - first) if w in need_bf)
        if 2 in P.sublayers:
            P.deferred_p2 = sorted(set(need_bf) - first - early)
        else:
            P.deferred_p1 = sorted(set(need_bf) - first)
        need_bf = sorted(first)

    ones_f = P.sb("ones_f", [128, 1], F32)
    ones_b = P.sb("ones_b", [128, 1], BF16)
    onesr_f = P.sb("onesr_f", [1, 128], F32)
    ident_b = P.sb("ident_b", [128, 128], BF16)
    masks = P.sb("masks", [128, 4 * 512], F32)
    itf = P.sb("iota_f", [128, 512], F32)
    iw = P.sb("iota_i32", [128, 512], I32)
    eps_t = P.sb("eps_t", [128, 1], F32)
    P.sb("ffn_xr", [128, DC * 512], F32)
    P.sb("ffn_xb", [128, DC * 512], BF16)
    P.sb("ffn_xsq", [128, DC * 512], BF16)
    P.sb("ffn_ta", [128, 2 * 512], F32)
    P.sb("ffn_tb", [128, 2 * 512], F32)
    P.sb("ffn_rout", [128, DC * 512], F32)
    P.sb("ffn_mrow", [1, 512], F32)
    P.sb("ffn_vrow", [1, 512], F32)
    P.sb("ffn_trow", [1, 512], F32)
    P.sb("ffn_wab", [128, 3 * 2 * 8 * 256], BF16)
    P.sb("ffn_hh", [128, 3 * 32 * 128], BF16)
    hin_t = P.sb("ffn_hin", [128, 32 * 512], BF16)
    g1_t = P.sb("ffn_g1", [128, 32 * 512], BF16)
    cv_in = hin_t.ap().bitcast(F32)
    cv_out = g1_t.ap()
    c_ld = P.ctr("pre_ld")
    c_ld2 = P.ctr("pre_ld2")
    c_cv = [P.ctr("pre_cv0"), P.ctr("pre_cv1")]
    c_g = P.ctr("pre_g")
    c_v = P.ctr("pre_v")

    with nc.Block() as blk:
        @blk.sync
        def _(sy):
            for nm in need_sm:
                shp = W_SMALL[nm]
                ap = P.din(nm, shp, F32)
                t = P.sb("sm_" + nm, [shp[0], shp[1]], F32)
                sy.dma_start(out=t[:], in_=ap[:]).then_inc(c_ld.sem, 16)
                c_ld.n += 16
            sy.wait_ge(c_ld.sem, c_ld.n)

        @blk.gpsimd
        def _(g):
            g.memset(ones_f[:], 1.0)
            g.memset(onesr_f[:], 1.0)
            g.memset(eps_t[:], EPS)
            g.iota(iw[:], [[1, 512]], base=0, channel_multiplier=-1)
            g.drain()
            g.tensor_copy(ones_b[:], ones_f[:])
            g.tensor_copy(itf[:], iw[:])
            g.drain()
            g.nop().then_inc(c_g.sem, 1)
        c_g.n += 1

        @blk.vector
        def _(v):
            c_g.wait(v)
            v.tensor_scalar(ident_b[:], itf[:, 0:128], 0.0, None, op0=ALU.is_equal)
            for mi in range(4):
                v.tensor_scalar(masks[:, mi * 512:(mi + 1) * 512], itf[:], float(mi * 128), None, op0=ALU.is_ge)
            v.drain()
            v.nop().then_inc(c_v.sem, 1)
        c_v.n += 1

    jobs = []
    for nm in need_bf:
        R, C = W_BF16[nm]
        for r0 in range(0, R, 128):
            jobs.append((nm, r0, min(128, R - r0), C))
    big = [j for j in jobs if j[3] > 2048]
    rest = [j for j in jobs if j[3] <= 2048]
    half = len(big) // 2
    work = {"act": big[:half], "dve": big[half:], "pool": rest}
    xr_t = P.sb("ffn_xr", [128, DC * 512], F32)
    rout_t = P.sb("ffn_rout", [128, DC * 512], F32)
    wab_t = P.t("ffn_wab"); hh_t = P.t("ffn_hh")
    xb_t = P.t("ffn_xb"); xsq_t = P.t("ffn_xsq")
    slots = {
        "act": ([cv_in[:, 0:4096], cv_in[:, 4096:8192]],
                [cv_out[:, 0:4096], cv_out[:, 4096:8192]]),
        "dve": ([xr_t.ap(), rout_t.ap()],
                [cv_out[:, 8192:12288], cv_out[:, 12288:16384]]),
        "pool": ([wab_t.ap().bitcast(F32)[:, 0:2048], hh_t.ap().bitcast(F32)[:, 0:2048]],
                 [xb_t.ap()[:, 0:2048], xsq_t.ap()[:, 0:2048]]),
    }
    c_ld_e = {k: P.ctr("pre_cvld" + k) for k in ("act", "pool")}
    c_ld_dve = [P.ctr("pre_cvldv0"), P.ctr("pre_cvldv1")]
    c_st_e = {k: [P.ctr("pre_cvst" + k + "0"), P.ctr("pre_cvst" + k + "1")] for k in work}

    def emit_convert(eng, key):
        ins, outs = slots[key]
        cl = c_ld_e[key]
        ci = 0
        for nm, r0, pr, C in work[key]:
            srcw = P.din(nm, W_BF16[nm], F32)
            par = ci % 2
            if ci >= 2:
                eng.wait_ge(c_st_e[key][par].sem, 16 * (ci // 2))
            eng.dma_start(out=ins[par][0:pr, 0:C], in_=srcw[r0:r0 + pr, :]).then_inc(cl.sem, 16)
            cl.n += 16
            eng.wait_ge(cl.sem, cl.n)
            if key == "act":
                eng.copy(outs[par][0:pr, 0:C], ins[par][0:pr, 0:C])
            else:
                eng.tensor_copy(outs[par][0:pr, 0:C], ins[par][0:pr, 0:C])
            eng.drain()
            bf_store(eng, P, nm, r0, pr, 0, C, outs[par][0:pr, 0:C]).then_inc(c_st_e[key][par].sem, 16)
            c_st_e[key][par].n += 16
            ci += 1
        for cs in c_st_e[key]:
            eng.wait_ge(cs.sem, cs.n)

    c_cvc = P.ctr("pre_cvc")
    with nc.Block() as blk2:
        @blk2.scalar
        def _(s):
            emit_convert(s, "act")

        @blk2.sync
        def _(sy):
            ins, outs = slots["dve"]
            wk = work["dve"]
            for ci, (nm, r0, pr, C) in enumerate(wk):
                srcw = P.din(nm, W_BF16[nm], F32)
                par = ci % 2
                if ci >= 2:
                    sy.wait_ge(c_st_e["dve"][par].sem, 16 * (ci // 2))
                sy.dma_start(out=ins[par][0:pr, 0:C], in_=srcw[r0:r0 + pr, :]).then_inc(c_ld_dve[par].sem, 16)
                c_ld_dve[par].n += 16
                if ci >= 1:
                    pnm, pr0, ppr, pC = wk[ci - 1]
                    c_cvc.wait(sy, ci)
                    bf_store(sy, P, pnm, pr0, ppr, 0, pC, outs[(ci - 1) % 2][0:ppr, 0:pC]
                             ).then_inc(c_st_e["dve"][(ci - 1) % 2].sem, 16)
                    c_st_e["dve"][(ci - 1) % 2].n += 16
            if wk:
                ci = len(wk)
                pnm, pr0, ppr, pC = wk[ci - 1]
                c_cvc.wait(sy, ci)
                bf_store(sy, P, pnm, pr0, ppr, 0, pC, outs[(ci - 1) % 2][0:ppr, 0:pC]
                         ).then_inc(c_st_e["dve"][(ci - 1) % 2].sem, 16)
                c_st_e["dve"][(ci - 1) % 2].n += 16
            for cs in c_st_e["dve"]:
                sy.wait_ge(cs.sem, cs.n)

        @blk2.vector
        def _(v):
            ins, outs = slots["dve"]
            for ci, (nm, r0, pr, C) in enumerate(work["dve"]):
                par = ci % 2
                v.wait_ge(c_ld_dve[par].sem, 16 * (ci // 2 + 1))
                v.tensor_copy(outs[par][0:pr, 0:C], ins[par][0:pr, 0:C])
                v.drain()
                v.nop().then_inc(c_cvc.sem, 1)
                c_cvc.n += 1

        @blk2.gpsimd
        def _(g):
            emit_convert(g, "pool")


class DConv:

    def __init__(self, P, tag, names, arena="ffn_hh"):
        self.P = P
        hh_t = P.t(arena)
        hv = hh_t.ap().bitcast(F32)
        self.ins = [hv[:, 0:2048], hv[:, 2048:4096]]
        self.outs = [hh_t.ap()[:, 8192:10240], hh_t.ap()[:, 10240:12288]]
        self.cl = [P.ctr(tag + "_gld0"), P.ctr(tag + "_gld1")]
        self.cs = [P.ctr(tag + "_gst0"), P.ctr(tag + "_gst1")]
        self.jobs = []
        for nm in names:
            R, C = W_BF16[nm]
            for r0 in range(0, R, 128):
                for c0 in range(0, C, 2048):
                    self.jobs.append((nm, r0, min(128, R - r0), c0, min(2048, C - c0)))
        self.n_ld = 0
        self.n_pr = 0

    def _load(self, g):
        nm, r0, pr, c0, pc = self.jobs[self.n_ld]
        srcw = self.P.din(nm, W_BF16[nm], F32)
        par = self.n_ld % 2
        g.dma_start(out=self.ins[par][0:pr, 0:pc],
                    in_=srcw[r0:r0 + pr, c0:c0 + pc]).then_inc(self.cl[par].sem, 16)
        self.cl[par].n += 16
        self.n_ld += 1

    def step(self, g, count):
        while self.n_ld < min(2, len(self.jobs)):
            self._load(g)
        for _ in range(count):
            if self.n_pr >= len(self.jobs):
                return
            ci = self.n_pr
            nm, r0, pr, c0, pc = self.jobs[ci]
            par = ci % 2
            if ci >= 2:
                g.wait_ge(self.cs[par].sem, 16 * (ci // 2))
            g.wait_ge(self.cl[par].sem, 16 * (ci // 2 + 1))
            g.tensor_copy(self.outs[par][0:pr, 0:pc], self.ins[par][0:pr, 0:pc])
            g.drain()
            bf_store(g, self.P, nm, r0, pr, c0, pc,
                     self.outs[par][0:pr, 0:pc]).then_inc(self.cs[par].sem, 16)
            self.cs[par].n += 16
            self.n_pr += 1
            if self.n_ld < len(self.jobs):
                self._load(g)

    def finish(self, g):
        self.step(g, len(self.jobs) - self.n_pr)
        for c in self.cs:
            if c.n:
                g.wait_ge(c.sem, c.n)



def emit_ffn(P, L, tag, src, dst):
    nc = P.nc
    w1 = bf_scratch(P, f"fw1T{L}")
    w2 = bf_scratch(P, f"fw2T{L}")
    h1 = bf_scratch(P, f"fh1T{L}")
    h2 = bf_scratch(P, f"fh2T{L}")
    w3 = bf_scratch(P, f"fw3T{L}")
    s1 = P.t(f"sm_fs1{L}"); s2 = P.t(f"sm_fs2{L}")
    sh1 = P.t(f"sm_fsh1{L}"); sh2 = P.t(f"sm_fsh2{L}")
    s3 = P.t(f"sm_fs3{L}")

    xr = P.sb("ffn_xr", [128, DC * 512], F32)
    xb = P.sb("ffn_xb", [128, DC * 512], BF16)
    xsq = P.sb("ffn_xsq", [128, DC * 512], BF16)
    hin = P.sb("ffn_hin", [128, 32 * 512], BF16)
    g1b = P.sb("ffn_g1", [128, 32 * 512], BF16)
    ta = P.sb("ffn_ta", [128, 2 * 512], F32)
    tb = P.sb("ffn_tb", [128, 2 * 512], F32)
    rout = P.sb("ffn_rout", [128, DC * 512], F32)
    mrow = P.sb("ffn_mrow", [1, 512], F32)
    vrow = P.sb("ffn_vrow", [1, 512], F32)
    trow = P.sb("ffn_trow", [1, 512], F32)
    RB = 3; RH = 3
    wab = P.sb("ffn_wab", [128, RB * 2 * 8 * 256], BF16)
    hh = P.sb("ffn_hh", [128, RH * 32 * 128], BF16)

    ones_b = P.t("ones_b"); onesr_f = P.t("onesr_f")

    C = lambda nm: P.ctr(tag + "_" + nm)
    c_ldx = C("ldx")
    c_wB = [C("wB0"), C("wB1"), C("wB2")]
    c_wH = [C("wH0"), C("wH1"), C("wH2")]
    c_sq = C("sq"); c_stat = C("stat"); c_var = C("var"); c_rstd = C("rstd")
    c_mr = C("mr"); c_bc = C("bc"); c_lnc = C("lnc"); c_xq = C("xq")
    c_Ba = C("Ba"); c_Bb = C("Bb"); c_h = C("h")
    c_C = C("C"); c_D = C("D"); c_E = C("E")
    c_st = [C("st0"), C("st1")]
    base = {c: c.n for c in [c_ldx, c_sq, c_stat, c_var, c_rstd,
                             c_mr, c_bc, c_Ba, c_Bb, c_h, c_C, c_D, c_E] + c_wB + c_wH + c_st}
    B = lambda c: base[c]
    ev = P.ps.ev
    evb = [e.n for e in ev]
    PERT = [20, 20, 16, 16, 32, 32, 2, 2]

    def evw(eng, k, val):
        if val > 0:
            eng.wait_ge(ev[k].sem, evb[k] + val)

    NTILE = B_CORE * NT

    def tiles():
        for b in range(B_CORE):
            for ti in range(NT):
                yield b * NT + ti, b, ti * 512

    cons_sig = []
    for idx in range(NTILE):
        for j in range(32):
            cons_sig.append((c_C, B(c_C) + 32 * idx + j + 1))
        for j in range(32):
            cons_sig.append((c_D, B(c_D) + 32 * idx + j + 1))
        for j in range(8):
            cons_sig.append((c_E, B(c_E) + 8 * idx + j + 1))

    with nc.Block() as blk:
        @blk.sync
        def _(sy):
            slot = 0
            prev = None
            for idx, b, t0 in tiles():
                for i in range(DC):
                    if idx > 0:
                        evw(sy, i % 2, 20 * (idx - 1) + 16 + i // 2 + 1)
                    sy.dma_start(out=xr.ap()[:, i * 512:(i + 1) * 512],
                                 in_=src[b, i, :, t0:t0 + 512]
                                 ).then_inc(c_ldx.sem, 16)
                    c_ldx.n += 16
                if idx > 0:
                    evw(sy, 0, 20 * idx)
                    evw(sy, 1, 20 * idx)
                    pb, pt0 = prev
                    sy.dma_start(out=dst[pb, :, :, pt0:pt0 + 512].transpose([1, 0, 2]),
                                 in_=rout.ap().rearrange("p (c k) -> p c k", c=DC)
                                 ).then_inc(c_st[(idx - 1) % 2].sem, 16)
                    c_st[(idx - 1) % 2].n += 16
                prev = (b, t0)
                for j in range(16):
                    if slot >= RB:
                        sidx, sj = divmod(slot - RB, 16)
                        c_Bb.wait(sy, B(c_Bb) + 32 * sidx + 2 * (sj + 1))
                    cw = c_wB[slot % RB]
                    off = (slot % RB) * 2 * 8 * 256
                    sy.dma_start(out=wab[:, off:off + 8 * 256],
                                 in_=w1[j]).then_inc(cw.sem, 16)
                    sy.dma_start(out=wab[:, off + 8 * 256:off + 16 * 256],
                                 in_=w2[j]).then_inc(cw.sem, 16)
                    cw.n += 32
                    slot += 1
            evw(sy, 0, 20 * NTILE)
            evw(sy, 1, 20 * NTILE)
            pb, pt0 = prev
            cs = c_st[(NTILE - 1) % 2]
            sy.dma_start(out=dst[pb, :, :, pt0:pt0 + 512].transpose([1, 0, 2]),
                         in_=rout.ap().rearrange("p (c k) -> p c k", c=DC)
                         ).then_inc(cs.sem, 16)
            cs.n += 16
            sy.wait_ge(c_st[0].sem, c_st[0].n)
            sy.wait_ge(c_st[1].sem, c_st[1].n)

        @blk.gpsimd
        def _(g):
            slot = 0
            for idx, b, t0 in tiles():
                for phase, wsrc, nslot in (("C", h1, 32), ("D", h2, 32), ("E", w3, 8)):
                    for j in range(nslot):
                        if slot >= RH:
                            ctr, val = cons_sig[slot - RH]
                            ctr.wait(g, val)
                        cw = c_wH[slot % RH]
                        off = (slot % RH) * 32 * 128
                        g.dma_start(out=hh[:, off:off + 32 * 128],
                                    in_=wsrc[j]
                                    ).then_inc(cw.sem, 16)
                        cw.n += 16
                        slot += 1

        @blk.tensor
        def _(t):
            bslot = 0
            hslot = 0
            for idx, b, t0 in tiles():
                c_sq.wait(t, B(c_sq) + idx + 1)
                evw(t, 6, 2 * idx); evw(t, 7, 2 * idx)
                for dc in range(DC):
                    t.matmul(P.ps.tile(6, 1, 512), ones_b[:], xb[:, dc * 512:(dc + 1) * 512],
                             start=(dc == 0), stop=(dc == DC - 1))
                for dc in range(DC):
                    inst = t.matmul(P.ps.tile(7, 1, 512), ones_b[:], xsq[:, dc * 512:(dc + 1) * 512],
                                    start=(dc == 0), stop=(dc == DC - 1))
                c_stat.inc(inst)
                c_mr.wait(t, B(c_mr) + idx + 1)
                evw(t, 6, 2 * idx + 1); evw(t, 7, 2 * idx + 1)
                t.matmul(P.ps.tile(6), onesr_f[:], trow[:], start=True, stop=True)
                inst = t.matmul(P.ps.tile(7), onesr_f[:], mrow[:], start=True, stop=True)
                c_bc.inc(inst)
                evw(t, 6, 2 * (idx + 1)); evw(t, 7, 2 * (idx + 1))
                for j in range(16):
                    c_wB[bslot % RB].wait(t, B(c_wB[bslot % RB]) + 32 * (bslot // RB + 1))
                    off = (bslot % RB) * 2 * 8 * 256
                    for h in range(2):
                        i = 2 * j + h
                        ka = i % 2
                        kb = 2 + i % 2
                        evw(t, ka, 20 * idx + i // 2)
                        for k in range(8):
                            inst = t.matmul(P.ps.tile(ka), wab[:, off + k * 256 + h * 128: off + k * 256 + h * 128 + 128],
                                            xb[:, k * 512:(k + 1) * 512], start=(k == 0), stop=(k == 7))
                        c_Ba.inc(inst)
                        evw(t, kb, 16 * idx + i // 2)
                        for k in range(8):
                            inst = t.matmul(P.ps.tile(kb), wab[:, off + 8 * 256 + k * 256 + h * 128: off + 8 * 256 + k * 256 + h * 128 + 128],
                                            xb[:, k * 512:(k + 1) * 512], start=(k == 0), stop=(k == 7))
                        c_Bb.inc(inst)
                    bslot += 1
                evw(t, 2, 16 * (idx + 1)); evw(t, 3, 16 * (idx + 1))
                for i in range(32):
                    c_wH[hslot % RH].wait(t, B(c_wH[hslot % RH]) + 16 * (hslot // RH + 1))
                    off = (hslot % RH) * 32 * 128
                    kk = 4 + i % 2
                    evw(t, kk, 32 * idx + i // 2)
                    for k in range(32):
                        inst = t.matmul(P.ps.tile(kk), hh[:, off + k * 128: off + (k + 1) * 128],
                                        hin[:, k * 512:(k + 1) * 512], start=(k == 0), stop=(k == 31))
                    c_C.inc(inst)
                    hslot += 1
                evw(t, 4, 32 * idx + 16); evw(t, 5, 32 * idx + 16)
                for i in range(32):
                    c_wH[hslot % RH].wait(t, B(c_wH[hslot % RH]) + 16 * (hslot // RH + 1))
                    off = (hslot % RH) * 32 * 128
                    kk = 4 + i % 2
                    evw(t, kk, 32 * idx + 16 + i // 2)
                    for k in range(32):
                        inst = t.matmul(P.ps.tile(kk), hh[:, off + k * 128: off + (k + 1) * 128],
                                        g1b[:, k * 512:(k + 1) * 512], start=(k == 0), stop=(k == 31))
                    c_D.inc(inst)
                    hslot += 1
                evw(t, 4, 32 * (idx + 1)); evw(t, 5, 32 * (idx + 1))
                for i in range(8):
                    c_wH[hslot % RH].wait(t, B(c_wH[hslot % RH]) + 16 * (hslot // RH + 1))
                    off = (hslot % RH) * 32 * 128
                    kk = i % 2
                    evw(t, kk, 20 * idx + 16 + i // 2)
                    for k in range(32):
                        inst = t.matmul(P.ps.tile(kk), hh[:, off + k * 128: off + (k + 1) * 128],
                                        hin[:, k * 512:(k + 1) * 512], start=(k == 0), stop=(k == 31))
                    c_E.inc(inst)
                    hslot += 1

        @blk.scalar
        def _(s):
            for idx, b, t0 in tiles():
                c_ldx.wait(s, B(c_ldx) + 128 * (idx + 1))
                if idx > 0:
                    c_Bb.wait(s, B(c_Bb) + 32 * idx)
                inst = s.copy(xb[:, 0:2048], xr[:, 0:2048])
                s.copy(xb[:, 2048:4096], xr[:, 2048:4096])
                s.drain()
                s.activation(xsq[:, 0:2048], xb[:, 0:2048], AF.Square)
                inst = s.activation(xsq[:, 2048:4096], xb[:, 2048:4096], AF.Square)
                c_sq.inc(inst)
                c_var.wait(s, B(c_var) + idx + 1)
                if idx > 0:
                    c_bc.wait(s, B(c_bc) + idx)
                s.activation(trow[:], vrow[:], AF.Ln, bias=P.t('eps_t')[0:1, :])
                s.drain()
                inst = s.activation(trow[:], trow[:], AF.Exp, scale=-0.5)
                c_rstd.inc(inst)
                for i in range(32):
                    c_Ba.wait(s, B(c_Ba) + 32 * idx + i + 1)
                    if 32 * idx + i >= 2:
                        evw(s, 2 + i % 2, 16 * idx + i // 2)
                    ka = i % 2
                    inst = s.activation(ta[:, (i % 2) * 512:(i % 2 + 1) * 512], P.ps.tile(ka),
                                        AF.Silu, scale=s1[:, i:i + 1])
                    P.ps.release(ka, inst)
                if idx > 0:
                    c_D.wait(s, B(c_D) + 32 * idx)
                for i in range(32):
                    c_C.wait(s, B(c_C) + 32 * idx + i + 1)
                    kk = 4 + i % 2
                    inst = s.activation(g1b[:, i * 512:(i + 1) * 512], P.ps.tile(kk),
                                        AF.Gelu, scale=sh1[:, i:i + 1])
                    P.ps.release(kk, inst)
                c_C.wait(s, B(c_C) + 32 * (idx + 1))
                for i in range(32):
                    c_D.wait(s, B(c_D) + 32 * idx + i + 1)
                    kk = 4 + i % 2
                    inst = s.activation(hin[:, i * 512:(i + 1) * 512], P.ps.tile(kk),
                                        AF.Gelu, scale=sh2[:, i:i + 1])
                    P.ps.release(kk, inst)

        @blk.vector
        def _(v):
            for idx, b, t0 in tiles():
                c_stat.wait(v, B(c_stat) + idx + 1)
                if idx > 0:
                    c_bc.wait(v, B(c_bc) + idx)
                inst = v.tensor_scalar(mrow[:], P.ps.tile(6, 1, 512), 1.0 / D, None, op0=ALU.mult)
                P.ps.release(6, inst)
                inst = v.tensor_scalar(vrow[:], P.ps.tile(7, 1, 512), 1.0 / D, None, op0=ALU.mult)
                P.ps.release(7, inst)
                v.drain()
                v.tensor_tensor(trow[:], mrow[:], mrow[:], op=ALU.mult)
                v.drain()
                inst = v.tensor_tensor(vrow[:], vrow[:], trow[:], op=ALU.subtract)
                c_var.inc(inst)
                v.drain()
                c_rstd.wait(v, B(c_rstd) + idx + 1)
                inst = v.tensor_tensor(mrow[:], mrow[:], trow[:], op=ALU.mult)
                c_mr.inc(inst)
                v.drain()
                c_bc.wait(v, B(c_bc) + idx + 1)
                for dc in range(0, DC, 2):
                    s0 = slice(dc * 512, (dc + 1) * 512)
                    s1_ = slice((dc + 1) * 512, (dc + 2) * 512)
                    v.tensor_tensor(tb[:, 0:512], xr[:, s0], P.ps.tile(6), op=ALU.mult)
                    v.tensor_tensor(tb[:, 512:1024], xr[:, s1_], P.ps.tile(6), op=ALU.mult)
                    v.drain()
                    inst1 = v.tensor_tensor(xb[:, s0], tb[:, 0:512], P.ps.tile(7), op=ALU.subtract)
                    inst2 = v.tensor_tensor(xb[:, s1_], tb[:, 512:1024], P.ps.tile(7), op=ALU.subtract)
                    v.drain()
                    if dc == DC - 2:
                        P.ps.release(6, inst1); P.ps.release(7, inst2)
                if idx > 0:
                    c_E.wait(v, B(c_E) + 8 * idx)
                for i in range(32):
                    c_Bb.wait(v, B(c_Bb) + 32 * idx + i + 1)
                    kb = 2 + i % 2
                    evw(v, i % 2, 20 * idx + i // 2 + 1)
                    inst = v.scalar_tensor_tensor(hin[:, i * 512:(i + 1) * 512], P.ps.tile(kb),
                                                  s2[:, i:i + 1], ta[:, (i % 2) * 512:(i % 2 + 1) * 512],
                                                  op0=ALU.mult, op1=ALU.mult)
                    P.ps.release(kb, inst)
                if idx > 0:
                    p_ = (idx - 1) % 2
                    c_st[p_].wait(v, B(c_st[p_]) + 16 * ((idx - 1) // 2 + 1))
                for i in range(DC):
                    c_E.wait(v, B(c_E) + 8 * idx + i + 1)
                    kk = i % 2
                    inst = v.scalar_tensor_tensor(rout[:, i * 512:(i + 1) * 512], P.ps.tile(kk),
                                                  s3[:, i:i + 1], xr[:, i * 512:(i + 1) * 512],
                                                  op0=ALU.mult, op1=ALU.add)
                    P.ps.release(kk, inst)



def emit_attn_a0(P, src, dst):
    nc = P.nc
    wq = P.dscratch("wqT0_bf", (1024, 1024), BF16)
    wk = P.dscratch("wkT0_bf", (1024, 1024), BF16)
    wv = P.dscratch("wvT0_bf", (1024, 1024), BF16)
    wo = P.dscratch("woT0_bf", (1024, 1024), BF16)
    qbuf = P.dscratch("a0_qbuf", (B_CORE, DC, 128, S), BF16)
    kbuf = P.dscratch("a0_kbuf", (B_CORE, DC, 128, S), BF16)
    vtbuf = P.dscratch("a0_vtbuf", (B_CORE, S // 128, 128, 1024), BF16)

    xr = P.t("ffn_xr"); xb = P.t("ffn_xb"); xsq = P.t("ffn_xsq")
    hin = P.t("ffn_hin"); g1b = P.t("ffn_g1")
    ta = P.t("ffn_ta"); tb = P.t("ffn_tb")
    rout = P.t("ffn_rout")
    mrow = P.t("ffn_mrow"); vrow = P.t("ffn_vrow"); trow = P.t("ffn_trow")
    wab = P.t("ffn_wab")
    hh = P.t("ffn_hh")
    masks = P.t("masks")
    ones_b = P.t("ones_b"); onesr_f = P.t("onesr_f"); ident_b = P.t("ident_b")
    bq = P.t("sm_bq0"); bk = P.t("sm_bk0"); bv = P.t("sm_bv0"); bo = P.t("sm_bo0")
    bq32 = P.sb("a0_bq32", [128, DC], F32)

    q_sb = [hin.ap()[:, 0:4096], hin.ap()[:, 4096:8192]]
    k_sb = [hin.ap()[:, 8192:12288], hin.ap()[:, 12288:16384]]
    vt_sb = [g1b.ap()[:, 0:4096], g1b.ap()[:, 4096:8192]]
    vtmp = ta.ap().bitcast(BF16)
    P_all = hin.ap()
    qblk = g1b.ap()[:, 0:4096]
    out_blk = g1b.ap()[:, 4096:8192]
    KR = 4; VR = 4
    kring = g1b.ap()[:, 8192:8192 + KR * 1024]
    vring = g1b.ap()[:, 12288:12288 + VR * 1024]
    rb_sb = tb.ap()[:, 0:512]
    tmp_abs = [ta.ap()[:, 0:512], ta.ap()[:, 512:1024]]
    _dg = xsq.ap().bitcast(F32)
    tmp_dg = [_dg[:, 0:512], _dg[:, 512:1024]]

    Ct = lambda nm: P.ctr("a0_" + nm)
    c_ldx = Ct("ldx"); c_sq = Ct("sq"); c_stat = Ct("stat"); c_var = Ct("var")
    c_rstd = Ct("rstd"); c_mr = Ct("mr"); c_bc = Ct("bc")
    c_pq = Ct("pq"); c_pk = Ct("pk"); c_pv = Ct("pv"); c_ptr = Ct("ptr")
    c_wr = [Ct("wr0"), Ct("wr1"), Ct("wr2")]
    c_stq = [Ct("stq0"), Ct("stq1")]; c_stk = [Ct("stk0"), Ct("stk1")]; c_stv = [Ct("stv0"), Ct("stv1")]
    c_ldq = Ct("ldq"); c_ldx2 = Ct("ldx2")
    c_S = Ct("S"); c_abs = Ct("abs"); c_Pa = Ct("Pa"); c_Pd = Ct("Pd")
    c_den = Ct("den"); c_bcm = Ct("bcm"); c_rb = Ct("rb")
    c_AV = Ct("AV"); c_ave = Ct("ave"); c_Wo = Ct("Wo")
    c_kr = [Ct(f"kr{i}") for i in range(KR)]; c_vr = [Ct(f"vr{i}") for i in range(VR)]
    c_wor = [Ct("wor0"), Ct("wor1"), Ct("wor2")]
    c_sto = [Ct("sto0"), Ct("sto1")]
    ev = P.ps.ev
    c_pvv = Ct("pvv")
    allc = ([c_ldx, c_sq, c_stat, c_var, c_rstd, c_mr, c_bc, c_pq, c_pk, c_pv, c_ptr, c_pvv,
             c_ldq, c_ldx2, c_S, c_abs, c_Pa, c_Pd, c_den, c_bcm, c_rb, c_AV, c_ave, c_Wo]
            + c_wr + c_stq + c_stk + c_stv + c_kr + c_vr + c_wor + c_sto)
    base = {c: c.n for c in allc}
    B = lambda c: base[c]
    NTILE = B_CORE * NT

    evb = [e.n for e in ev]
    def evw(eng, kk, val):
        if val > 0:
            eng.wait_ge(ev[kk].sem, evb[kk] + val)
    with nc.Block() as blk:
        @blk.vector
        def _v0(v):
            v.tensor_scalar(bq32[:], bq[:], 1.0 / 32.0, None, op0=ALU.mult)
            v.drain()

        @blk.sync
        def _(sy):
            slot = 0
            for idx in range(NTILE):
                b, ti = divmod(idx, NT); t0 = ti * 512
                if idx > 0:
                    evw(sy, 6, 18 * (idx - 1) + 2)
                sy.dma_start(out=xr.ap().rearrange("p (c k) -> p c k", c=DC),
                             in_=src[b, :, :, t0:t0 + 512].transpose([1, 0, 2])
                             ).then_inc(c_ldx.sem, 16)
                c_ldx.n += 16
                for i in range(8):
                    if slot >= 3:
                        sidx, si = divmod(slot - 3, 8)
                        c_pv.wait(sy, B(c_pv) + 8 * sidx + si + 1)
                    cw = c_wr[slot % 3]
                    off = (slot % 3) * 4096
                    for wsrc, o2 in ((wq, 0), (wk, 1024), (wv, 2048)):
                        sy.dma_start(out=wab[:, off + o2:off + o2 + 1024].rearrange("p (k i) -> p k i", k=8),
                                     in_=wsrc[:, i * 128:(i + 1) * 128].rearrange("(k p) i -> p k i", p=128)
                                     ).then_inc(cw.sem, 16)
                        cw.n += 16
                    slot += 1
                if idx > 0:
                    pidx = idx - 1
                    pb, pti = divmod(pidx, NT); pt0 = pti * 512
                    par = pidx % 2
                    evw(sy, 0, 4 * (pidx + 1)); evw(sy, 1, 4 * (pidx + 1))
                    cs = c_stq[par]
                    sy.dma_start(out=qbuf[pb, :, :, pt0:pt0 + 512].transpose([1, 0, 2]),
                                 in_=q_sb[par].rearrange("p (c k) -> p c k", c=DC)).then_inc(cs.sem, 16)
                    cs.n += 16
                    evw(sy, 2, 4 * (pidx + 1)); evw(sy, 3, 4 * (pidx + 1))
                    cs = c_stk[par]
                    sy.dma_start(out=kbuf[pb, :, :, pt0:pt0 + 512].transpose([1, 0, 2]),
                                 in_=k_sb[par].rearrange("p (c k) -> p c k", c=DC)).then_inc(cs.sem, 16)
                    cs.n += 16
                    evw(sy, 6, 18 * (pidx + 1)); evw(sy, 7, 18 * (pidx + 1))
                    cs = c_stv[par]
                    sy.dma_start(out=vtbuf[pb, pti * 4:(pti + 1) * 4].rearrange("c p k -> p c k"),
                                 in_=vt_sb[par].rearrange("p (c k) -> p c k", c=4)).then_inc(cs.sem, 16)
                    cs.n += 16
            pidx = NTILE - 1
            pb, pti = divmod(pidx, NT); pt0 = pti * 512
            par = pidx % 2
            evw(sy, 0, 4 * (pidx + 1)); evw(sy, 1, 4 * (pidx + 1))
            cs = c_stq[par]
            sy.dma_start(out=qbuf[pb, :, :, pt0:pt0 + 512].transpose([1, 0, 2]),
                         in_=q_sb[par].rearrange("p (c k) -> p c k", c=DC)).then_inc(cs.sem, 16)
            cs.n += 16
            evw(sy, 2, 4 * (pidx + 1)); evw(sy, 3, 4 * (pidx + 1))
            cs = c_stk[par]
            sy.dma_start(out=kbuf[pb, :, :, pt0:pt0 + 512].transpose([1, 0, 2]),
                         in_=k_sb[par].rearrange("p (c k) -> p c k", c=DC)).then_inc(cs.sem, 16)
            cs.n += 16
            evw(sy, 6, 18 * (pidx + 1)); evw(sy, 7, 18 * (pidx + 1))
            cs = c_stv[par]
            sy.dma_start(out=vtbuf[pb, pti * 4:(pti + 1) * 4].rearrange("c p k -> p c k"),
                         in_=vt_sb[par].rearrange("p (c k) -> p c k", c=4)).then_inc(cs.sem, 16)
            cs.n += 16
            for cs in c_stq + c_stk + c_stv:
                sy.wait_ge(cs.sem, cs.n)

        @blk.tensor
        def _(t):
            slot = 0
            for idx in range(NTILE):
                c_sq.wait(t, B(c_sq) + idx + 1)
                evw(t, 6, 18 * idx); evw(t, 7, 18 * idx)
                for dc in range(DC):
                    t.matmul(P.ps.tile(6, 1, 512), ones_b[:], xb[:, dc * 512:(dc + 1) * 512],
                             start=(dc == 0), stop=(dc == DC - 1))
                for dc in range(DC):
                    inst = t.matmul(P.ps.tile(7, 1, 512), ones_b[:], xsq[:, dc * 512:(dc + 1) * 512],
                                    start=(dc == 0), stop=(dc == DC - 1))
                c_stat.inc(inst)
                c_mr.wait(t, B(c_mr) + idx + 1)
                evw(t, 6, 18 * idx + 1); evw(t, 7, 18 * idx + 1)
                t.matmul(P.ps.tile(6), onesr_f[:], trow[:], start=True, stop=True)
                inst = t.matmul(P.ps.tile(7), onesr_f[:], mrow[:], start=True, stop=True)
                c_bc.inc(inst)
                evw(t, 6, 18 * idx + 2); evw(t, 7, 18 * idx + 2)
                ntr = 0
                for i in range(8):
                    cw = c_wr[slot % 3]
                    cw.wait(t, B(cw) + 48 * (slot // 3 + 1))
                    off = (slot % 3) * 4096
                    evw(t, i % 2, 4 * idx + i // 2)
                    for k in range(8):
                        inst = t.matmul(P.ps.tile(i % 2), wab[:, off + k * 128:off + (k + 1) * 128],
                                        xb[:, k * 512:(k + 1) * 512], start=(k == 0), stop=(k == 7))
                    c_pq.inc(inst)
                    evw(t, 2 + i % 2, 4 * idx + i // 2)
                    for k in range(8):
                        inst = t.matmul(P.ps.tile(2 + i % 2), wab[:, off + 1024 + k * 128:off + 1024 + (k + 1) * 128],
                                        xb[:, k * 512:(k + 1) * 512], start=(k == 0), stop=(k == 7))
                    c_pk.inc(inst)
                    evw(t, 4 + i % 2, 4 * idx + i // 2)
                    for k in range(8):
                        inst = t.matmul(P.ps.tile(4 + i % 2), wab[:, off + 2048 + k * 128:off + 2048 + (k + 1) * 128],
                                        xb[:, k * 512:(k + 1) * 512], start=(k == 0), stop=(k == 7))
                    c_pv.inc(inst)
                    slot += 1
                    if i >= 1:
                        ii = i - 1
                        evw(t, 4 + ii % 2, 4 * idx + ii // 2 + 1)
                        for sub in range(4):
                            tr = ii * 4 + sub
                            evw(t, 6 + tr % 2, 18 * idx + 2 + tr // 2)
                            inst = t.transpose(P.ps.tile(6 + tr % 2).bitcast(BF16)[:, 0:128],
                                               vtmp[:, (ii % 2) * 512 + sub * 128:(ii % 2) * 512 + (sub + 1) * 128],
                                               ident_b[:])
                            c_ptr.inc(inst)
                ii = 7
                evw(t, 4 + ii % 2, 4 * idx + ii // 2 + 1)
                for sub in range(4):
                    tr = ii * 4 + sub
                    evw(t, 6 + tr % 2, 18 * idx + 2 + tr // 2)
                    inst = t.transpose(P.ps.tile(6 + tr % 2).bitcast(BF16)[:, 0:128],
                                       vtmp[:, (ii % 2) * 512 + sub * 128:(ii % 2) * 512 + (sub + 1) * 128],
                                       ident_b[:])
                    c_ptr.inc(inst)

        @blk.scalar
        def _(s):
            for idx in range(NTILE):
                par = idx % 2
                c_ldx.wait(s, B(c_ldx) + 16 * (idx + 1))
                if idx > 0:
                    c_pk.wait(s, B(c_pk) + 8 * idx)
                    c_pv.wait(s, B(c_pv) + 8 * idx)
                s.copy(xb[:], xr[:])
                s.drain()
                inst = s.activation(xsq[:], xb[:], AF.Square)
                c_sq.inc(inst)
                c_var.wait(s, B(c_var) + idx + 1)
                if idx > 0:
                    c_bc.wait(s, B(c_bc) + idx)
                s.activation(trow[:], vrow[:], AF.Ln, bias=P.t('eps_t')[0:1, :])
                s.drain()
                inst = s.activation(trow[:], trow[:], AF.Exp, scale=-0.5)
                c_rstd.inc(inst)
                if idx >= 2:
                    for cs in (c_stq[par], c_stk[par], c_stv[par]):
                        s.wait_ge(cs.sem, B(cs) + 16 * (idx // 2))
                for i in range(8):
                    c_pq.wait(s, B(c_pq) + 8 * idx + i + 1)
                    inst = s.activation(q_sb[par][:, i * 512:(i + 1) * 512], P.ps.tile(i % 2),
                                        AF.Identity, scale=1.0 / 32.0, bias=bq32[:, i:i + 1])
                    P.ps.release(i % 2, inst)
                    c_pk.wait(s, B(c_pk) + 8 * idx + i + 1)
                    inst = s.activation(k_sb[par][:, i * 512:(i + 1) * 512], P.ps.tile(2 + i % 2),
                                        AF.Identity, bias=bk[:, i:i + 1])
                    P.ps.release(2 + i % 2, inst)
                    c_pv.wait(s, B(c_pv) + 8 * idx + i + 1)
                    if i >= 2:
                        c_ptr.wait(s, B(c_ptr) + 32 * idx + 4 * (i - 1))
                    inst = s.activation(vtmp[:, (i % 2) * 512:(i % 2 + 1) * 512], P.ps.tile(4 + i % 2),
                                        AF.Identity, bias=bv[:, i:i + 1])
                    P.ps.release(4 + i % 2, inst)

        @blk.vector
        def _(v):
            for idx in range(NTILE):
                par = idx % 2
                c_stat.wait(v, B(c_stat) + idx + 1)
                if idx > 0:
                    c_bc.wait(v, B(c_bc) + idx)
                inst = v.tensor_scalar(mrow[:], P.ps.tile(6, 1, 512), 1.0 / D, None, op0=ALU.mult)
                P.ps.release(6, inst)
                inst = v.tensor_scalar(vrow[:], P.ps.tile(7, 1, 512), 1.0 / D, None, op0=ALU.mult)
                P.ps.release(7, inst)
                v.drain()
                v.tensor_tensor(trow[:], mrow[:], mrow[:], op=ALU.mult)
                v.drain()
                inst = v.tensor_tensor(vrow[:], vrow[:], trow[:], op=ALU.subtract)
                c_var.inc(inst)
                v.drain()
                c_rstd.wait(v, B(c_rstd) + idx + 1)
                inst = v.tensor_tensor(mrow[:], mrow[:], trow[:], op=ALU.mult)
                c_mr.inc(inst)
                v.drain()
                c_bc.wait(v, B(c_bc) + idx + 1)
                for dc in range(DC):
                    sl = slice(dc * 512, (dc + 1) * 512)
                    inst1 = v.tensor_tensor(tb[:, 0:512], xr[:, sl], P.ps.tile(6), op=ALU.mult)
                    v.drain()
                    inst2 = v.tensor_tensor(xb[:, sl], tb[:, 0:512], P.ps.tile(7), op=ALU.subtract)
                    if dc == DC - 1:
                        P.ps.release(6, inst1); P.ps.release(7, inst2)
                    v.drain()
                if idx >= 2:
                    cs = c_stv[par]
                    v.wait_ge(cs.sem, B(cs) + 16 * (idx // 2))
                for tr in range(32):
                    c_ptr.wait(v, B(c_ptr) + 32 * idx + tr + 1)
                    i, sub = divmod(tr, 4)
                    kk = 6 + tr % 2
                    inst = v.tensor_copy(vt_sb[par][:, sub * 1024 + i * 128: sub * 1024 + (i + 1) * 128],
                                         P.ps.tile(kk).bitcast(BF16)[:, 0:128])
                    P.ps.release(kk, inst)

    evb = [e.n for e in ev]

    cum01 = []; cum25 = []; cum67 = []; cumS = []; cumPd = []
    a0 = a25 = a67 = aS = aPd = 0
    for g in range(NTILE):
        qb = g % NT; nkt = 4 * qb + 4
        cum01.append(a0); cum25.append(a25); cum67.append(a67)
        cumS.append(aS); cumPd.append(aPd)
        a0 += (2 * qb + 2) + 1 + 4
        a25 += 1; a67 += 2; aS += nkt; aPd += 4
    totS = aS

    sig_k = []; sig_v = []
    for g in range(NTILE):
        qb = g % NT; nkt = 4 * qb + 4
        for kt in range(nkt):
            sig_k.append(B(c_S) + cumS[g] + kt + 1)
            sig_v.append(B(c_AV) + cumS[g] + kt + 1)

    c_wo16 = Ct("wo16")
    wo_sb = hh.ap()[:, 0:8192]

    with nc.Block() as blk:
        @blk.gpsimd
        def _(g_):
            g_.dma_start(out=wo_sb.rearrange("p (k i) -> p k i", k=8),
                         in_=wo.rearrange("(k p) i -> p k i", p=128)).then_inc(c_wo16.sem, 16)
            c_wo16.n += 16
            if P.deferred_p1:
                dc = DConv(P, "dca", P.deferred_p1, arena="ffn_wab")
                dc.finish(g_)

        @blk.sync
        def _(sy):
            kl = 0; vl = 0
            prev = None
            for g in range(NTILE):
                b, qb = divmod(g, NT); t0 = qb * 512; nkt = 4 * qb + 4
                if g > 0:
                    evw(sy, 0, cum01[g]); evw(sy, 1, cum01[g])
                    pb, pt0 = prev
                    cs = c_sto[(g - 1) % 2]
                    sy.dma_start(out=dst[pb, :, :, pt0:pt0 + 512].transpose([1, 0, 2]),
                                 in_=rout.ap().rearrange("p (c k) -> p c k", c=DC)).then_inc(cs.sem, 16)
                    cs.n += 16
                prev = (b, t0)
                sy.dma_start(out=xr.ap().rearrange("p (c k) -> p c k", c=DC),
                             in_=src[b, :, :, t0:t0 + 512].transpose([1, 0, 2])).then_inc(c_ldx2.sem, 16)
                c_ldx2.n += 16
                c_S.wait(sy, B(c_S) + cumS[g])
                sy.dma_start(out=qblk.rearrange("p (c k) -> p c k", c=DC),
                             in_=qbuf[b, :, :, t0:t0 + 512].transpose([1, 0, 2])).then_inc(c_ldq.sem, 16)
                c_ldq.n += 16
                for kt in range(nkt):
                    if kl >= KR:
                        c_S.wait(sy, sig_k[kl - KR])
                    cw = c_kr[kl % KR]
                    sy.dma_start(out=kring[:, (kl % KR) * 1024:(kl % KR + 1) * 1024].rearrange("p (c k) -> p c k", c=DC),
                                 in_=kbuf[b, :, :, kt * 128:(kt + 1) * 128].transpose([1, 0, 2])
                                 ).then_inc(cw.sem, 16)
                    cw.n += 16
                    kl += 1
                for kt in range(nkt):
                    if vl >= VR:
                        c_AV.wait(sy, sig_v[vl - VR])
                    cw = c_vr[vl % VR]
                    sy.dma_start(out=vring[:, (vl % VR) * 1024:(vl % VR + 1) * 1024],
                                 in_=vtbuf[b, kt]).then_inc(cw.sem, 16)
                    cw.n += 16
                    vl += 1
            evw(sy, 0, cum01[NTILE - 1] + (2 * (NT - 1) + 2) + 5)
            evw(sy, 1, cum01[NTILE - 1] + (2 * (NT - 1) + 2) + 5)
            pb, pt0 = prev
            cs = c_sto[(NTILE - 1) % 2]
            sy.dma_start(out=dst[pb, :, :, pt0:pt0 + 512].transpose([1, 0, 2]),
                         in_=rout.ap().rearrange("p (c k) -> p c k", c=DC)).then_inc(cs.sem, 16)
            cs.n += 16
            sy.wait_ge(c_sto[0].sem, c_sto[0].n)
            sy.wait_ge(c_sto[1].sem, c_sto[1].n)

        @blk.tensor
        def _(t):
            kl = 0; vl = 0
            c_wo16.wait(t)
            for g in range(NTILE):
                b, qb = divmod(g, NT); nkt = 4 * qb + 4
                c_ldq.wait(t, B(c_ldq) + 16 * (g + 1))
                def denom_mm(kt):
                    if kt == 0:
                        evw(t, 6, cum67[g])
                    c_Pa.wait(t, B(c_Pa) + cumS[g] + kt + 1)
                    if kt >= nkt - 4:
                        c_Pd.wait(t, B(c_Pd) + cumPd[g] + (kt - (nkt - 4)) + 1)
                    inst = t.matmul(P.ps.tile(6, 1, 512), ones_b[:], P_all[:, kt * 512:(kt + 1) * 512],
                                    start=(kt == 0), stop=(kt == nkt - 1))
                    if kt == nkt - 1:
                        c_den.inc(inst)
                for kt in range(nkt):
                    cw = c_kr[kl % KR]
                    cw.wait(t, B(cw) + 16 * (kl // KR + 1))
                    evw(t, kt % 2, cum01[g] + kt // 2)
                    koff = (kl % KR) * 1024
                    for dc in range(DC):
                        inst = t.matmul(P.ps.tile(kt % 2), kring[:, koff + dc * 128:koff + (dc + 1) * 128],
                                        qblk[:, dc * 512:(dc + 1) * 512], start=(dc == 0), stop=(dc == DC - 1))
                    c_S.inc(inst)
                    kl += 1
                    if kt >= 2:
                        denom_mm(kt - 2)
                denom_mm(nkt - 2); denom_mm(nkt - 1)
                evw(t, 6, cum67[g] + 1)
                evw(t, 7, cum67[g])
                inst = t.matmul(P.ps.tile(7), onesr_f[:], mrow[:], start=True, stop=True)
                c_bcm.inc(inst)
                for kt in range(nkt):
                    cw = c_vr[vl % VR]
                    cw.wait(t, B(cw) + 16 * (vl // VR + 1))
                    voff = (vl % VR) * 1024
                    if kt == 0:
                        evw(t, 0, cum01[g] + 2 * qb + 2); evw(t, 1, cum01[g] + 2 * qb + 2)
                        for kk in range(2, 6):
                            evw(t, kk, cum25[g])
                        evw(t, 6, cum67[g] + 1); evw(t, 7, cum67[g] + 1)
                    for vd in range(8):
                        inst = t.matmul(P.ps.tile(vd), vring[:, voff + vd * 128:voff + (vd + 1) * 128],
                                        P_all[:, kt * 512:(kt + 1) * 512], start=(kt == 0), stop=(kt == nkt - 1))
                    c_AV.inc(inst)
                    vl += 1
                for kk in range(2, 6):
                    evw(t, kk, cum25[g] + 1)
                evw(t, 6, cum67[g] + 2); evw(t, 7, cum67[g] + 2)
                for i in range(8):
                    evw(t, i % 2, cum01[g] + (2 * qb + 2) + 1 + i // 2)
                    for k in range(8):
                        inst = t.matmul(P.ps.tile(i % 2), wo_sb[:, k * 1024 + i * 128:k * 1024 + (i + 1) * 128],
                                        out_blk[:, k * 512:(k + 1) * 512], start=(k == 0), stop=(k == 7))
                    c_Wo.inc(inst)

        @blk.scalar
        def _(s):
            for g in range(NTILE):
                b, qb = divmod(g, NT); nkt = 4 * qb + 4
                if g > 0:
                    c_AV.wait(s, B(c_AV) + cumS[g])
                for kt in range(nkt):
                    c_S.wait(s, B(c_S) + cumS[g] + kt + 1)
                    inst = s.activation(tmp_abs[kt % 2], P.ps.tile(kt % 2), AF.Abs)
                    P.ps.release(kt % 2, inst)
                    s.drain()
                    diag = kt >= nkt - 4
                    if diag:
                        mi = kt - (nkt - 4)
                        if mi >= 2:
                            c_Pd.wait(s, B(c_Pd) + cumPd[g] + mi - 1)
                        out_ap = tmp_dg[kt % 2]
                    else:
                        out_ap = P_all[:, kt * 512:(kt + 1) * 512]
                    inst = s.activation(out_ap, tmp_abs[kt % 2], AF.Exp)
                    c_Pa.inc(inst)
                    s.drain()

        @blk.vector
        def _(v):
            for g in range(NTILE):
                b, qb = divmod(g, NT); nkt = 4 * qb + 4
                def mask_op(ktd):
                    mi = ktd - (nkt - 4)
                    c_Pa.wait(v, B(c_Pa) + cumS[g] + ktd + 1)
                    inst = v.tensor_tensor(P_all[:, ktd * 512:(ktd + 1) * 512], tmp_dg[ktd % 2],
                                           masks[:, mi * 512:(mi + 1) * 512], op=ALU.mult)
                    c_Pd.inc(inst)
                for ktd in range(max(0, nkt - 4), nkt):
                    mask_op(ktd)
                c_den.wait(v, B(c_den) + g + 1)
                inst = v.reciprocal(mrow[:], P.ps.tile(6, 1, 512))
                P.ps.release(6, inst)
                v.drain()
                c_bcm.wait(v, B(c_bcm) + g + 1)
                inst = v.tensor_copy(rb_sb, P.ps.tile(7))
                P.ps.release(7, inst)
                v.drain()
                c_AV.wait(v, B(c_AV) + cumS[g] + nkt)
                for vd in range(8):
                    inst = v.tensor_tensor(out_blk[:, vd * 512:(vd + 1) * 512], P.ps.tile(vd), rb_sb, op=ALU.mult)
                    P.ps.release(vd, inst)
                if g > 0:
                    p_ = (g - 1) % 2
                    c_sto[p_].wait(v, B(c_sto[p_]) + 16 * ((g - 1) // 2 + 1))
                c_ldx2.wait(v, B(c_ldx2) + 16 * (g + 1))
                for i in range(8):
                    c_Wo.wait(v, B(c_Wo) + 8 * g + i + 1)
                    inst = v.scalar_tensor_tensor(rout.ap()[:, i * 512:(i + 1) * 512], P.ps.tile(i % 2),
                                                  bo[:, i:i + 1], xr[:, i * 512:(i + 1) * 512],
                                                  op0=ALU.add, op1=ALU.add)
                    P.ps.release(i % 2, inst)



def emit_qkv_phaseP(P, tag, w1, b1c, qbuf, kbuf, ktbuf, vtbuf, src):
    nc = P.nc

    xr = P.t("ffn_xr"); xb = P.t("ffn_xb"); xsq = P.t("ffn_xsq")
    hin = P.t("ffn_hin"); g1b = P.t("ffn_g1")
    ta = P.t("ffn_ta"); tb = P.t("ffn_tb")
    rout = P.t("ffn_rout")
    mrow = P.t("ffn_mrow"); vrow = P.t("ffn_vrow"); trow = P.t("ffn_trow")
    wab = P.t("ffn_wab"); hh = P.t("ffn_hh")
    masks = P.t("masks")
    ones_b = P.t("ones_b"); onesr_f = P.t("onesr_f"); ident_b = P.t("ident_b")

    q_sb = [hin.ap()[:, 0:4096], hin.ap()[:, 4096:8192]]
    k_sb = [hin.ap()[:, 8192:12288], hin.ap()[:, 12288:16384]]
    vt_sb = [g1b.ap()[:, 0:4096], g1b.ap()[:, 4096:8192]]
    kt_sb = [g1b.ap()[:, 8192:12288], g1b.ap()[:, 12288:16384]]
    vtmp = ta.ap().bitcast(BF16)

    Ct = lambda nm: P.ctr(tag + "_" + nm)
    c_ldx = Ct("ldx"); c_sq = Ct("sq"); c_stat = Ct("stat"); c_var = Ct("var")
    c_rstd = Ct("rstd"); c_mr = Ct("mr"); c_bc = Ct("bc")
    c_pq = Ct("pq"); c_pk = Ct("pk"); c_pv = Ct("pv")
    c_ek = Ct("ek"); c_ptrk = Ct("ptrk"); c_ptrv = Ct("ptrv")
    c_wr = [Ct("wr0"), Ct("wr1"), Ct("wr2")]
    c_stq = [Ct("stq0"), Ct("stq1")]; c_stk = [Ct("stk0"), Ct("stk1")]
    c_stkt = [Ct("stkt0"), Ct("stkt1")]; c_stv = [Ct("stv0"), Ct("stv1")]
    ev = P.ps.ev
    allc = ([c_ldx, c_sq, c_stat, c_var, c_rstd, c_mr, c_bc, c_pq, c_pk, c_pv,
             c_ek, c_ptrk, c_ptrv]
            + c_wr + c_stq + c_stk + c_stkt + c_stv)
    base = {c: c.n for c in allc}
    B = lambda c: base[c]
    NTILE = B_CORE * NT

    evb = [e.n for e in ev]
    def evw(eng, kk, val):
        if val > 0:
            eng.wait_ge(ev[kk].sem, evb[kk] + val)
    with nc.Block() as blk:
        @blk.sync
        def _(sy):
            slot = 0
            for idx in range(NTILE):
                b, ti = divmod(idx, NT); t0 = ti * 512
                if idx > 0:
                    evw(sy, 6, 34 * (idx - 1) + 2)
                sy.dma_start(out=xr.ap().rearrange("p (c k) -> p c k", c=DC),
                             in_=src[b, :, :, t0:t0 + 512].transpose([1, 0, 2])
                             ).then_inc(c_ldx.sem, 16)
                c_ldx.n += 16
                for i in range(24):
                    if slot >= 3:
                        sidx, si = divmod(slot - 3, 24)
                        gctr = c_pk if si < 8 else (c_pq if si < 16 else c_pv)
                        gctr.wait(sy, B(gctr) + 8 * sidx + (si % 8) + 1)
                    cw = c_wr[slot % 3]
                    off = (slot % 3) * 4096
                    sy.dma_start(out=wab[:, off:off + 1024].rearrange("p (k i) -> p k i", k=8),
                                 in_=w1[:, i * 128:(i + 1) * 128].rearrange("(k p) i -> p k i", p=128)
                                 ).then_inc(cw.sem, 16)
                    cw.n += 16
                    slot += 1
                def stores(pidx):
                    pb, pti = divmod(pidx, NT); pt0 = pti * 512
                    par = pidx % 2
                    evw(sy, 0, 4 * (pidx + 1)); evw(sy, 1, 4 * (pidx + 1))
                    cs = c_stq[par]
                    sy.dma_start(out=qbuf[pb, :, :, pt0:pt0 + 512].transpose([1, 0, 2]),
                                 in_=q_sb[par].rearrange("p (c k) -> p c k", c=DC)).then_inc(cs.sem, 16)
                    cs.n += 16
                    evw(sy, 2, 4 * (pidx + 1)); evw(sy, 3, 4 * (pidx + 1))
                    cs = c_stk[par]
                    sy.dma_start(out=kbuf[pb, :, :, pt0:pt0 + 512].transpose([1, 0, 2]),
                                 in_=k_sb[par].rearrange("p (c k) -> p c k", c=DC)).then_inc(cs.sem, 16)
                    cs.n += 16
                    evw(sy, 6, 34 * (pidx + 1)); evw(sy, 7, 34 * (pidx + 1))
                    cs = c_stkt[par]
                    sy.dma_start(out=ktbuf[pb, pti * 4:(pti + 1) * 4].rearrange("c p k -> p c k"),
                                 in_=kt_sb[par].rearrange("p (c k) -> p c k", c=4)).then_inc(cs.sem, 16)
                    cs.n += 16
                    cs = c_stv[par]
                    sy.dma_start(out=vtbuf[pb, pti * 4:(pti + 1) * 4].rearrange("c p k -> p c k"),
                                 in_=vt_sb[par].rearrange("p (c k) -> p c k", c=4)).then_inc(cs.sem, 16)
                    cs.n += 16
                if idx > 0:
                    stores(idx - 1)
            stores(NTILE - 1)
            for cs in c_stq + c_stk + c_stkt + c_stv:
                sy.wait_ge(cs.sem, cs.n)

        @blk.tensor
        def _(t):
            slot = 0
            for idx in range(NTILE):
                c_sq.wait(t, B(c_sq) + idx + 1)
                evw(t, 6, 34 * idx); evw(t, 7, 34 * idx)
                for dc in range(DC):
                    t.matmul(P.ps.tile(6, 1, 512), ones_b[:], xb[:, dc * 512:(dc + 1) * 512],
                             start=(dc == 0), stop=(dc == DC - 1))
                for dc in range(DC):
                    inst = t.matmul(P.ps.tile(7, 1, 512), ones_b[:], xsq[:, dc * 512:(dc + 1) * 512],
                                    start=(dc == 0), stop=(dc == DC - 1))
                c_stat.inc(inst)
                c_mr.wait(t, B(c_mr) + idx + 1)
                evw(t, 6, 34 * idx + 1); evw(t, 7, 34 * idx + 1)
                t.matmul(P.ps.tile(6), onesr_f[:], trow[:], start=True, stop=True)
                inst = t.matmul(P.ps.tile(7), onesr_f[:], mrow[:], start=True, stop=True)
                c_bc.inc(inst)
                evw(t, 6, 34 * idx + 2); evw(t, 7, 34 * idx + 2)
                par = idx % 2

                def vtrs(ii):
                    for sub in range(4):
                        tr = ii * 4 + sub
                        evw(t, 4 + ii % 2, 4 * idx + ii // 2 + 1)
                        evw(t, 6 + tr % 2, 34 * idx + 2 + tr // 2)
                        inst = t.transpose(P.ps.tile(6 + tr % 2).bitcast(BF16)[:, 0:128],
                                           vtmp[:, (ii % 2) * 512 + sub * 128:(ii % 2) * 512 + (sub + 1) * 128],
                                           ident_b[:])
                        c_ptrv.inc(inst)
                for i in range(24):
                    cw = c_wr[slot % 3]
                    cw.wait(t, B(cw) + 16 * (slot // 3 + 1))
                    off = (slot % 3) * 4096
                    if i < 8:
                        bank = 2 + i % 2; ctr = c_pk; rel = 4 * idx + i // 2
                    elif i < 16:
                        bank = (i - 8) % 2; ctr = c_pq; rel = 4 * idx + (i - 8) // 2
                    else:
                        bank = 4 + (i - 16) % 2; ctr = c_pv; rel = 4 * idx + (i - 16) // 2
                    evw(t, bank, rel)
                    for k in range(8):
                        inst = t.matmul(P.ps.tile(bank), wab[:, off + k * 128:off + (k + 1) * 128],
                                        xb[:, k * 512:(k + 1) * 512], start=(k == 0), stop=(k == 7))
                    ctr.inc(inst)
                    slot += 1
                    if i >= 17:
                        vtrs(i - 17)
                vtrs(7)
                for trk in range(32):
                    i, sub = divmod(trk, 4)
                    tr = 32 + trk
                    evw(t, 2 + i % 2, 4 * idx + i // 2 + 1)
                    evw(t, 6 + tr % 2, 34 * idx + 2 + tr // 2)
                    inst = t.transpose(P.ps.tile(6 + tr % 2).bitcast(BF16)[:, 0:128],
                                       k_sb[par][:, i * 512 + sub * 128: i * 512 + (sub + 1) * 128],
                                       ident_b[:])
                    c_ptrk.inc(inst)

        @blk.scalar
        def _(s):
            for idx in range(NTILE):
                par = idx % 2
                c_ldx.wait(s, B(c_ldx) + 16 * (idx + 1))
                if idx > 0:
                    c_pv.wait(s, B(c_pv) + 8 * idx)
                s.copy(xb[:], xr[:])
                s.drain()
                inst = s.activation(xsq[:], xb[:], AF.Square)
                c_sq.inc(inst)
                c_var.wait(s, B(c_var) + idx + 1)
                if idx > 0:
                    c_bc.wait(s, B(c_bc) + idx)
                s.activation(trow[:], vrow[:], AF.Ln, bias=P.t('eps_t')[0:1, :])
                s.drain()
                inst = s.activation(trow[:], trow[:], AF.Exp, scale=-0.5)
                c_rstd.inc(inst)
                if idx >= 2:
                    for cs in (c_stq[par], c_stk[par], c_stkt[par], c_stv[par]):
                        s.wait_ge(cs.sem, B(cs) + 16 * (idx // 2))
                    c_ptrk.wait(s, B(c_ptrk) + 32 * (idx - 1))
                for i in range(8):
                    c_pk.wait(s, B(c_pk) + 8 * idx + i + 1)
                    inst = s.activation(k_sb[par][:, i * 512:(i + 1) * 512], P.ps.tile(2 + i % 2),
                                        AF.Identity, bias=b1c[:, i:i + 1])
                    P.ps.release(2 + i % 2, inst)
                for i in range(8):
                    c_pq.wait(s, B(c_pq) + 8 * idx + i + 1)
                    inst = s.activation(q_sb[par][:, i * 512:(i + 1) * 512], P.ps.tile(i % 2),
                                        AF.Identity, bias=b1c[:, 8 + i:9 + i])
                    P.ps.release(i % 2, inst)
                for i in range(8):
                    c_pv.wait(s, B(c_pv) + 8 * idx + i + 1)
                    if i >= 2:
                        c_ptrv.wait(s, B(c_ptrv) + 32 * idx + 4 * (i - 1))
                    inst = s.activation(vtmp[:, (i % 2) * 512:(i % 2 + 1) * 512], P.ps.tile(4 + i % 2),
                                        AF.Identity, bias=b1c[:, 16 + i:17 + i])
                    P.ps.release(4 + i % 2, inst)

        @blk.vector
        def _(v):
            for idx in range(NTILE):
                par = idx % 2
                c_stat.wait(v, B(c_stat) + idx + 1)
                if idx > 0:
                    c_bc.wait(v, B(c_bc) + idx)
                inst = v.tensor_scalar(mrow[:], P.ps.tile(6, 1, 512), 1.0 / D, None, op0=ALU.mult)
                P.ps.release(6, inst)
                inst = v.tensor_scalar(vrow[:], P.ps.tile(7, 1, 512), 1.0 / D, None, op0=ALU.mult)
                P.ps.release(7, inst)
                v.drain()
                v.tensor_tensor(trow[:], mrow[:], mrow[:], op=ALU.mult)
                v.drain()
                inst = v.tensor_tensor(vrow[:], vrow[:], trow[:], op=ALU.subtract)
                c_var.inc(inst)
                v.drain()
                c_rstd.wait(v, B(c_rstd) + idx + 1)
                inst = v.tensor_tensor(mrow[:], mrow[:], trow[:], op=ALU.mult)
                c_mr.inc(inst)
                v.drain()
                c_bc.wait(v, B(c_bc) + idx + 1)
                for dc in range(DC):
                    sl = slice(dc * 512, (dc + 1) * 512)
                    inst1 = v.tensor_tensor(tb[:, 0:512], xr[:, sl], P.ps.tile(6), op=ALU.mult)
                    v.drain()
                    inst2 = v.tensor_tensor(xb[:, sl], tb[:, 0:512], P.ps.tile(7), op=ALU.subtract)
                    if dc == DC - 1:
                        P.ps.release(6, inst1); P.ps.release(7, inst2)
                    v.drain()
                if idx >= 2:
                    for cs in (c_stkt[par], c_stv[par]):
                        v.wait_ge(cs.sem, B(cs) + 16 * (idx // 2))
                for tr in range(64):
                    (c_ptrv if tr < 32 else c_ptrk).wait(
                        v, (B(c_ptrv) if tr < 32 else B(c_ptrk)) + 32 * idx + (tr % 32) + 1)
                    kk = 6 + tr % 2
                    if tr < 32:
                        i, sub = divmod(tr, 4)
                        dst_ap = vt_sb[par][:, sub * 1024 + i * 128: sub * 1024 + (i + 1) * 128]
                    else:
                        i, sub = divmod(tr - 32, 4)
                        dst_ap = kt_sb[par][:, sub * 1024 + i * 128: sub * 1024 + (i + 1) * 128]
                    inst = v.tensor_copy(dst_ap, P.ps.tile(kk).bitcast(BF16)[:, 0:128])
                    P.ps.release(kk, inst)


def emit_attn_a1(P, src, dst):
    nc = P.nc
    w1 = P.dscratch("w1T1_bf", (1024, 3072), BF16)
    w2 = P.dscratch("w2T1_bf", (1024, 1024), BF16)
    qbuf = P.dscratch("a1_qbuf", (B_CORE, DC, 128, S), BF16)
    kbuf = P.dscratch("a1_kbuf", (B_CORE, DC, 128, S), BF16)
    ktbuf = P.dscratch("a1_ktbuf", (B_CORE, S // 128, 128, 1024), BF16)
    vtbuf = P.dscratch("a1_vtbuf", (B_CORE, S // 128, 128, 1024), BF16)
    b1c = P.t("sm_b1_1"); b2c = P.t("sm_b2_1")
    emit_qkv_phaseP(P, "a1p", w1, b1c, qbuf, kbuf, ktbuf, vtbuf, src)

    xr = P.t("ffn_xr")
    hin = P.t("ffn_hin"); g1b = P.t("ffn_g1")
    rout = P.t("ffn_rout")
    hh = P.t("ffn_hh")
    masks = P.t("masks")
    qblk = hin.ap()[:, 0:4096]
    kblk = hin.ap()[:, 4096:8192]
    ktblk = hin.ap()[:, 8192:12288]
    vtblk = hin.ap()[:, 12288:16384]
    yall = g1b.ap()[:, 0:4096]
    Abuf = g1b.ap().bitcast(BF16)[:, 4096 * 2:4096 * 2 + 1024]
    Mf = P.sb("a1_Mf", [128, 8 * 64], F32)
    Mb = P.sb("a1_Mb", [128, 8 * 64], BF16)
    w2_sb = hh.ap()[:, 0:8192]
    Ct = lambda nm: P.ctr("a1_" + nm)
    c_ldq = Ct("ldq"); c_ldk = Ct("ldk"); c_ldkt = Ct("ldkt"); c_ldvt = Ct("ldvt"); c_ldx2 = Ct("ldx2")
    c_S = Ct("S"); c_ymm = Ct("ymm"); c_dM = Ct("dM"); c_Mu = Ct("Mu")
    c_W2 = Ct("W2"); c_wo16 = Ct("wo16")
    c_sto = [Ct("sto0"), Ct("sto1")]
    ev = P.ps.ev
    allc = [c_ldq, c_ldk, c_ldkt, c_ldvt, c_ldx2, c_S, c_ymm, c_dM, c_Mu, c_W2, c_wo16] + c_sto
    base = {c: c.n for c in allc}
    B = lambda c: base[c]
    NTILE = B_CORE * NT
    evb = [e.n for e in ev]
    def evw(eng, kk, val):
        if val > 0:
            eng.wait_ge(ev[kk].sem, evb[kk] + val)
    with nc.Block() as blk:
        @blk.gpsimd
        def _(g_):
            g_.dma_start(out=w2_sb.rearrange("p (k i) -> p k i", k=8),
                         in_=w2.rearrange("(k p) i -> p k i", p=128)).then_inc(c_wo16.sem, 16)
            c_wo16.n += 16

        @blk.sync
        def _(sy):
            prev = None
            for g in range(NTILE):
                b, bk = divmod(g, NT); t0 = bk * 512
                if g > 0:
                    evw(sy, 6, 4 * g); evw(sy, 7, 4 * g)
                    pb, pt0 = prev
                    cs = c_sto[(g - 1) % 2]
                    sy.dma_start(out=dst[pb, :, :, pt0:pt0 + 512].transpose([1, 0, 2]),
                                 in_=rout.ap().rearrange("p (c k) -> p c k", c=DC)).then_inc(cs.sem, 16)
                    cs.n += 16
                prev = (b, t0)
                sy.dma_start(out=xr.ap().rearrange("p (c k) -> p c k", c=DC),
                             in_=src[b, :, :, t0:t0 + 512].transpose([1, 0, 2])).then_inc(c_ldx2.sem, 16)
                c_ldx2.n += 16
                c_S.wait(sy, B(c_S) + 64 * g)
                sy.dma_start(out=qblk.rearrange("p (c k) -> p c k", c=DC),
                             in_=qbuf[b, :, :, t0:t0 + 512].transpose([1, 0, 2])).then_inc(c_ldq.sem, 16)
                c_ldq.n += 16
                sy.dma_start(out=kblk.rearrange("p (c k) -> p c k", c=DC),
                             in_=kbuf[b, :, :, t0:t0 + 512].transpose([1, 0, 2])).then_inc(c_ldk.sem, 16)
                c_ldk.n += 16
                c_dM.wait(sy, B(c_dM) + 16 * g)
                sy.dma_start(out=ktblk.rearrange("p (c k) -> p c k", c=4),
                             in_=ktbuf[b, bk * 4:(bk + 1) * 4].rearrange("c p k -> p c k")).then_inc(c_ldkt.sem, 16)
                c_ldkt.n += 16
                sy.dma_start(out=vtblk.rearrange("p (c k) -> p c k", c=4),
                             in_=vtbuf[b, bk * 4:(bk + 1) * 4].rearrange("c p k -> p c k")).then_inc(c_ldvt.sem, 16)
                c_ldvt.n += 16
            evw(sy, 6, 4 * NTILE); evw(sy, 7, 4 * NTILE)
            pb, pt0 = prev
            cs = c_sto[(NTILE - 1) % 2]
            sy.dma_start(out=dst[pb, :, :, pt0:pt0 + 512].transpose([1, 0, 2]),
                         in_=rout.ap().rearrange("p (c k) -> p c k", c=DC)).then_inc(cs.sem, 16)
            cs.n += 16
            sy.wait_ge(c_sto[0].sem, c_sto[0].n)
            sy.wait_ge(c_sto[1].sem, c_sto[1].n)

        @blk.tensor
        def _(t):
            c_wo16.wait(t)
            for g in range(NTILE):
                b, bk = divmod(g, NT)
                c_ldq.wait(t, B(c_ldq) + 16 * (g + 1))
                c_ldk.wait(t, B(c_ldk) + 16 * (g + 1))
                c_ldkt.wait(t, B(c_ldkt) + 16 * (g + 1))
                c_ldvt.wait(t, B(c_ldvt) + 16 * (g + 1))
                if bk > 0:
                    c_Mu.wait(t, B(c_Mu) + 16 * g)
                for h in range(16):
                    hp, hf = divmod(h, 2)
                    pb_ = 64 * hf
                    yb = h % 2
                    evw(t, yb, 8 * g + h // 2)
                    if bk > 0:
                        t.matmul(P.ps.tile(yb, 64, 512), Mb[pb_:pb_ + 64, hp * 64:(hp + 1) * 64],
                                 qblk[pb_:pb_ + 64, hp * 512:(hp + 1) * 512], start=True, stop=False)
                    for sub in range(4):
                        sb_ = (h * 4 + sub) % 2
                        evw(t, 2 + sb_, 32 * g + (h * 4 + sub) // 2)
                        inst = t.matmul(P.ps.tile(2 + sb_), kblk[pb_:pb_ + 64, hp * 512 + sub * 128:hp * 512 + (sub + 1) * 128],
                                        qblk[pb_:pb_ + 64, hp * 512:(hp + 1) * 512], start=True, stop=True)
                        c_S.inc(inst)
                    for sub in range(4):
                        j = 4 * h + sub
                        evw(t, 2 + j % 2, 32 * g + j // 2 + 1)
                        inst = t.matmul(P.ps.tile(yb, 64, 512), vtblk[:, sub * 1024 + h * 64:sub * 1024 + (h + 1) * 64],
                                        Abuf[:, ((h * 4 + sub) % 2) * 512:(((h * 4 + sub) % 2) + 1) * 512],
                                        start=(bk == 0 and sub == 0), stop=(sub == 3))
                        c_ymm.inc(inst)
                    mb = 4 + h % 2
                    evw(t, mb, 8 * g + h // 2)
                    for sub in range(4):
                        inst = t.matmul(P.ps.tile(mb, 64, 64), ktblk[:, sub * 1024 + h * 64:sub * 1024 + (h + 1) * 64],
                                        vtblk[:, sub * 1024 + h * 64:sub * 1024 + (h + 1) * 64],
                                        start=(sub == 0), stop=(sub == 3))
                    c_dM.inc(inst)
                evw(t, 0, 8 * (g + 1)); evw(t, 1, 8 * (g + 1))
                for i in range(8):
                    evw(t, 6 + i % 2, 4 * g + i // 2)
                    for k in range(8):
                        inst = t.matmul(P.ps.tile(6 + i % 2), w2_sb[:, k * 1024 + i * 128:k * 1024 + (i + 1) * 128],
                                        yall[:, k * 512:(k + 1) * 512], start=(k == 0), stop=(k == 7))
                    c_W2.inc(inst)

        @blk.scalar
        def _(s):
            for g in range(NTILE):
                b, bk = divmod(g, NT)
                if g > 0:
                    c_W2.wait(s, B(c_W2) + 8 * g)
                for h in range(16):
                    hp, hf = divmod(h, 2)
                    c_ymm.wait(s, B(c_ymm) + 64 * g + 4 * h + 4)
                    yb = h % 2
                    inst = s.copy(yall[64 * hf:64 * hf + 64, hp * 512:(hp + 1) * 512],
                                  P.ps.tile(yb, 64, 512))
                    P.ps.release(yb, inst)
                    s.drain()

        @blk.vector
        def _(v):
            for g in range(NTILE):
                b, bk = divmod(g, NT)
                if bk == 0:
                    c_dM.wait(v, B(c_dM) + 16 * g)
                    v.memset(Mf[:], 0.0)
                    v.drain()
                for h in range(16):
                    for sub in range(4):
                        j = 4 * h + sub
                        c_S.wait(v, B(c_S) + 64 * g + j + 1)
                        if 64 * g + j >= 2:
                            c_ymm.wait(v, B(c_ymm) + 64 * g + j - 1)
                        sb_ = j % 2
                        inst = v.tensor_tensor(Abuf[:, sb_ * 512:(sb_ + 1) * 512], P.ps.tile(2 + sb_),
                                               masks[:, sub * 512:(sub + 1) * 512], op=ALU.mult)
                        P.ps.release(2 + sb_, inst)
                    hp, hf = divmod(h, 2)
                    mb = 4 + h % 2
                    c_dM.wait(v, B(c_dM) + 16 * g + h + 1)
                    msl = Mf[64 * hf:64 * hf + 64, hp * 64:(hp + 1) * 64]
                    inst = v.tensor_tensor(msl, P.ps.tile(mb, 64, 64), msl, op=ALU.add)
                    P.ps.release(mb, inst)
                    v.drain()
                    inst = v.tensor_copy(Mb[64 * hf:64 * hf + 64, hp * 64:(hp + 1) * 64], msl)
                    c_Mu.inc(inst)
                    v.drain()
                if g > 0:
                    p_ = (g - 1) % 2
                    c_sto[p_].wait(v, B(c_sto[p_]) + 16 * ((g - 1) // 2 + 1))
                c_ldx2.wait(v, B(c_ldx2) + 16 * (g + 1))
                for i in range(8):
                    c_W2.wait(v, B(c_W2) + 8 * g + i + 1)
                    inst = v.scalar_tensor_tensor(rout.ap()[:, i * 512:(i + 1) * 512], P.ps.tile(6 + i % 2),
                                                  b2c[:, i:i + 1], xr[:, i * 512:(i + 1) * 512],
                                                  op0=ALU.add, op1=ALU.add)
                    P.ps.release(6 + i % 2, inst)




def emit_attn_b0(P, src, dst):
    nc = P.nc
    w1 = P.dscratch("wb0_bf", (1024, 3072), BF16)
    wo = P.dscratch("toT_bf", (1024, 1024), BF16)
    qbuf = P.dscratch("b0_qbuf", (B_CORE, DC, 128, S), BF16)
    kbuf = P.dscratch("b0_kbuf", (B_CORE, DC, 128, S), BF16)
    ktbuf = P.dscratch("b0_ktbuf", (B_CORE, S // 128, 128, 1024), BF16)
    vtbuf = P.dscratch("b0_vtbuf", (B_CORE, S // 128, 128, 1024), BF16)
    b0z = P.t("sm_b0z")
    emit_qkv_phaseP(P, "b0p", w1, b0z, qbuf, kbuf, ktbuf, vtbuf, src)

    RRD = float(np.sqrt(np.sqrt(16.0)))
    CQ2 = float(1.0 / (np.sqrt(2.0) * 4.0))
    S32 = float(1.0 / np.sqrt(32.0))
    RMS_EPS = 1e-5

    xr = P.t("ffn_xr")
    hin = P.t("ffn_hin"); g1b = P.t("ffn_g1")
    ta = P.t("ffn_ta"); tb = P.t("ffn_tb")
    rout = P.t("ffn_rout")
    mrow = P.t("ffn_mrow"); vrow = P.t("ffn_vrow")
    hh = P.t("ffn_hh")
    masks = P.t("masks")
    ones_b = P.t("ones_b"); onesr_f = P.t("onesr_f")
    gcol = P.t("sm_tg")
    iw = P.t("iota_i32"); itf = P.t("iota_f")

    qblk = hin.ap()[:, 0:4096]
    kblk = hin.ap()[:, 4096:8192]
    ktblk = hin.ap()[:, 8192:12288]
    vtblk = hin.ap()[:, 12288:16384]
    g1h = g1b.ap()
    yall = g1h[:, 0:4096]
    qf_c0 = g1h[:, 4096:4608]; qf_c1 = g1h[:, 4608:5120]
    Abuf = g1h[:, 5120:7168]
    kfT_s = [g1h[:, 7168:7168 + 4 * 273], g1h[:, 15016:15016 + 4 * 273]]
    ysq_sb = g1h[:, 8360:8872]
    xbv = P.t("ffn_xb").ap()
    xsqv = P.t("ffn_xsq").ap()
    y_sb = xbv.bitcast(F32)[0:64, 0:512]
    qfc2 = xbv[0:32, 1024:1536]
    Rt = xbv[:, 1536:2048]
    onesrow_b = xbv[0:1, 512:1024]
    sqb_t = xbv.bitcast(F32)[:, 1536:1537]
    rb_sb = xbv.bitcast(F32)[0:64, 1024:1536]
    Rf = tb.ap()[:, 0:512]
    qrep_sb = ta.ap()
    t1_sb = tb.ap()
    Mf = g1h.bitcast(F32)[:, 4436:7508]
    Mb = xsqv[:, 0:3072]
    wo_sb = hh.ap()[:, 0:8192]

    Ct = lambda nm: P.ctr("b0_" + nm)
    c_ldq = Ct("ldq"); c_ldk = Ct("ldk"); c_ldkt = Ct("ldkt"); c_ldvt = Ct("ldvt"); c_ldx2 = Ct("ldx2")
    c_rep = Ct("rep"); c_qf = Ct("qf"); c_S = Ct("S"); c_sq2 = Ct("sq2"); c_A = Ct("A")
    c_yd = Ct("yd"); c_dM = Ct("dM"); c_Mu = Ct("Mu"); c_kf = Ct("kf")
    c_ysq = Ct("ysq"); c_rs = Ct("rs"); c_rr = Ct("rr"); c_rbc = Ct("rbc"); c_yn = Ct("yn")
    c_Wo = Ct("Wo"); c_wo16 = Ct("wo16"); c_init = Ct("init")
    c_sto = [Ct("sto0"), Ct("sto1")]
    ev = P.ps.ev
    allc = [c_ldq, c_ldk, c_ldkt, c_ldvt, c_ldx2, c_rep, c_qf, c_S, c_sq2, c_A,
            c_yd, c_dM, c_Mu, c_kf, c_ysq, c_rs, c_rr, c_rbc, c_yn, c_Wo, c_wo16, c_init] + c_sto
    base = {c: c.n for c in allc}
    B = lambda c: base[c]
    NTILE = B_CORE * NT
    evb = [e.n for e in ev]
    def evw(eng, kk, val):
        if val > 0:
            eng.wait_ge(ev[kk].sem, evb[kk] + val)
    SQB = float(np.sqrt(2.0) / 2.0)

    with nc.Block() as blk:
        @blk.gpsimd
        def _(g_):
            g_.dma_start(out=wo_sb.rearrange("p (k i) -> p k i", k=8),
                         in_=wo.rearrange("(k p) i -> p k i", p=128)).then_inc(c_wo16.sem, 16)
            c_wo16.n += 16
            g_.iota(iw[:, 0:128].rearrange("p (a b) -> p a b", a=8), [[1, 8], [0, 16]], base=0, channel_multiplier=-1)
            g_.iota(iw[:, 128:256].rearrange("p (a b) -> p a b", a=8), [[0, 8], [1, 16]], base=0, channel_multiplier=-1)
            g_.iota(iw[:, 256:384].rearrange("p (a b) -> p a b", a=8), [[1, 8], [0, 16]], base=8, channel_multiplier=-1)
            g_.drain()
            g_.tensor_copy(itf[:], iw[:])
            g_.drain()
            g_.nop().then_inc(c_init.sem, 1)
            dc2 = DConv(P, "dcb", P.deferred_p2, arena="ffn_wab") if P.deferred_p2 else None
            if dc2:
                dc2.step(g_, 0)
            for g in range(NTILE):
                for h in range(16):
                    evw(g_, 7, 16 * g + h + 1)
                    inst = g_.tensor_copy(Mb[:, h * 192:(h + 1) * 192], Mf[:, h * 192:(h + 1) * 192])
                    c_Mu.inc(inst)
                    g_.drain()
                    if dc2:
                        dc2.step(g_, 1)
            if dc2:
                dc2.finish(g_)
        c_init.n += 1

        @blk.vector
        def _(v):
            c_init.wait(v, B(c_init) + 1)
            v.tensor_scalar(Rf[:, 0:384], itf[:, 0:384], 0.0, None, op0=ALU.is_equal)
            v.drain()
            v.tensor_copy(Rt[:, 0:384], Rf[:, 0:384])
            v.memset(onesrow_b[:], 1.0)
            v.memset(sqb_t[:], SQB)
            v.drain()
            v.nop().then_inc(c_init.sem, 1)
        c_init.n += 1

        @blk.sync
        def _(sy):
            sy.wait_ge(c_init.sem, B(c_init) + 2)
            sy.dma_start(out=Rt[64:80, 0:384], in_=Rt[0:16, 0:384]).then_inc(c_init.sem, 16)
            c_init.n += 16
            sy.dma_start(out=qfc2[16:17, :], in_=onesrow_b[:]).then_inc(c_init.sem, 16)
            c_init.n += 16
            sy.wait_ge(c_init.sem, B(c_init) + 34)
            prev = None
            for g in range(NTILE):
                b, bk = divmod(g, NT); t0 = bk * 512
                if g > 0:
                    evw(sy, 0, 36 * g); evw(sy, 1, 36 * g)
                    pb, pt0 = prev
                    cs = c_sto[(g - 1) % 2]
                    sy.dma_start(out=dst[pb, :, :, pt0:pt0 + 512].transpose([1, 0, 2]),
                                 in_=rout.ap().rearrange("p (c k) -> p c k", c=DC)).then_inc(cs.sem, 16)
                    cs.n += 16
                prev = (b, t0)
                sy.dma_start(out=xr.ap().rearrange("p (c k) -> p c k", c=DC),
                             in_=src[b, :, :, t0:t0 + 512].transpose([1, 0, 2])).then_inc(c_ldx2.sem, 16)
                c_ldx2.n += 16
                c_S.wait(sy, B(c_S) + 64 * g)
                sy.dma_start(out=qblk.rearrange("p (c k) -> p c k", c=DC),
                             in_=qbuf[b, :, :, t0:t0 + 512].transpose([1, 0, 2])).then_inc(c_ldq.sem, 16)
                c_ldq.n += 16
                sy.dma_start(out=kblk.rearrange("p (c k) -> p c k", c=DC),
                             in_=kbuf[b, :, :, t0:t0 + 512].transpose([1, 0, 2])).then_inc(c_ldk.sem, 16)
                c_ldk.n += 16
                c_kf.wait(sy, B(c_kf) + 16 * g)
                sy.dma_start(out=ktblk.rearrange("p (c k) -> p c k", c=4),
                             in_=ktbuf[b, bk * 4:(bk + 1) * 4].rearrange("c p k -> p c k")).then_inc(c_ldkt.sem, 16)
                c_ldkt.n += 16
                c_dM.wait(sy, B(c_dM) + 16 * g)
                sy.dma_start(out=vtblk.rearrange("p (c k) -> p c k", c=4),
                             in_=vtbuf[b, bk * 4:(bk + 1) * 4].rearrange("c p k -> p c k")).then_inc(c_ldvt.sem, 16)
                c_ldvt.n += 16
            evw(sy, 0, 36 * NTILE); evw(sy, 1, 36 * NTILE)
            pb, pt0 = prev
            cs = c_sto[(NTILE - 1) % 2]
            sy.dma_start(out=dst[pb, :, :, pt0:pt0 + 512].transpose([1, 0, 2]),
                         in_=rout.ap().rearrange("p (c k) -> p c k", c=DC)).then_inc(cs.sem, 16)
            cs.n += 16
            sy.wait_ge(c_sto[0].sem, c_sto[0].n)
            sy.wait_ge(c_sto[1].sem, c_sto[1].n)

        @blk.tensor
        def _(t):
            c_wo16.wait(t)
            for g in range(NTILE):
                b, bk = divmod(g, NT)
                c_ldq.wait(t, B(c_ldq) + 16 * (g + 1))
                c_ldk.wait(t, B(c_ldk) + 16 * (g + 1))
                c_ldvt.wait(t, B(c_ldvt) + 16 * (g + 1))
                if bk > 0:
                    c_Mu.wait(t, B(c_Mu) + 16 * g)

                def head_front(h):
                    hb = 64 * (h % 2); hc = h // 2
                    qh = qblk[hb:hb + 16, hc * 512:(hc + 1) * 512]
                    kh = kblk[hb:hb + 16, hc * 512:(hc + 1) * 512]
                    for c in range(2):
                        evw(t, 0, 36 * g + 2 * h + c)
                        t.matmul(P.ps.tile(0), Rt[hb:hb + 16, 256 * c:256 * c + 128], qh, start=True, stop=True)
                        evw(t, 1, 36 * g + 2 * h + c)
                        inst = t.matmul(P.ps.tile(1), Rt[hb:hb + 16, 128:256], qh, start=True, stop=True)
                        c_rep.inc(inst)
                    yb = 4 + h % 2
                    evw(t, yb, 8 * g + h // 2)
                    evw(t, 1, 36 * g + 2 * h + 2)
                    c_qf.wait(t, B(c_qf) + 16 * g + h + 1)
                    if bk > 0:
                        t.matmul(P.ps.tile(yb, 64, 512), Mb[:, h * 192:h * 192 + 64], qf_c0, start=True, stop=False)
                        t.matmul(P.ps.tile(yb, 64, 512), Mb[:, h * 192 + 64:h * 192 + 128], qf_c1, start=False, stop=False)
                        t.matmul(P.ps.tile(yb, 64, 512), Mb[0:17, h * 192 + 128:h * 192 + 192], qfc2[0:17, :], start=False, stop=False)
                    for sub in range(4):
                        j = 4 * h + sub
                        evw(t, 2 + j % 2, 32 * g + j // 2)
                        inst = t.matmul(P.ps.tile(2 + j % 2), kh[:, sub * 128:(sub + 1) * 128], qh,
                                        start=True, stop=True)
                        c_S.inc(inst)

                def head_back(h):
                    yb = 4 + h % 2
                    for sub in range(4):
                        c_A.wait(t, B(c_A) + 4 * (16 * g + h) + sub + 1)
                        inst = t.matmul(P.ps.tile(yb, 64, 512), vtblk[:, sub * 1024 + 64 * h:sub * 1024 + 64 * h + 64],
                                        Abuf[:, sub * 512:(sub + 1) * 512],
                                        start=(bk == 0 and sub == 0), stop=(sub == 3))
                    c_yd.inc(inst)
                    evw(t, 7, 16 * g + h)
                    c_kf.wait(t, B(c_kf) + 16 * g + h + 1)
                    kfTh = kfT_s[h % 2]
                    for c in range(3):
                        fc = 128 if c < 2 else 17
                        for sub in range(4):
                            inst = t.matmul(P.ps.banks[7][0:fc, 64 * c:64 * c + 64],
                                            kfTh[:, sub * 273 + 128 * c: sub * 273 + 128 * c + fc],
                                            vtblk[:, sub * 1024 + 64 * h:sub * 1024 + 64 * h + 64],
                                            start=(sub == 0), stop=(sub == 3))
                    c_dM.inc(inst)

                def head_back2(h):
                    evw(t, 6, 32 * g + 2 * h)
                    c_ysq.wait(t, B(c_ysq) + 16 * g + h + 1)
                    inst = t.matmul(P.ps.tile(6, 1, 512), ones_b[0:64, :], ysq_sb[0:64, :], start=True, stop=True)
                    c_rs.inc(inst)
                    evw(t, 6, 32 * g + 2 * h + 1)
                    c_rr.wait(t, B(c_rr) + 16 * g + h + 1)
                    inst = t.matmul(P.ps.tile(6, 64, 512), onesr_f[:, 0:64], mrow[:], start=True, stop=True)
                    c_rbc.inc(inst)

                for h in range(18):
                    if h < 16:
                        head_front(h)
                    if 1 <= h <= 16:
                        head_back(h - 1)
                    if h >= 2:
                        head_back2(h - 2)
                c_yn.wait(t, B(c_yn) + 16 * (g + 1))
                for i in range(8):
                    evw(t, i % 2, 36 * g + 32 + i // 2)
                    for k in range(8):
                        inst = t.matmul(P.ps.tile(i % 2), wo_sb[:, k * 1024 + i * 128:k * 1024 + (i + 1) * 128],
                                        yall[:, k * 512:(k + 1) * 512], start=(k == 0), stop=(k == 7))
                    c_Wo.inc(inst)

        @blk.scalar
        def _(s):
            for g in range(NTILE):
                b, bk = divmod(g, NT)

                def a_front(h):
                    for c in range(2):
                        c_rep.wait(s, B(c_rep) + 2 * (16 * g + h) + c + 1)
                        evw(s, 1, 36 * g + 2 * h + c - 1)
                        inst = s.copy(qrep_sb[:, c * 512:(c + 1) * 512], P.ps.tile(0))
                        P.ps.release(0, inst)
                    for sub in range(4):
                        j = 4 * h + sub
                        cb = 4 * (16 * g + h) + sub
                        c_S.wait(s, B(c_S) + cb + 1)
                        if cb >= 2:
                            c_A.wait(s, B(c_A) + cb - 1)
                        inst = s.activation(t1_sb[:, (j % 2) * 512:(j % 2 + 1) * 512], P.ps.tile(2 + j % 2),
                                            AF.Square, scale=S32, bias=sqb_t[:, 0:1])
                        P.ps.release(2 + j % 2, inst)

                def a_mid(h):
                    yb = 4 + h % 2
                    c_yd.wait(s, B(c_yd) + 16 * g + h + 1)
                    c_rs.wait(s, B(c_rs) + 16 * g + h)
                    inst = s.activation(ysq_sb[0:64, :], P.ps.tile(yb, 64, 512), AF.Square)
                    c_ysq.inc(inst)
                    c_yn.wait(s, B(c_yn) + 16 * g + h)
                    inst = s.copy(y_sb[:], P.ps.tile(yb, 64, 512))
                    P.ps.release(yb, inst)

                def a_back(h):
                    c_rs.wait(s, B(c_rs) + 16 * g + h + 1)
                    if 16 * g + h > 0:
                        c_rbc.wait(s, B(c_rbc) + 16 * g + h)
                    inst = s.activation(mrow[:], P.ps.tile(6, 1, 512), AF.Ln, bias=P.t('eps_t')[0:1, :], scale=1.0 / 64.0)
                    P.ps.release(6, inst)
                    s.drain()
                    inst = s.activation(mrow[:], mrow[:], AF.Exp, scale=-0.5)
                    c_rr.inc(inst)
                    c_rbc.wait(s, B(c_rbc) + 16 * g + h + 1)
                    c_yn.wait(s, B(c_yn) + 16 * g + h)
                    inst = s.copy(rb_sb[:], P.ps.tile(6, 64, 512))
                    P.ps.release(6, inst)

                for h in range(18):
                    if h < 16:
                        a_front(h)
                    if h >= 2:
                        a_back(h - 2)
                    if 1 <= h <= 16:
                        a_mid(h - 1)

        @blk.vector
        def _(v):
            for g in range(NTILE):
                b, bk = divmod(g, NT)
                c_ldkt.wait(v, B(c_ldkt) + 16 * (g + 1))
                if bk == 0:
                    if g > 0:
                        c_Mu.wait(v, B(c_Mu) + 16 * g)
                    v.memset(Mf[:], 0.0)
                    v.drain()
                if g == 0:
                    for sl_ in kfT_s:
                        for sub in range(4):
                            v.memset(sl_[:, sub * 273 + 272:sub * 273 + 273], 1.0)
                    v.drain()

                def d_front(h):
                    hb = 64 * (h % 2); hc = h // 2
                    kfT = kfT_s[h % 2]
                    c_dM.wait(v, B(c_dM) + 16 * g + h - 1)
                    for sub in range(4):
                        kts = ktblk[:, sub * 1024 + 64 * h:sub * 1024 + 64 * h + 16]
                        in_i = kts.unsqueeze(2).broadcast_to([128, 16, 16])
                        in_j = kts.unsqueeze(1).broadcast_to([128, 16, 16])
                        v.scalar_tensor_tensor(kfT[:, sub * 273:sub * 273 + 256].rearrange("p (i j) -> p i j", i=16),
                                               in_i, CQ2, in_j, op0=ALU.mult, op1=ALU.mult)
                        inst = v.tensor_scalar(kfT[:, sub * 273 + 256:sub * 273 + 272], kts, 1.0 / RRD, None, op0=ALU.mult)
                    c_kf.inc(inst)
                    for c in range(2):
                        evw(v, 0, 36 * g + 2 * h + c + 1)
                        c_rep.wait(v, B(c_rep) + 2 * (16 * g + h) + c + 1)
                        inst = v.scalar_tensor_tensor(qf_c0 if c == 0 else qf_c1, P.ps.tile(1),
                                                      CQ2, qrep_sb[:, c * 512:(c + 1) * 512],
                                                      op0=ALU.mult, op1=ALU.mult)
                        P.ps.release(1, inst)
                    inst = v.tensor_scalar(qfc2[0:16, :], qh_ap(qblk, h), 1.0 / RRD, None, op0=ALU.mult)
                    c_qf.inc(inst)
                    for sub in range(4):
                        j = 4 * h + sub
                        evw(v, 2 + j % 2, 32 * g + j // 2 + 1)
                        if 16 * g + h > 0 and sub == 0:
                            c_yd.wait(v, B(c_yd) + 16 * g + h)
                        inst = v.scalar_tensor_tensor(Abuf[:, sub * 512:(sub + 1) * 512],
                                                      t1_sb[:, (j % 2) * 512:(j % 2 + 1) * 512], 0.5,
                                                      masks[:, sub * 512:(sub + 1) * 512],
                                                      op0=ALU.add, op1=ALU.mult)
                        c_A.inc(inst)

                def d_mid(h):
                    c_dM.wait(v, B(c_dM) + 16 * g + h + 1)
                    if g > 0:
                        c_Mu.wait(v, B(c_Mu) + 16 * (g - 1) + h + 1)
                    msl = Mf[:, h * 192:(h + 1) * 192]
                    inst = v.tensor_tensor(msl, P.ps.banks[7][:, 0:192], msl, op=ALU.add)
                    P.ps.release(7, inst)

                def d_back(h):
                    hb = 64 * (h % 2); hc = h // 2
                    evw(v, 4 + h % 2, 8 * g + h // 2 + 1)
                    evw(v, 6, 32 * g + 2 * h + 2)
                    inst = v.scalar_tensor_tensor(yall[hb:hb + 64, hc * 512:(hc + 1) * 512],
                                                  y_sb[:], gcol[:, 0:1], rb_sb[:],
                                                  op0=ALU.mult, op1=ALU.mult)
                    c_yn.inc(inst)

                for h in range(18):
                    if h < 16:
                        d_front(h)
                    if 1 <= h <= 16:
                        d_mid(h - 1)
                    if h >= 2:
                        d_back(h - 2)
                if g > 0:
                    p_ = (g - 1) % 2
                    c_sto[p_].wait(v, B(c_sto[p_]) + 16 * ((g - 1) // 2 + 1))
                c_ldx2.wait(v, B(c_ldx2) + 16 * (g + 1))
                for i in range(8):
                    c_Wo.wait(v, B(c_Wo) + 8 * g + i + 1)
                    inst = v.tensor_tensor(rout.ap()[:, i * 512:(i + 1) * 512], P.ps.tile(i % 2),
                                           xr[:, i * 512:(i + 1) * 512], op=ALU.add)
                    P.ps.release(i % 2, inst)



def qh_ap(qblk, h):
    hb = 64 * (h % 2); hc = h // 2
    return qblk[hb:hb + 16, hc * 512:(hc + 1) * 512]


def emit_ssm(P, src, dst):
    nc = P.nc
    w_in = P.dscratch("sinT_bf", (1024, 1024), BF16)
    w_xp = P.dscratch("sxT_bf", (512, 80), BF16)
    w_dt = P.dscratch("sdtT_bf", (64, 512), BF16)
    w_out = P.dscratch("soutT_bf", (1024, 1024), BF16)
    hbuf = P.dscratch("s_hbuf", (B_CORE, 8, 128, S), F32)
    xcf = P.dscratch("s_xcf", (B_CORE, 4, 128, S), F32)
    xcb = P.dscratch("s_xcb", (B_CORE, 4, 128, S), BF16)
    zbuf = P.dscratch("s_zbuf", (B_CORE, 4, 128, S), BF16)
    dbuf = P.dscratch("s_dbuf", (B_CORE, 4, 128, S), F32)
    bcbuf = P.dscratch("s_bcbuf", (B_CORE, 16, S), F32)
    ybuf = P.dscratch("s_ybuf", (B_CORE, 4, 128, S), BF16)

    xr = P.t("ffn_xr"); xb_ = P.t("ffn_xb"); xsq = P.t("ffn_xsq")
    hin = P.t("ffn_hin"); g1b = P.t("ffn_g1")
    ta = P.t("ffn_ta"); tb = P.t("ffn_tb")
    rout = P.t("ffn_rout")
    mrow = P.t("ffn_mrow"); vrow = P.t("ffn_vrow"); trow = P.t("ffn_trow")
    wab = P.t("ffn_wab"); hh = P.t("ffn_hh")
    ones_b = P.t("ones_b"); onesr_f = P.t("onesr_f")
    skx = P.t("sm_skx"); skz = P.t("sm_skz")
    sbdt = P.t("sm_sbdt"); salog = P.t("sm_salog"); sD = P.t("sm_sD")
    acol = P.sb("s_acol", [128, 32], F32)
    nacol = P.sb("s_nacol", [128, 32], F32)
    brow = P.sb("s_brow", [1, 2048], F32)
    scols = P.sb("s_scols", [128, 32], F32)

    win_sb = hh.ap()[:, 0:8192]
    wout_sb = hh.ap()[:, 0:8192]
    wxp_sb = wab.ap()[:, 0:320]
    wdt_sb = wab.ap()[:, 512:1024]

    Ct = lambda nm: P.ctr("ssm_" + nm)
    c_init = Ct("init"); c_w1 = Ct("w1"); c_w5 = Ct("w5")
    c_sq = Ct("sq"); c_stat = Ct("stat"); c_var = Ct("var"); c_rstd = Ct("rstd")
    c_mr = Ct("mr"); c_bc = Ct("bc")
    c_ldx = Ct("ldx"); c_mm1 = Ct("mm1"); c_sth = [Ct("sth0"), Ct("sth1")]
    c_ldh = Ct("ldh"); c_cv = Ct("cv"); c_cs = Ct("cs"); c_stc = [Ct("stc0"), Ct("stc1")]
    c_ldxc = Ct("ldxc"); c_xp = Ct("xp"); c_dte = Ct("dte"); c_dtm = Ct("dtm")
    c_u = Ct("u"); c_del = Ct("del"); c_dchain = Ct("dchain")
    c_std = [Ct("std0"), Ct("std1")]; c_stbc = Ct("stbc")
    c_ldd = [Ct("ldd0"), Ct("ldd1")]; c_ldxf = Ct("ldxf"); c_ldbc = Ct("ldbc"); c_bcm4 = Ct("bcm4")
    c_dec = Ct("dec"); c_clip = Ct("clip"); c_u1 = Ct("u1"); c_drv = Ct("drv")
    c_scan = Ct("scan"); c_acc = Ct("acc"); c_y = Ct("y"); c_sty = [Ct("sty0"), Ct("sty1")]
    c_ldy = Ct("ldy"); c_ldx5 = Ct("ldx5"); c_mm5 = Ct("mm5"); c_sto = [Ct("sto0"), Ct("sto1")]
    ev = P.ps.ev
    allc = [c_init, c_w1, c_w5, c_sq, c_stat, c_var, c_rstd, c_mr, c_bc, c_ldx, c_mm1,
            c_ldh, c_cv, c_cs, c_ldxc, c_xp, c_dte, c_dtm, c_u, c_del, c_dchain, c_stbc,
            c_ldxf, c_ldbc, c_bcm4, c_dec, c_clip, c_u1, c_drv, c_scan, c_acc,
            c_y, c_ldy, c_ldx5, c_mm5] + c_sth + c_stc + c_std + c_sty + c_sto + c_ldd
    base = {c: c.n for c in allc}
    B = lambda c: base[c]
    NTILE = B_CORE * NT

    with nc.Block() as blk:
        @blk.sync
        def _(sy):
            sy.dma_start(out=win_sb.rearrange("p (k i) -> p k i", k=8),
                         in_=w_in.rearrange("(k p) i -> p k i", p=128)).then_inc(c_w1.sem, 16)
            c_w1.n += 16
            sy.dma_start(out=wxp_sb.rearrange("p (k i) -> p k i", k=4),
                         in_=w_xp.rearrange("(k p) i -> p k i", p=128)).then_inc(c_w1.sem, 16)
            c_w1.n += 16
            sy.dma_start(out=wdt_sb[0:64, :].rearrange("p (c i) -> p c i", c=4),
                         in_=w_dt.rearrange("k (c i) -> k c i", c=4)).then_inc(c_w1.sem, 16)
            c_w1.n += 16

        @blk.scalar
        def _(s):
            s.activation(acol[:], salog[:], AF.Abs)
            s.drain()
            s.activation(acol[:], acol[:], AF.Exp, scale=-1.0)
            s.drain()
            inst = s.activation(acol[:], acol[:], AF.Ln, bias=1.0)
            c_init.inc(inst)

        @blk.vector
        def _(v):
            c_init.wait(v)
            v.tensor_scalar(nacol[:], salog[:], 0.0, None, op0=ALU.max)
            v.drain()
            v.tensor_tensor(acol[:], acol[:], nacol[:], op=ALU.add)
            v.drain()
            v.tensor_scalar(acol[:], acol[:], 1e-4, None, op0=ALU.add)
            v.drain()
            v.tensor_scalar(nacol[:], acol[:], -1.0, None, op0=ALU.mult)
            v.drain()

    evb = [e.n for e in ev]
    def evw(eng, kk, val):
        if val > 0:
            eng.wait_ge(ev[kk].sem, evb[kk] + val)
    with nc.Block() as blk:
        @blk.sync
        def _(sy):
            def store(pidx):
                pb, pti = divmod(pidx, NT)
                for kk in range(4):
                    evw(sy, kk, 2 * (pidx + 1))
                cs = c_sth[pidx % 2]
                sy.dma_start(out=hbuf[pb, :, :, pti * 512:(pti + 1) * 512].transpose([1, 0, 2]),
                             in_=rout.ap().rearrange("p (c k) -> p c k", c=DC)).then_inc(cs.sem, 16)
                cs.n += 16
            for idx in range(NTILE):
                b, ti = divmod(idx, NT); t0 = ti * 512
                if idx > 0:
                    evw(sy, 6, 2 * (idx - 1) + 2)
                sy.dma_start(out=xr.ap().rearrange("p (c k) -> p c k", c=DC),
                             in_=src[b, :, :, t0:t0 + 512].transpose([1, 0, 2])).then_inc(c_ldx.sem, 16)
                c_ldx.n += 16
                if idx > 0:
                    store(idx - 1)
            store(NTILE - 1)
            sy.wait_ge(c_sth[0].sem, c_sth[0].n)
            sy.wait_ge(c_sth[1].sem, c_sth[1].n)

        @blk.tensor
        def _(t):
            c_w1.wait(t)
            for idx in range(NTILE):
                c_sq.wait(t, B(c_sq) + idx + 1)
                evw(t, 6, 2 * idx); evw(t, 7, 2 * idx)
                for dc in range(DC):
                    t.matmul(P.ps.tile(6, 1, 512), ones_b[:], xb_[:, dc * 512:(dc + 1) * 512],
                             start=(dc == 0), stop=(dc == DC - 1))
                for dc in range(DC):
                    inst = t.matmul(P.ps.tile(7, 1, 512), ones_b[:], xsq[:, dc * 512:(dc + 1) * 512],
                                    start=(dc == 0), stop=(dc == DC - 1))
                c_stat.inc(inst)
                c_mr.wait(t, B(c_mr) + idx + 1)
                evw(t, 6, 2 * idx + 1); evw(t, 7, 2 * idx + 1)
                t.matmul(P.ps.tile(6), onesr_f[:], trow[:], start=True, stop=True)
                inst = t.matmul(P.ps.tile(7), onesr_f[:], mrow[:], start=True, stop=True)
                c_bc.inc(inst)
                evw(t, 6, 2 * idx + 2); evw(t, 7, 2 * idx + 2)
                for i in range(8):
                    evw(t, i % 4, 2 * idx + i // 4)
                    for k in range(8):
                        inst = t.matmul(P.ps.tile(i % 4), win_sb[:, k * 1024 + i * 128:k * 1024 + (i + 1) * 128],
                                        xb_[:, k * 512:(k + 1) * 512], start=(k == 0), stop=(k == 7))
                    c_mm1.inc(inst)

        @blk.scalar
        def _(s):
            for idx in range(NTILE):
                c_ldx.wait(s, B(c_ldx) + 16 * (idx + 1))
                if idx > 0:
                    c_mm1.wait(s, B(c_mm1) + 8 * idx)
                s.copy(xb_[:], xr[:])
                s.drain()
                inst = s.activation(xsq[:], xb_[:], AF.Square)
                c_sq.inc(inst)
                c_var.wait(s, B(c_var) + idx + 1)
                if idx > 0:
                    c_bc.wait(s, B(c_bc) + idx)
                s.activation(trow[:], vrow[:], AF.Ln, bias=P.t('eps_t')[0:1, :])
                s.drain()
                inst = s.activation(trow[:], trow[:], AF.Exp, scale=-0.5)
                c_rstd.inc(inst)
                if idx > 0:
                    cs = c_sth[(idx - 1) % 2]
                    s.wait_ge(cs.sem, B(cs) + 16 * ((idx - 1) // 2 + 1))
                for i in range(8):
                    c_mm1.wait(s, B(c_mm1) + 8 * idx + i + 1)
                    inst = s.copy(rout.ap()[:, i * 512:(i + 1) * 512], P.ps.tile(i % 4))
                    P.ps.release(i % 4, inst)
                    s.drain()

        @blk.vector
        def _(v):
            for idx in range(NTILE):
                c_stat.wait(v, B(c_stat) + idx + 1)
                if idx > 0:
                    c_bc.wait(v, B(c_bc) + idx)
                inst = v.tensor_scalar(mrow[:], P.ps.tile(6, 1, 512), 1.0 / D, None, op0=ALU.mult)
                P.ps.release(6, inst)
                inst = v.tensor_scalar(vrow[:], P.ps.tile(7, 1, 512), 1.0 / D, None, op0=ALU.mult)
                P.ps.release(7, inst)
                v.drain()
                v.tensor_tensor(trow[:], mrow[:], mrow[:], op=ALU.mult)
                v.drain()
                inst = v.tensor_tensor(vrow[:], vrow[:], trow[:], op=ALU.subtract)
                c_var.inc(inst)
                v.drain()
                c_rstd.wait(v, B(c_rstd) + idx + 1)
                inst = v.tensor_tensor(mrow[:], mrow[:], trow[:], op=ALU.mult)
                c_mr.inc(inst)
                v.drain()
                c_bc.wait(v, B(c_bc) + idx + 1)
                for dc in range(DC):
                    sl = slice(dc * 512, (dc + 1) * 512)
                    inst1 = v.tensor_tensor(tb[:, 0:512], xr[:, sl], P.ps.tile(6), op=ALU.mult)
                    v.drain()
                    inst2 = v.tensor_tensor(xb_[:, sl], tb[:, 0:512], P.ps.tile(7), op=ALU.subtract)
                    if dc == DC - 1:
                        P.ps.release(6, inst1); P.ps.release(7, inst2)
                    v.drain()

    hv = hin.ap().bitcast(F32)
    hp = hv[:, 0:S + 2]
    q_ = rout.ap()[:, 0:S]
    xc_t = g1b.ap().bitcast(F32)[:, 0:S]
    xcb_t = g1b.ap()[:, 2 * S:3 * S]
    NCH = B_CORE * 8
    store_sig = []
    with nc.Block() as blk:
        @blk.vector
        def _(v):
            v.memset(hp[:, 0:1], 0.0)
            v.memset(hp[:, S + 1:S + 2], 0.0)
            v.drain()
            v.nop().then_inc(c_init.sem, 1)
        c_init.n += 1

        @blk.sync
        def _(sy):
            sy.wait_ge(c_init.sem, c_init.n)
            for j in range(NCH):
                b, ch = divmod(j, 8)
                if j > 0:
                    c_cv.wait(sy, B(c_cv) + j)
                sy.dma_start(out=hp[:, 1:S + 1], in_=hbuf[b, ch]).then_inc(c_ldh.sem, 16)
                c_ldh.n += 16
                if j > 0:
                    pj = j - 1
                    pb, pch = divmod(pj, 8)
                    c_cs.wait(sy, B(c_cs) + 2 * (pj + 1))
                    cs = c_stc[pj % 2]
                    if pch < 4:
                        sy.dma_start(out=xcf[pb, pch], in_=xc_t[:]).then_inc(cs.sem, 16)
                        cs.n += 16
                        sy.dma_start(out=xcb[pb, pch], in_=xcb_t[:]).then_inc(cs.sem, 16)
                        cs.n += 16
                    else:
                        sy.dma_start(out=zbuf[pb, pch - 4], in_=xcb_t[:]).then_inc(cs.sem, 16)
                        cs.n += 16
                    store_sig.append((cs, cs.n))
            pj = NCH - 1
            pb, pch = divmod(pj, 8)
            c_cs.wait(sy, B(c_cs) + 2 * (pj + 1))
            cs = c_stc[pj % 2]
            sy.dma_start(out=zbuf[pb, pch - 4], in_=xcb_t[:]).then_inc(cs.sem, 16)
            cs.n += 16
            store_sig.append((cs, cs.n))
            sy.wait_ge(c_stc[0].sem, c_stc[0].n)
            sy.wait_ge(c_stc[1].sem, c_stc[1].n)

        @blk.vector
        def _(v):
            for j in range(NCH):
                b, ch = divmod(j, 8)
                kc = skx if ch < 4 else skz
                cc = ch % 4
                v.wait_ge(c_ldh.sem, B(c_ldh) + 16 * (j + 1))
                if j > 0:
                    c_cs.wait(v, B(c_cs) + 2 * j)
                v.tensor_scalar(q_[:], hp[:, 0:S], kc[:, cc * 3:cc * 3 + 1], None, op0=ALU.mult)
                v.drain()
                v.scalar_tensor_tensor(q_[:], hp[:, 1:S + 1], kc[:, cc * 3 + 1:cc * 3 + 2], q_[:],
                                       op0=ALU.mult, op1=ALU.add)
                v.drain()
                inst = v.scalar_tensor_tensor(q_[:], hp[:, 2:S + 2], kc[:, cc * 3 + 2:cc * 3 + 3], q_[:],
                                              op0=ALU.mult, op1=ALU.add)
                c_cv.inc(inst)
                v.drain()

        @blk.scalar
        def _(s):
            for j in range(NCH):
                c_cv.wait(s, B(c_cv) + j + 1)
                if j >= 1:
                    ctr, val = store_sig[j - 1]
                    s.wait_ge(ctr.sem, val)
                inst = s.activation(xc_t[:], q_[:], AF.Silu)
                c_cs.inc(inst)
                s.drain()
                inst = s.copy(xcb_t[:], xc_t[:])
                c_cs.inc(inst)
                s.drain()
    evb = [e.n for e in ev]
    xcb_t3 = hin.ap()[:, 0:2048]
    dtr_sb = hin.ap()[:, 2048:2560]
    dstage = rout.ap()[:, 0:2048]
    bc_st = g1b.ap().bitcast(F32)[0:16, 0:512]
    u_sb = g1b.ap().bitcast(F32)[:, 512:1024]
    ua_sb = g1b.ap().bitcast(F32)[:, 1024:1536]
    with nc.Block() as blk:
        @blk.sync
        def _(sy):
            def store(pidx):
                pb, pti = divmod(pidx, NT); pt0 = pti * 512
                c_dchain.wait(sy, B(c_dchain) + 4 * (pidx + 1))
                cs = c_std[pidx % 2]
                sy.dma_start(out=dbuf[pb, :, :, pt0:pt0 + 512].transpose([1, 0, 2]),
                             in_=dstage.rearrange("p (c k) -> p c k", c=4)).then_inc(cs.sem, 16)
                cs.n += 16
                evw(sy, 4, pidx + 1)
                sy.dma_start(out=bcbuf[pb, :, pt0:pt0 + 512], in_=bc_st[:]).then_inc(c_stbc.sem, 16)
                c_stbc.n += 16
            for idx in range(NTILE):
                b, ti = divmod(idx, NT); t0 = ti * 512
                if idx > 0:
                    c_xp.wait(sy, B(c_xp) + idx)
                sy.dma_start(out=xcb_t3.rearrange("p (c k) -> p c k", c=4),
                             in_=xcb[b, :, :, t0:t0 + 512].transpose([1, 0, 2])).then_inc(c_ldxc.sem, 16)
                c_ldxc.n += 16
                if idx > 0:
                    store(idx - 1)
            store(NTILE - 1)
            sy.wait_ge(c_std[0].sem, c_std[0].n)
            sy.wait_ge(c_std[1].sem, c_std[1].n)
            sy.wait_ge(c_stbc.sem, c_stbc.n)

        @blk.tensor
        def _(t):
            for idx in range(NTILE):
                c_ldxc.wait(t, B(c_ldxc) + 16 * (idx + 1))
                evw(t, 4, idx)
                for k in range(4):
                    inst = t.matmul(P.ps.tile(4, 80, 512), wxp_sb[:, k * 80:(k + 1) * 80],
                                    xcb_t3[:, k * 512:(k + 1) * 512], start=(k == 0), stop=(k == 3))
                c_xp.inc(inst)
                c_dte.wait(t, B(c_dte) + idx + 1)
                for i in range(4):
                    evw(t, i, idx)
                    inst = t.matmul(P.ps.tile(i), wdt_sb[0:64, i * 128:(i + 1) * 128],
                                    dtr_sb[0:64, 0:512], start=True, stop=True)
                    c_dtm.inc(inst)

        @blk.scalar
        def _(s):
            for idx in range(NTILE):
                c_xp.wait(s, B(c_xp) + idx + 1)
                if idx > 0:
                    c_dtm.wait(s, B(c_dtm) + 4 * idx)
                inst = s.copy(dtr_sb[0:64, 0:512], P.ps.tile(4, 64, 512))
                c_dte.inc(inst)
                s.drain()
                if idx > 0:
                    s.wait_ge(c_stbc.sem, B(c_stbc) + 16 * idx)
                inst = s.activation(bc_st[:], P.ps.banks[4][64:80, 0:512], AF.Tanh)
                P.ps.release(4, inst)
                s.drain()
                for i in range(4):
                    cb = 4 * idx + i
                    evw(s, i, idx + 1)
                    if cb > 0:
                        c_dchain.wait(s, B(c_dchain) + cb)
                    s.activation(ua_sb[:], u_sb[:], AF.Abs)
                    s.drain()
                    s.activation(ua_sb[:], ua_sb[:], AF.Exp, scale=-1.0)
                    s.drain()
                    inst = s.activation(ua_sb[:], ua_sb[:], AF.Ln, bias=1.0)
                    c_del.inc(inst)
                    s.drain()

        @blk.vector
        def _(v):
            for idx in range(NTILE):
                for i in range(4):
                    cb = 4 * idx + i
                    c_dtm.wait(v, B(c_dtm) + cb + 1)
                    if cb > 0:
                        c_del.wait(v, B(c_del) + cb)
                    if i == 0 and idx > 0:
                        cs = c_std[(idx - 1) % 2]
                        v.wait_ge(cs.sem, B(cs) + 16 * ((idx - 1) // 2 + 1))
                    inst = v.tensor_scalar(u_sb[:], P.ps.tile(i), sbdt[:, i:i + 1], None, op0=ALU.add)
                    P.ps.release(i, inst)
                    v.drain()
                    c_del.wait(v, B(c_del) + cb + 1)
                    v.tensor_scalar(dstage[:, i * 512:(i + 1) * 512], u_sb[:], 0.0, None, op0=ALU.max)
                    v.drain()
                    v.tensor_tensor(dstage[:, i * 512:(i + 1) * 512], dstage[:, i * 512:(i + 1) * 512],
                                    ua_sb[:], op=ALU.add)
                    v.drain()
                    inst = v.tensor_scalar(dstage[:, i * 512:(i + 1) * 512], dstage[:, i * 512:(i + 1) * 512],
                                           1e-4, 1.0, op0=ALU.max, op1=ALU.min)
                    c_dchain.inc(inst)
                    v.drain()

    evb = [e.n for e in ev]
    TC = 1024
    NTC = S // TC
    hv4 = hin.ap().bitcast(F32)
    xcf_sb = hv4
    g4 = g1b.ap().bitcast(F32)
    btb = g4[:, 0:1024]
    ctb_hold = [g4[:, 1024:2048], g4[:, 4096:5120]]
    dec_sb = [g4[:, 2048:3072], g4[:, 3072:4096]]
    rv4 = rout.ap()
    del_sb = [rv4[:, 0:1024], rv4[:, 1024:2048]]
    u1_sb = [rv4[:, 2048:3072], rv4[:, 3072:4096]]
    xrv = xr.ap()
    drv_sb = [xrv[:, 0:1024], xrv[:, 1024:2048]]
    stt_sb = [xrv[:, 2048:3072], xrv[:, 3072:4096]]
    yac = hv4[:, 4096:8192]
    ybf = [xb_.ap()[:, 0:1024], xb_.ap()[:, 1024:2048]]
    with nc.Block() as blk:
        @blk.sync
        def _(sy):
            for cN in range(B_CORE * NTC):
                b, tc = divmod(cN, NTC); t0 = tc * TC
                if cN > 0:
                    c_y.wait(sy, B(c_y) + 4 * cN)
                for c in range(4):
                    sy.dma_start(out=xcf_sb[:, c * TC:(c + 1) * TC],
                                 in_=xcf[b, c, :, t0:t0 + TC]).then_inc(c_ldxf.sem, 16)
                    c_ldxf.n += 16
                for ds in range(8):
                    j = cN * 8 + ds
                    if j > 0:
                        c_bcm4.wait(sy, B(c_bcm4) + j)
                    sy.dma_start(out=brow[:, 0:TC], in_=bcbuf[b, ds, t0:t0 + TC].unsqueeze(0)
                                 ).then_inc(c_ldbc.sem, 16)
                    c_ldbc.n += 16
                    sy.dma_start(out=brow[:, TC:2 * TC], in_=bcbuf[b, 8 + ds, t0:t0 + TC].unsqueeze(0)
                                 ).then_inc(c_ldbc.sem, 16)
                    c_ldbc.n += 16
                    for c in range(4):
                        cb = j * 4 + c
                        if cb >= 2:
                            c_dec.wait(sy, B(c_dec) + cb - 1)
                        cl = c_ldd[cb % 2]
                        sy.dma_start(out=del_sb[cb % 2][:],
                                     in_=dbuf[b, c, :, t0:t0 + TC]).then_inc(cl.sem, 16)
                        cl.n += 16
                for c in range(4):
                    c_y.wait(sy, B(c_y) + 4 * cN + c + 1)
                    cs = c_sty[c % 2]
                    sy.dma_start(out=ybuf[b, c, :, t0:t0 + TC], in_=ybf[c % 2][:]).then_inc(cs.sem, 16)
                    cs.n += 16
            sy.wait_ge(c_sty[0].sem, c_sty[0].n)
            sy.wait_ge(c_sty[1].sem, c_sty[1].n)

        @blk.tensor
        def _(t):
            for cN in range(B_CORE * NTC):
                for ds in range(8):
                    j = cN * 8 + ds
                    t.wait_ge(c_ldbc.sem, B(c_ldbc) + 32 * (j + 1))
                    for q in range(2):
                        evw(t, 4 + q, j)
                        t.matmul(P.ps.tile(4 + q), onesr_f[:], brow[0:1, q * 512:(q + 1) * 512],
                                 start=True, stop=True)
                    for q in range(2):
                        evw(t, 6 + q, j)
                        inst = t.matmul(P.ps.tile(6 + q), onesr_f[:], brow[0:1, TC + q * 512:TC + (q + 1) * 512],
                                        start=True, stop=True)
                    c_bcm4.inc(inst)

        @blk.scalar
        def _(s):
            for cN in range(B_CORE * NTC):
                for ds in range(8):
                    j = cN * 8 + ds
                    for c in (0, 2):
                        e = j * 4 + c
                        s.wait_ge(c_ldd[0].sem, B(c_ldd[0]) + 16 * (e // 2 + 1))
                        s.wait_ge(c_ldd[1].sem, B(c_ldd[1]) + 16 * (e // 2 + 1))
                        if e >= 2:
                            c_scan.wait(s, B(c_scan) + e - 1)
                        inst = s.activation(dec_sb[0][:], del_sb[0][:], AF.Exp,
                                            scale=nacol[:, c * 8 + ds:c * 8 + ds + 1])
                        c_dec.inc(inst)
                        inst = s.activation(dec_sb[1][:], del_sb[1][:], AF.Exp,
                                            scale=nacol[:, (c + 1) * 8 + ds:(c + 1) * 8 + ds + 1])
                        c_dec.inc(inst)

        @blk.gpsimd
        def _(g_):
            for cN in range(B_CORE * NTC):
                c_y.wait(g_, B(c_y) + 4 * cN)
                for ds in range(8):
                    j = cN * 8 + ds
                    for c in (0, 2):
                        e = j * 4 + c; o = e + 1
                        c_scan.wait(g_, B(c_scan) + o + 1)
                        if ds > 0:
                            c_acc.wait(g_, B(c_acc) + e - 2)
                        inst = g_.tensor_tensor(stt_sb[0][:], stt_sb[0][:], ctb_hold[j % 2][:], op=ALU.mult)
                        c_clip.inc(inst); k_s0 = c_clip.n
                        inst = g_.tensor_tensor(stt_sb[1][:], stt_sb[1][:], ctb_hold[j % 2][:], op=ALU.mult)
                        c_clip.inc(inst); k_s1 = c_clip.n
                        if ds == 0:
                            c_clip.wait(g_, k_s0)
                            inst = g_.tensor_copy(yac[:, c * TC:(c + 1) * TC], stt_sb[0][:])
                            c_acc.inc(inst)
                            c_clip.wait(g_, k_s1)
                            inst = g_.tensor_copy(yac[:, (c + 1) * TC:(c + 2) * TC], stt_sb[1][:])
                            c_acc.inc(inst)
                        else:
                            c_clip.wait(g_, k_s0)
                            inst = g_.tensor_tensor(yac[:, c * TC:(c + 1) * TC], yac[:, c * TC:(c + 1) * TC],
                                                    stt_sb[0][:], op=ALU.add)
                            c_acc.inc(inst)
                            c_clip.wait(g_, k_s1)
                            inst = g_.tensor_tensor(yac[:, (c + 1) * TC:(c + 2) * TC], yac[:, (c + 1) * TC:(c + 2) * TC],
                                                    stt_sb[1][:], op=ALU.add)
                            c_acc.inc(inst)

        @blk.vector
        def _(v):
            for cN in range(B_CORE * NTC):
                b, tc = divmod(cN, NTC)
                c_ldxf.wait(v, B(c_ldxf) + 64 * (cN + 1))
                for ds in range(8):
                    j = cN * 8 + ds
                    c_bcm4.wait(v, B(c_bcm4) + j + 1)
                    inst = v.tensor_copy(btb[:, 0:512], P.ps.tile(4))
                    P.ps.release(4, inst)
                    inst = v.tensor_copy(btb[:, 512:1024], P.ps.tile(5))
                    P.ps.release(5, inst)
                    inst = v.tensor_copy(ctb_hold[j % 2][:, 0:512], P.ps.tile(6))
                    P.ps.release(6, inst)
                    inst = v.tensor_copy(ctb_hold[j % 2][:, 512:1024], P.ps.tile(7))
                    P.ps.release(7, inst)
                    evw(v, 7, j + 1)
                    for c in (0, 2):
                        e = j * 4 + c
                        if e >= 2:
                            c_acc.wait(v, B(c_acc) + e)
                        inst = v.tensor_tensor(u1_sb[0][:], btb[:], xcf_sb[:, c * TC:(c + 1) * TC], op=ALU.mult)
                        c_u1.inc(inst)
                        inst = v.tensor_tensor(u1_sb[1][:], btb[:], xcf_sb[:, (c + 1) * TC:(c + 2) * TC], op=ALU.mult)
                        c_u1.inc(inst)
                        c_dec.wait(v, B(c_dec) + e + 2)
                        inst = v.tensor_scalar(dec_sb[0][:], dec_sb[0][:], 1e-4, 1.0,
                                               op0=ALU.max, op1=ALU.min)
                        c_u1.inc(inst); k_ce = c_u1.n
                        inst = v.tensor_scalar(dec_sb[1][:], dec_sb[1][:], 1e-4, 1.0,
                                               op0=ALU.max, op1=ALU.min)
                        c_u1.inc(inst)
                        c_u1.wait(v, k_ce)
                        inst = v.tensor_tensor(drv_sb[0][:], dec_sb[0][:], u1_sb[0][:], op=ALU.mult)
                        c_u1.inc(inst); k_d1 = c_u1.n
                        c_u1.wait(v, k_ce + 1)
                        inst = v.tensor_tensor(drv_sb[1][:], dec_sb[1][:], u1_sb[1][:], op=ALU.mult)
                        c_u1.inc(inst)
                        c_u1.wait(v, k_d1)
                        inst = v.tensor_tensor(drv_sb[0][:], u1_sb[0][:], drv_sb[0][:], op=ALU.subtract)
                        c_u1.inc(inst); k_d2 = c_u1.n
                        c_u1.wait(v, k_d1 + 1)
                        inst = v.tensor_tensor(drv_sb[1][:], u1_sb[1][:], drv_sb[1][:], op=ALU.subtract)
                        c_u1.inc(inst)
                        init_e = 0.0 if tc == 0 else scols[:, c * 8 + ds:c * 8 + ds + 1]
                        init_o = 0.0 if tc == 0 else scols[:, (c + 1) * 8 + ds:(c + 1) * 8 + ds + 1]
                        c_u1.wait(v, k_d2)
                        inst = v.tensor_tensor_scan(stt_sb[0][:], dec_sb[0][:], drv_sb[0][:],
                                                    init_e, op0=ALU.mult, op1=ALU.add)
                        c_u1.inc(inst); k_se = c_u1.n
                        c_u1.wait(v, k_d2 + 1)
                        inst = v.tensor_tensor_scan(stt_sb[1][:], dec_sb[1][:], drv_sb[1][:],
                                                    init_o, op0=ALU.mult, op1=ALU.add)
                        c_u1.inc(inst); k_so = c_u1.n
                        c_u1.wait(v, k_se)
                        inst = v.tensor_copy(scols[:, c * 8 + ds:c * 8 + ds + 1],
                                             stt_sb[0][:, TC - 1:TC])
                        c_scan.inc(inst)
                        c_u1.wait(v, k_so)
                        inst = v.tensor_copy(scols[:, (c + 1) * 8 + ds:(c + 1) * 8 + ds + 1],
                                             stt_sb[1][:, TC - 1:TC])
                        c_scan.inc(inst)
                c_acc.wait(v, B(c_acc) + 32 * (cN + 1))
                for c in (0, 2):
                    for cc in (c, c + 1):
                        if cN >= 1 or cc >= 2:
                            cs = c_sty[cc % 2]
                            nst = (4 * cN + cc - 2) // 2 + 1
                            v.wait_ge(cs.sem, B(cs) + 16 * nst)
                        inst = v.scalar_tensor_tensor(yac[:, cc * TC:(cc + 1) * TC], xcf_sb[:, cc * TC:(cc + 1) * TC],
                                                      sD[:, cc:cc + 1], yac[:, cc * TC:(cc + 1) * TC],
                                                      op0=ALU.mult, op1=ALU.add)
                        c_drv.inc(inst)
                        if cc == c:
                            k_y0 = c_drv.n
                    c_drv.wait(v, k_y0)
                    inst = v.tensor_copy(ybf[c % 2][:], yac[:, c * TC:(c + 1) * TC])
                    c_y.inc(inst)
                    c_drv.wait(v, k_y0 + 1)
                    inst = v.tensor_copy(ybf[(c + 1) % 2][:], yac[:, (c + 1) * TC:(c + 2) * TC])
                    c_y.inc(inst)

    evb = [e.n for e in ev]
    rhs5 = hin.ap()[:, 0:4096]
    with nc.Block() as blk:
        @blk.gpsimd
        def _(g_):
            g_.dma_start(out=wout_sb.rearrange("p (k i) -> p k i", k=8),
                         in_=w_out.rearrange("(k p) i -> p k i", p=128)).then_inc(c_w5.sem, 16)
            c_w5.n += 16

        @blk.sync
        def _(sy):
            prev = None
            for idx in range(NTILE):
                b, ti = divmod(idx, NT); t0 = ti * 512
                if idx > 0:
                    evw(sy, 0, 4 * idx); evw(sy, 1, 4 * idx)
                    pb, pt0 = prev
                    cs = c_sto[(idx - 1) % 2]
                    sy.dma_start(out=dst[pb, :, :, pt0:pt0 + 512].transpose([1, 0, 2]),
                                 in_=rout.ap().rearrange("p (c k) -> p c k", c=DC)).then_inc(cs.sem, 16)
                    cs.n += 16
                prev = (b, t0)
                sy.dma_start(out=xr.ap().rearrange("p (c k) -> p c k", c=DC),
                             in_=src[b, :, :, t0:t0 + 512].transpose([1, 0, 2])).then_inc(c_ldx5.sem, 16)
                c_ldx5.n += 16
                if idx > 0:
                    c_mm5.wait(sy, B(c_mm5) + 8 * idx)
                sy.dma_start(out=rhs5[:, 0:2048].rearrange("p (c k) -> p c k", c=4),
                             in_=ybuf[b, :, :, t0:t0 + 512].transpose([1, 0, 2])).then_inc(c_ldy.sem, 16)
                c_ldy.n += 16
                sy.dma_start(out=rhs5[:, 2048:4096].rearrange("p (c k) -> p c k", c=4),
                             in_=zbuf[b, :, :, t0:t0 + 512].transpose([1, 0, 2])).then_inc(c_ldy.sem, 16)
                c_ldy.n += 16
            evw(sy, 0, 4 * NTILE); evw(sy, 1, 4 * NTILE)
            pb, pt0 = prev
            cs = c_sto[(NTILE - 1) % 2]
            sy.dma_start(out=dst[pb, :, :, pt0:pt0 + 512].transpose([1, 0, 2]),
                         in_=rout.ap().rearrange("p (c k) -> p c k", c=DC)).then_inc(cs.sem, 16)
            cs.n += 16
            sy.wait_ge(c_sto[0].sem, c_sto[0].n)
            sy.wait_ge(c_sto[1].sem, c_sto[1].n)

        @blk.tensor
        def _(t):
            c_w5.wait(t)
            for idx in range(NTILE):
                c_ldy.wait(t, B(c_ldy) + 32 * (idx + 1))
                for i in range(8):
                    evw(t, i % 2, 4 * idx + i // 2)
                    for k in range(8):
                        inst = t.matmul(P.ps.tile(i % 2), wout_sb[:, k * 1024 + i * 128:k * 1024 + (i + 1) * 128],
                                        rhs5[:, k * 512:(k + 1) * 512], start=(k == 0), stop=(k == 7))
                    c_mm5.inc(inst)

        @blk.vector
        def _(v):
            for idx in range(NTILE):
                if idx > 0:
                    p_ = (idx - 1) % 2
                    c_sto[p_].wait(v, B(c_sto[p_]) + 16 * ((idx - 1) // 2 + 1))
                c_ldx5.wait(v, B(c_ldx5) + 16 * (idx + 1))
                for i in range(8):
                    c_mm5.wait(v, B(c_mm5) + 8 * idx + i + 1)
                    inst = v.tensor_tensor(rout.ap()[:, i * 512:(i + 1) * 512], P.ps.tile(i % 2),
                                           xr[:, i * 512:(i + 1) * 512], op=ALU.add)
                    P.ps.release(i % 2, inst)



def emit_sublayer(P, sub, src, dst):
    if sub in (1, 3):
        emit_ffn(P, 0, f"f{sub}", src, dst)
    elif sub in (5, 7):
        emit_ffn(P, 1, f"f{sub}", src, dst)
    elif sub == 0:
        emit_attn_a0(P, src, dst)
    elif sub == 2:
        emit_attn_b0(P, src, dst)
    elif sub == 4:
        emit_attn_a1(P, src, dst)
    elif sub == 6:
        emit_ssm(P, src, dst)
    else:
        raise NotImplementedError(sub)


def build_group(subs):
    P = Prog(subs)
    x_in = P.din("x_in", (B_CORE, DC, 128, S), F32)
    x_out = P.dout("x_out", (B_CORE, DC, 128, S), F32)
    emit_preamble(P)
    n = len(subs)
    resid = P.dscratch("resid", (B_CORE, DC, 128, S), F32) if n > 1 else None
    for j, sub in enumerate(subs):
        src = x_in if j == 0 else resid
        dst = x_out if j == n - 1 else resid
        emit_sublayer(P, sub, src, dst)
    P.nc.finalize()
    return P


def input_names(P):
    names = []
    for name, ap in P._dram.items():
        t = ap.tensor
        try:
            kind = t.kind
        except Exception:
            kind = None
        names.append(name)
    return names


def kernel(**inputs):
    per_core = host_layout(inputs)
    groups = GROUPS if GROUPS is not None else [list(SUBLAYERS)]
    cur = [m["x_in"] for m in per_core]
    for subs in groups:
        P = build_group(subs)
        ext_in = sorted(P.ext_inputs)
        in_maps = []
        for ci in range(8):
            m = {}
            for nm in ext_in:
                m[nm] = cur[ci] if nm == "x_in" else per_core[ci][nm]
            in_maps.append(m)
        res = run_bass_kernel_spmd(P.nc, in_maps, core_ids=list(range(8)))
        cur = [np.asarray(res.results[ci]["x_out"]) for ci in range(8)]
    out = np.empty((16, S, D), dtype=np.float32)
    for ci in range(8):
        xs = cur[ci].reshape(B_CORE, D, S)
        out[ci * B_CORE:(ci + 1) * B_CORE] = xs.transpose(0, 2, 1)
    return out
